# version 36
# baseline (speedup 1.0000x reference)
import numpy as np
from contextlib import ExitStack

import concourse.bass as bass
import concourse.mybir as mybir
from concourse.bass_utils import run_bass_kernel_spmd

F32 = mybir.dt.float32
F32R = mybir.dt.float32r
BF16 = mybir.dt.bfloat16
I8 = mybir.dt.int8
AF = mybir.ActivationFunctionType
ALU = mybir.AluOpType
AX = mybir.AxisListType

NCORES = 8
B_LOC = 4
T = 2048
D = 1024
KT = D // 128
TT = T // 128
NQ = T // 512
H_DN = 4
DH = 256
NCH = 16
NE = 8
CAP = 9830
GR = 512
NG = 24
NSLOT = NG * GR
EPS_LN = 1e-5
EPS_L2 = 1e-6
EPS_RMS = 1e-5
LN16 = 2.772588722239781
DEBUG_SPILLS = False

ENGINES = ("sync", "scalar", "tensor", "vector", "gpsimd")


class Op:
    __slots__ = ("idx", "eng", "fn", "deps", "dma", "need_inc", "val", "key")

    def __init__(self, idx, eng, fn, deps, dma, ind, key):
        self.idx = idx
        self.eng = eng
        self.fn = fn
        self.deps = deps
        self.dma = dma
        self.need_inc = False
        self.val = None
        self.key = key if key else eng + ("_i" if ind else "_d" if dma else "")


class Sched:
    def __init__(self, nc, ctx):
        self.nc = nc
        self.ctx = ctx
        self.ops = []

    def add(self, eng, fn, deps=(), dma=False, ind=False, key=None):
        flat = []
        for d in deps:
            if d is None:
                continue
            if isinstance(d, (list, tuple)):
                flat.extend(q for q in d if q is not None)
            else:
                flat.append(d)
        o = Op(len(self.ops), eng, fn, flat, dma or ind, ind, key)
        self.ops.append(o)
        return o

    def emit(self):
        for o in self.ops:
            if o.dma:
                o.need_inc = True
            for d in o.deps:
                assert d.idx < o.idx, "dependency must precede op"
                d.need_inc = True
        counters = {}
        for o in self.ops:
            if o.need_inc:
                counters[o.key] = counters.get(o.key, 0) + (16 if o.dma else 1)
                o.val = counters[o.key]
        sems = {k: self.ctx.enter_context(self.nc.semaphore("sem_" + k)) for k in counters}
        per_eng = {e: [o for o in self.ops if o.eng == e] for e in ENGINES}

        def stream(eng_name, eng):
            for o in per_eng[eng_name]:
                waits = {}
                for d in o.deps:
                    waits[d.key] = max(waits.get(d.key, 0), d.val)
                for key, v in waits.items():
                    eng.wait_ge(sems[key], v)
                r = o.fn(eng)
                if o.need_inc:
                    instrs = r if isinstance(r, (list, tuple)) else [r]
                    if o.dma:
                        assert len(instrs) == 1
                        instrs[0].then_inc(sems[o.key], 16)
                    else:
                        instrs[-1].then_inc(sems[o.key], 1)

        with self.nc.Block() as blk:
            for name in ENGINES:
                getattr(blk, name)(lambda e, n=name: stream(n, e))


def build_program():
    nc = bass.Bass("TRN2", target_bir_lowering=False)
    ctx = ExitStack()
    ap = lambda t: t.ap()

    x = ap(nc.dram_tensor("x", [B_LOC, T, D], F32, kind="ExternalInput"))
    wq = ap(nc.dram_tensor("wq", [D, D], F32, kind="ExternalInput"))
    wk = ap(nc.dram_tensor("wk", [D, D], F32, kind="ExternalInput"))
    wv = ap(nc.dram_tensor("wv", [D, D], F32, kind="ExternalInput"))
    wo = ap(nc.dram_tensor("wo", [D, D], F32, kind="ExternalInput"))
    cwq = ap(nc.dram_tensor("cwq", [D, 4], F32, kind="ExternalInput"))
    cwk = ap(nc.dram_tensor("cwk", [D, 4], F32, kind="ExternalInput"))
    cwv = ap(nc.dram_tensor("cwv", [D, 4], F32, kind="ExternalInput"))
    onw = ap(nc.dram_tensor("onw", [DH], F32, kind="ExternalInput"))
    fwq = ap(nc.dram_tensor("fwq", [D, D], F32, kind="ExternalInput"))
    fwk = ap(nc.dram_tensor("fwk", [D, D], F32, kind="ExternalInput"))
    fwv = ap(nc.dram_tensor("fwv", [D, D], F32, kind="ExternalInput"))
    fbq = ap(nc.dram_tensor("fbq", [D], F32, kind="ExternalInput"))
    fbk = ap(nc.dram_tensor("fbk", [D], F32, kind="ExternalInput"))
    fbv = ap(nc.dram_tensor("fbv", [D], F32, kind="ExternalInput"))
    fwe = ap(nc.dram_tensor("fwe", [NE, D, D], F32, kind="ExternalInput"))
    fbe = ap(nc.dram_tensor("fbe", [NE, D], F32, kind="ExternalInput"))
    fwsw = ap(nc.dram_tensor("fwsw", [NE, D], F32, kind="ExternalInput"))
    fbsw = ap(nc.dram_tensor("fbsw", [NE], F32, kind="ExternalInput"))
    fgam = ap(nc.dram_tensor("fgam", [D], F32, kind="ExternalInput"))
    fbet = ap(nc.dram_tensor("fbet", [D], F32, kind="ExternalInput"))
    pmask = ap(nc.dram_tensor("pmask", [NE, 1], F32, kind="ExternalInput"))
    out = ap(nc.dram_tensor("out", [B_LOC, T, D], F32, kind="ExternalOutput"))

    dbg = dict(kind="ExternalOutput") if DEBUG_SPILLS else {}
    sc_w = [ap(nc.dram_tensor(n, [D, D], BF16)) for n in ("scfwq", "scfwk", "scfwv")]
    sc_we = ap(nc.dram_tensor("scfwe", [NE, 128, 9, 1024], BF16))
    sp_attn = ap(nc.dram_tensor("spattn", [B_LOC, T, D], F32, **dbg))
    sp_hd = ap(nc.dram_tensor("sphd", [B_LOC, D, T], BF16, **dbg))
    sp_hdt = ap(nc.dram_tensor("sphdt", [B_LOC * T + 128, D], BF16, **dbg))
    denM = ap(nc.dram_tensor("denM", [16, T], F32))
    if DEBUG_SPILLS:
        dbgq = ap(nc.dram_tensor("dbgq", [128, KT * T], BF16, kind="ExternalOutput"))
        dbgk = ap(nc.dram_tensor("dbgk", [128, KT * T], BF16, kind="ExternalOutput"))
        dbgv = ap(nc.dram_tensor("dbgv", [128, KT * T], BF16, kind="ExternalOutput"))
        dbgo = ap(nc.dram_tensor("dbgo", [128, TT * DH], BF16, kind="ExternalOutput"))
        dbgon = ap(nc.dram_tensor("dbgon", [128, TT * DH], BF16, kind="ExternalOutput"))
        dbgx = ap(nc.dram_tensor("dbgx", [128, KT * T], BF16, kind="ExternalOutput"))
        dbgs = ap(nc.dram_tensor("dbgs", [128, 512], BF16, kind="ExternalOutput"))
        dbgc = ap(nc.dram_tensor("dbgc", [128, 512], F32, kind="ExternalOutput"))
    idxT = ap(nc.dram_tensor("idxT", [NSLOT, 2], mybir.dt.uint32))
    yP = ap(nc.dram_tensor("yP", [NSLOT, D], F32))
    slotT = ap(nc.dram_tensor("slotT", [TT * B_LOC, 128], mybir.dt.uint32))
    aoD = ap(nc.dram_tensor("aoD", [NE, 2], mybir.dt.uint32))
    cc_in_t = nc.dram_tensor("ccin", [1, NE], F32)
    cc_out_t = nc.dram_tensor("ccout", [NCORES, NE], F32)
    cc_in = ap(cc_in_t)
    cc_out = ap(cc_out_t)

    sb = lambda name, shape, dt: ctx.enter_context(nc.sbuf_tensor(name, shape, dt))
    xnD = sb("xnD", [128, KT * T], BF16)
    qD = sb("qD", [128, KT * T], BF16)
    kD = sb("kD", [128, KT * T], BF16)
    vD = sb("vD", [128, KT * T], BF16)
    stg = [sb(f"stg{i}", [128, 515], BF16) for i in range(2)]
    cvo = [sb(f"cvo{i}", [128, 512], F32) for i in range(2)]
    denQ = cvo[0][0:16, 0:512]
    xtb = [sb(f"xtb{i}", [128, D], F32) for i in range(2)]

    bnb = sb("bnb", [128, 12], F32)
    bnm = sb("bnm", [128, 2], F32)
    mrs = sb("mrs", [128, 2 * TT * B_LOC], F32)
    Sb1 = sb("Sb1", [128, 2 * DH], BF16)
    oT = sb("oT", [128, TT * DH], BF16)
    rmsq = sb("rmsq", [128, TT * DH], BF16)
    rmsr = sb("rmsr", [128, TT], F32)
    ev0 = sb("ev0", [128, D], F32)
    ev1 = sb("ev1", [128, D], F32)
    QpT = [sb(f"QpT{j}", [128, 128], F32R) for j in range(6)]
    QtT = [sb(f"QtT{j}", [128, 128], F32R) for j in range(7)]
    CabT = [sb(f"CabT{i}", [128, DH], F32R) for i in range(2)]
    QpR = [t[:, :] for t in QpT]
    QtR = [t[:, :] for t in QtT]
    CabR = [t[:, :] for t in CabT]
    CabF = [t[:, :].bitcast(F32) for t in CabT]
    S1 = xtb[1][:, 0:512]
    vaug = [sb(f"vaug{i}", [128, TT * 65], BF16) for i in range(2)]
    wbufA = [vaug[0], vaug[1]]
    slabT = sb("slabT", [128, 2048], BF16)
    slab = [slabT[:, i * 512:(i + 1) * 512] for i in range(4)]
    xnT = [slabT[:, 0:1024], slabT[:, 1024:2048]]
    KTt = slabT[:, 1024:1280]
    VTt = slabT[:, 1280:1536]
    Usb = vaug[0][:, 0:256]
    Gm = vaug[0][:, 256:384]
    tsum = sb("tsum", [65, 2 * TT], F32)
    sufc = sb("sufc", [65, 2 * NQ], F32)
    denr = [sb(f"denr{i}", [65, 512], F32) for i in range(2)]
    rowst = denr[1][0:1, 0:512]
    oddst = [sb(f"odd{i}", [64, 512], BF16) for i in range(2)]
    beS = [oddst[0][0:1, 0:512], oddst[1][0:1, 0:512]]
    lgt = sb("lgt", [128, NE], F32)
    prb = sb("prb", [128, NE], F32)
    rt0 = sb("rt0", [128, NE], F32)
    rt1 = sb("rt1", [128, 1], F32)
    rt2 = sb("rt2", [128, 1], F32)
    rt3 = sb("rt3", [128, 1], F32)
    oneh8 = sb("oneh8", [128, NE], BF16)
    selP = sb("selP", [128, TT * B_LOC], F32)
    lpos = sb("lpos", [128, TT * B_LOC], F32)
    carry = sb("carry", [1, NE], F32)
    base8 = sb("base8", [128, NE], F32)
    selZb = sb("selZb", [128, TT * B_LOC], F32)
    selYb = sb("selYb", [128, TT * B_LOC], F32)
    slotu = sb("slotu", [128, 1], mybir.dt.uint32)
    idxC = sb("idxC", [128, 2 * TT * B_LOC], mybir.dt.uint32)
    idxS = [sb(f"idxS{i}", [128, 2], mybir.dt.uint32) for i in range(2)]
    oobU = sb("oobU", [128, 16], mybir.dt.uint32)
    routesF = sb("routesF", [128, TT * B_LOC], F32)
    widxP = sb("widxP", [128, 1], F32)
    egsb = sb("egsb", [128, NE], F32)
    widxF = egsb
    widxU = sb("widxU", [128, KT], mybir.dt.uint32)
    egcol = sb("egcol", [128, 1], F32)
    b2row = sb("b2row", [1, 48], F32)
    b2cb = sb("b2cb", [128, 1], BF16)
    egbU = sb("egbU", [128, 1], mybir.dt.uint32)
    widx24F = sb("widx24F", [128, 24], F32)
    widx24U = sb("widx24U", [128, 24], mybir.dt.uint32)
    slotS = idxS
    aosel = widxU[:, 0:2]
    ridxU = sb("ridxU", [128, 1], mybir.dt.uint32)
    cnts = sb("cnts", [NCORES, NE], F32)
    pmrow = sb("pmrow", [NE, 1], F32)
    ident = sb("ident", [128, 128], BF16)
    m_ltn = sb("m_ltn", [128, 128], BF16)
    m_utn = sb("m_utn", [128, 128], BF16)
    m_uti = sb("m_uti", [128, 128], BF16)
    m_fut = sb("m_fut", [128, 128], I8)
    onesb = sb("onesb", [128, 128], BF16)
    negb = sb("negb", [128, 128], F32)
    ones1 = sb("ones1", [1, 128], BF16)
    ones1f = sb("ones1f", [1, 128], F32)
    onescol = sb("onescol", [128, 1], BF16)
    iota8 = sb("iota8", [128, NE], F32)
    cst = sb("cst", [128, 10], F32)
    convw = sb("convw", [128, 3 * KT * 4], F32)
    onwB = sb("onwB", [128, DH], F32)
    gamB = sb("gamB", [128, D], F32)
    betB = sb("betB", [128, D], F32)
    bvB = sb("bvB", [128, D], BF16)
    bqc = sb("bqc", [128, KT], F32)
    bkc = sb("bkc", [128, KT], F32)
    wswT = sb("wswT", [128, KT * NE], BF16)
    bswB = sb("bswB", [128, NE], F32)

    psf = lambda name, shape: ctx.enter_context(nc.psum_tensor(name, shape, F32))
    ps = [psf(f"ps{i}", [128, 512]) for i in range(4)]
    bank4 = psf("bank4", [128, 512])
    bank5 = psf("bank5", [128, 512])
    bank6 = psf("bank6", [128, 512])
    bank7 = psf("bank7", [128, 512])
    pA = bank4[:, 0:128]
    pR = bank4[:, 128:136]
    pRc = bank4[0:1, 136:144]
    pB = bank5[:, 0:128]
    pL = bank5[0:1, 128:512]
    pL = bank5[0:1, 0:512]
    pC = bank6[:, 0:DH]
    pD = bank7[:, 0:DH]
    pT = bank7[:, 0:64].bitcast(BF16)
    cc_sem = ctx.enter_context(nc.semaphore("cc_sem"))

    S = Sched(nc, ctx)
    add = S.add

    def f_memsets(v):
        v.memset(onesb[:, :], 1.0)
        v.memset(ones1[:, :], 1.0)
        v.memset(ones1f[:, :], 1.0)
        v.memset(onescol[:, :], 1.0)
        v.memset(negb[:, :], -1.0)
        v.memset(cst[:, 0:1], EPS_LN)
        v.memset(cst[:, 1:2], EPS_L2)
        v.memset(cst[:, 2:3], EPS_RMS)
        v.memset(cst[:, 3:4], -1000.0)
        v.memset(cst[:, 4:5], 1.0)
        v.memset(cst[:, 5:6], float(CAP))
        v.memset(cst[:, 6:7], -LN16)
        v.memset(cst[:, 7:8], 0.0)
        v.memset(cst[:, 8:9], float(GR))
        v.memset(cst[:, 9:10], 65535.0)
        v.memset(carry[:, :], 0.0)
        return v.memset(cst[:, 7:8], 0.0)

    o_ms = add("vector", f_memsets)

    def f_consts(g):
        g.iota(iota8[:, :], [[1, NE]], base=0, channel_multiplier=0,
               allow_small_or_imprecise_dtypes=True)
        g.affine_select(ident[:, :], onesb[:, :], [[-1, 128]], ALU.is_equal, 0.0,
                        base=0, channel_multiplier=1)
        g.affine_select(m_uti[:, :], onesb[:, :], [[1, 128]], ALU.is_ge, 0.0,
                        base=0, channel_multiplier=-1)
        g.affine_select(m_ltn[:, :], negb[:, :], [[-1, 128]], ALU.is_gt, 0.0,
                        base=0, channel_multiplier=1)
        r = g.affine_select(m_utn[:, :], negb[:, :], [[1, 128]], ALU.is_gt, 0.0,
                            base=0, channel_multiplier=-1)
        return r

    o_cst = add("gpsimd", f_consts, deps=[o_ms])
    o_idc = add("gpsimd", lambda g: g.iota(idxC[:, :], [[128, TT * B_LOC], [0, 2]], base=0,
                                           channel_multiplier=1), deps=[o_ms])
    o_oob = add("gpsimd", lambda g: g.iota(oobU[:, :], [[0, 16]], base=B_LOC * T,
                                           channel_multiplier=1), deps=[o_ms])
    o_ikp = add("gpsimd", lambda g: g.iota(widxP[:, :], [[0, 1]], base=0,
                                           channel_multiplier=1,
                                           allow_small_or_imprecise_dtypes=True), deps=[o_ms])
    idxTv = idxT[:, :].rearrange("(a b) c -> b a c", b=128)
    o_pref = None
    for pi in range(12):
        o_pref = add("gpsimd", lambda g, pi=pi: g.dma_start(
            out=idxTv[:, pi * 8:(pi + 1) * 8, :],
            in_=oobU[:, :].rearrange("p (a c) -> p a c", c=2),
            allow_slow_non_contiguous=True), deps=[o_oob, o_pref], dma=True)
    o_z0 = add("vector", lambda v: v.memset(ev0[:, :], 0.0), deps=[])
    o_zhd = add("gpsimd", lambda g: g.dma_start(
        out=sp_hdt[B_LOC * T:B_LOC * T + 128, :], in_=ev0[:, :]), deps=[o_z0], dma=True)
    o_zlist = [o_zhd]
    o_f0 = add("gpsimd", lambda g: g.memset(m_fut[:, :], 1), deps=[o_cst])
    o_cst = add("gpsimd", lambda g: g.affine_select(
        m_fut[:, :], m_fut[:, :], [[-1, 128]], ALU.is_gt, 0, base=0, channel_multiplier=1),
        deps=[o_f0])

    for si, cw in enumerate([cwq, cwk, cwv]):
        for k in range(KT):
            add("gpsimd", lambda g, si=si, k=k, cw=cw: g.dma_start(
                out=convw[:, (si * KT + k) * 4:(si * KT + k) * 4 + 4],
                in_=cw[k * 128:(k + 1) * 128, :]), dma=True)
    for k in range(KT):
        add("gpsimd", lambda g, k=k: g.dma_start(
            out=bqc[:, k:k + 1], in_=fbq[k * 128:(k + 1) * 128].rearrange("(a b) -> a b", b=1)), dma=True)
        add("gpsimd", lambda g, k=k: g.dma_start(
            out=bkc[:, k:k + 1], in_=fbk[k * 128:(k + 1) * 128].rearrange("(a b) -> a b", b=1)), dma=True)
    for k in range(KT):
        add("gpsimd", lambda g, k=k: g.dma_start(
            out=wswT[:, k * NE:(k + 1) * NE],
            in_=fwsw.rearrange("e d -> d e")[k * 128:(k + 1) * 128, :],
            allow_slow_non_contiguous=True), dma=True)
    o_parm_last = add("gpsimd", lambda g: g.dma_start(out=pmrow[:, :], in_=pmask[:, :]), dma=True)

    o_prev = o_ms
    for srcv, dst, n in [(onw, onwB, DH), (fgam, gamB, D), (fbet, betB, D),
                         (fbv, bvB, D), (fbsw, bswB, NE)]:
        for cc in range(0, n, 512):
            w = min(512, n - cc)
            o_row = add("gpsimd", lambda g, srcv=srcv, cc=cc, w=w: g.dma_start(
                out=rowst[0:1, 0:w], in_=srcv[cc:cc + w].rearrange("(b a) -> b a", b=1)),
                deps=[o_prev], dma=True)
            o_m = add("tensor", lambda t, w=w: t.matmul(
                ps[0][:, 0:w], ones1f[:, :], rowst[0:1, 0:w], start=True, stop=True),
                deps=[o_row, o_prev])
            o_prev = add("vector", lambda v, dst=dst, cc=cc, w=w: v.tensor_copy(
                dst[:, cc:cc + w], ps[0][:, 0:w]), deps=[o_m])
    o_bc_done = o_prev

    o_bstage_prev = None
    p0rot = [pT, bank6[:, 0:64].bitcast(BF16),
             ps[2][:, 0:64].bitcast(BF16), ps[3][:, 0:64].bitcast(BF16)]
    o_p0rot = [None, None, None, None]
    o_prev_ld = [None, None]
    o_prev_ev = None
    o_prev_stores = []
    for src, dst, e in ([(fwq, sc_w[0], None), (fwk, sc_w[1], None), (fwv, sc_w[2], None)] +
                        [(fwe, sc_we, ee) for ee in range(NE)]):
        o_evs = []
        for r in range(8):
            inap = src[r * 128:(r + 1) * 128, :] if e is None else src[e, r * 128:(r + 1) * 128, :]
            o_ld = add("gpsimd", lambda g, r=r, inap=inap: g.dma_start(
                out=wbufA[r % 2][:, 0:1024], in_=inap), deps=[o_prev_ld[r % 2]], dma=True)
            for dk in range(8):
                rk = dk % 4
                prt = p0rot[rk]
                o_tr = add("tensor", lambda t, r=r, dk=dk, prt=prt: t.transpose(
                    prt, wbufA[r % 2][:, dk * 128:(dk + 1) * 128], ident[:, :]),
                    deps=[o_ld, o_cst, o_p0rot[rk]])
                o_prev_ev = add("vector", lambda v, r=r, dk=dk, prt=prt: v.tensor_copy(
                    kD[:, dk * 1024 + r * 128:dk * 1024 + (r + 1) * 128], prt),
                    deps=[o_tr] + o_prev_stores)
                o_p0rot[rk] = o_prev_ev
                o_evs.append(o_prev_ev)
            o_prev_ld[r % 2] = o_evs[-1]
        o_prev_stores = []
        for dk in range(8):
            if e is None:
                outap = dst[dk * 128:(dk + 1) * 128, :]
            else:
                outap = dst[e, :, dk, :]
            o_prev_stores.append(add("gpsimd", lambda g, outap=outap, dk=dk: g.dma_start(
                out=outap, in_=kD[:, dk * 1024:(dk + 1) * 1024]), deps=[o_evs], dma=True))
        if e is not None:
            o_br1 = add("gpsimd", lambda g, e=e: g.dma_start(
                out=rowst[0:1, 0:512], in_=fbe[e, 0:512].rearrange("(b a) -> b a", b=1)),
                deps=[o_bc_done, o_bstage_prev], dma=True)
            o_bm1 = add("tensor", lambda t: t.matmul(ps[0][:, 0:512], ones1f[:, :],
                                                     rowst[0:1, 0:512], start=True, stop=True),
                        deps=[o_br1])
            o_be1 = add("vector", lambda v: v.tensor_copy(kD[:, 8192:8704], ps[0][:, 0:512]),
                        deps=[o_bm1, o_bstage_prev])
            o_br2 = add("gpsimd", lambda g, e=e: g.dma_start(
                out=rowst[0:1, 0:512], in_=fbe[e, 512:1024].rearrange("(b a) -> b a", b=1)),
                deps=[o_be1], dma=True)
            o_bm2 = add("tensor", lambda t: t.matmul(ps[0][:, 0:512], ones1f[:, :],
                                                     rowst[0:1, 0:512], start=True, stop=True),
                        deps=[o_br2])
            o_be2 = add("vector", lambda v: v.tensor_copy(kD[:, 8704:9216], ps[0][:, 0:512]),
                        deps=[o_bm2])
            o_bst = add("gpsimd", lambda g, e=e: g.dma_start(
                out=sc_we[e, :, 8, :], in_=kD[:, 8192:9216]), deps=[o_be2], dma=True)
            o_bstage_prev = o_bst
            o_prev_stores.append(o_bst)
    o_wt_stores = o_prev_stores

    last_xnD = list(o_wt_stores)
    last_qD = [o_prev_ld[0], o_prev_ld[1]] + list(o_wt_stores)
    last_kD = list(o_wt_stores)
    last_vD = []
    last_oT = []
    o_route_tail = o_ms
    o_idx_all = []
    o_rts = {}
    o_cm_prev = None
    o_tt2_prev = None
    o_conv_dump = None
    o_psE = None
    o_psO = None
    o_pv_war = [None, None]
    o_rc = None
    o_spA = {}
    o_spH = {}
    o_spHT = {}
    last_wbufA = [None, None]
    o_scale = []
    all_last_A7 = []

    for b in range(B_LOC):
        o_tr_xn = None
        o_first_trs = []
        o_prev_ts = [None, None]
        o_ln_tail = None
        for tt in range(TT):
            o_xl = add("sync", lambda s, tt=tt, b=b: s.dma_start(
                out=xtb[tt % 2][:, :], in_=x[b, tt * 128:(tt + 1) * 128, :]),
                deps=[o_prev_ts[tt % 2]] +
                     ([o_route_tail] if tt < 2 and b > 0 else []), dma=True)
            o_s1 = add("vector", lambda v, tt=tt: v.bn_stats(bnb[:, 0:6], xtb[tt % 2][:, 0:512]),
                       deps=[o_xl, o_ln_tail])
            o_s2 = add("vector", lambda v, tt=tt: v.bn_stats(bnb[:, 6:12], xtb[tt % 2][:, 512:1024]),
                       deps=[o_s1])
            o_ag = add("vector", lambda v: v.bn_aggr(bnb[:, 0:2], bnb[:, 0:12]), deps=[o_s2])
            mcol = mrs[:, (b * TT + tt) * 2:(b * TT + tt) * 2 + 1]
            rcol = mrs[:, (b * TT + tt) * 2 + 1:(b * TT + tt) * 2 + 2]
            o_mc = add("vector", lambda v, mcol=mcol: v.tensor_copy(mcol, bnb[:, 0:1]),
                       deps=[o_ag])
            o_l1 = add("scalar", lambda sc: sc.activation(rt1[:, :], bnb[:, 1:2], AF.Ln,
                                                          bias=cst[:, 0:1]),
                       deps=[o_ag, o_bc_done if b == 0 and tt == 0 else None])
            o_l2 = add("scalar", lambda sc, rcol=rcol: sc.activation(rcol, rt1[:, :], AF.Exp,
                                                                     scale=-0.5), deps=[o_l1])
            o_ts = add("vector", lambda v, tt=tt, mcol=mcol, rcol=rcol: v.tensor_scalar(
                xnT[tt % 2][:, :], xtb[tt % 2][:, :], mcol, rcol, ALU.subtract, ALU.mult),
                deps=[o_l2, o_mc, o_tr_xn] +
                     ((o_pv_war + [o_route_tail]) if tt < 2 else []))
            o_prev_ts[tt % 2] = o_ts
            o_ln_tail = o_ts
            for k in range(KT):
                o_tr_xn = add("scalar", lambda sc, tt=tt, k=k: sc.dma_start_transpose(
                    xnD[:, k * T + tt * 128:k * T + (tt + 1) * 128],
                    xnT[tt % 2][:, k * 128:(k + 1) * 128]),
                    deps=[o_ts] + (last_xnD if tt == 0 else []), dma=True)
                if tt == 0:
                    o_first_trs.append(o_tr_xn)
        last_xnD = []

        if DEBUG_SPILLS and b == 0:
            o_dbgx = add("gpsimd", lambda g: g.dma_start(out=dbgx[:, :], in_=xnD[:, :]),
                         deps=[o_tr_xn], dma=True)
        dn_src = {}
        o_pl_last = None
        o_ps2_last = None
        o_lastmm_all = []
        for s_i, (wsrc, dstD) in enumerate([(wq, qD), (wk, kD), (wv, vD)]):
            dlast = [last_qD, last_kD, last_vD][s_i]
            cwoff = s_i * KT * 4
            o_wld = {}
            o_prev_ev = [None, None]
            o_silu = {}
            o_zs = {}
            o_conv_prev = {}
            for h in range(H_DN):
                for mi2, mi in enumerate((2 * h, 2 * h + 1)):
                    o_wld[mi] = add("gpsimd", lambda g, mi=mi, wsrc=wsrc: g.dma_start(
                        out=wbufA[mi % 2][:, 0:1024].rearrange("p (k m) -> p k m", k=KT),
                        in_=wsrc[:, mi * 128:(mi + 1) * 128].rearrange("(k p) m -> p k m", p=128)),
                        deps=[last_wbufA[mi % 2]], dma=True)
                    last_wbufA[mi % 2] = o_wld[mi]
                for mi2 in range(2):
                    o_zs[mi2] = add("vector", lambda v, mi2=mi2: v.memset(stg[mi2][:, 0:3], 0.0),
                                    deps=[o_conv_prev.get(mi2)])
                o_l2tail = None
                for qc in range(NQ):
                    for mi2, mi in enumerate((2 * h, 2 * h + 1)):
                        gi = mi * NQ + qc
                        pb = ps[gi % 2]
                        o_mm = add("tensor", lambda t, mi=mi, qc=qc, pb=pb: [
                            t.matmul(pb[:, :], wbufA[mi % 2][:, k * 128:(k + 1) * 128],
                                     xnD[:, k * T + qc * 512:k * T + (qc + 1) * 512],
                                     start=(k == 0), stop=(k == KT - 1))
                            for k in range(KT)][-1],
                            deps=[o_wld[mi], o_tr_xn, o_prev_ev[gi % 2]])
                        o_lastmm_all.append(o_mm)
                        last_wbufA[mi % 2] = o_mm
                        o_ev = add("vector", lambda v, mi2=mi2, pb=pb: v.tensor_copy(
                            stg[mi2][:, 3:515], pb[:, :]),
                            deps=[o_mm, o_zs[mi2]])
                        if DEBUG_SPILLS and b == 0 and s_i == 2 and mi == 0 and qc == 0:
                            o_dmp = add("gpsimd", lambda g: g.dma_start(
                                out=dbgs[:, :], in_=stg[0][:, 3:515]), deps=[o_ev], dma=True)
                            o_ev = o_dmp
                        o_prev_ev[gi % 2] = o_ev
                        wtap = [convw[:, cwoff + mi * 4 + tap:cwoff + mi * 4 + tap + 1]
                                for tap in range(4)]
                        o_c = add("vector", lambda v, mi2=mi2, w3=wtap[3]: v.tensor_scalar(
                            cvo[mi2][:, :], stg[mi2][:, 3:515], w3, None, ALU.mult),
                            deps=[o_ev, o_silu.get((mi2, qc - 1))] +
                                 ([o_parm_last] if b == 0 and s_i == 0 and h == 0 and qc == 0 else []) +
                                 ([o_rc] if b > 0 and s_i == 0 and h == 0 and qc == 0 else []))
                        for tap in (2, 1, 0):
                            o_c = add("vector", lambda v, mi2=mi2, tap=tap, wt=wtap[tap]:
                                      v.scalar_tensor_tensor(
                                          cvo[mi2][:, :], stg[mi2][:, tap:tap + 512], wt,
                                          cvo[mi2][:, :], ALU.mult, ALU.add), deps=[o_c])
                        o_cc = add("vector", lambda v, mi2=mi2: v.tensor_copy(
                            stg[mi2][:, 0:3], stg[mi2][:, 512:515]), deps=[o_c])
                        if DEBUG_SPILLS and b == 0 and s_i == 2 and mi == 0 and qc == 0:
                            o_dm2 = add("gpsimd", lambda g: g.dma_start(
                                out=dbgc[:, :], in_=cvo[0][:, :]), deps=[o_c], dma=True)
                            o_conv_dump = o_dm2
                        o_zs[mi2] = o_cc
                        o_conv_prev[mi2] = o_cc
                        dbg_cd = [o_conv_dump] if (DEBUG_SPILLS and b == 0 and s_i == 2
                                                    and mi == 0 and qc == 0) else []
                        if s_i < 2:
                            o_si = add("scalar", lambda sc, mi2=mi2: sc.activation(
                                cvo[mi2][:, :], cvo[mi2][:, :], AF.Silu), deps=[o_c] + dbg_cd)
                        else:
                            o_si = add("scalar", lambda sc, mi=mi, qc=qc, mi2=mi2: sc.activation(
                                dstD[:, mi * T + qc * 512:mi * T + (qc + 1) * 512],
                                cvo[mi2][:, :], AF.Silu),
                                deps=[o_c] + dbg_cd + (dlast if mi == 0 and qc == 0 else []))
                            dn_src[(s_i, mi, qc)] = o_si
                        o_silu[(mi2, qc)] = o_si
                    if s_i < 2:
                        o_q0 = add("scalar", lambda sc: sc.activation(
                            slab[0][:, :], cvo[0][:, :], AF.Square),
                            deps=[o_silu[(0, qc)], o_l2tail] +
                                 ([o_tr_xn] if s_i == 0 and h == 0 and qc == 0 else []))
                        o_q1 = add("scalar", lambda sc: sc.activation(
                            slab[1][:, :], cvo[1][:, :], AF.Square),
                            deps=[o_silu[(1, qc)], o_l2tail])
                        o_sm = add("tensor", lambda t: [
                            t.matmul(pL[:, :], onescol[:, :], slab[0][:, :], start=True, stop=False),
                            t.matmul(pL[:, :], onescol[:, :], slab[1][:, :], start=False, stop=True)][-1],
                            deps=[o_q0, o_q1, o_pl_last])
                        o_r1 = add("scalar", lambda sc: sc.activation(
                            rowst[0:1, 0:512], pL[:, :], AF.Ln, bias=cst[0:1, 1:2]), deps=[o_sm])
                        o_pl_last = o_r1
                        b_ap = cst[0:1, 6:7] if s_i == 0 else cst[0:1, 7:8]
                        o_r2 = add("scalar", lambda sc, b_ap=b_ap: sc.activation(
                            rowst[0:1, 0:512], rowst[0:1, 0:512], AF.Exp, bias=b_ap,
                            scale=-0.5), deps=[o_r1])
                        o_bc = add("tensor", lambda t: t.matmul(
                            ps[2][:, :], ones1f[:, :], rowst[0:1, 0:512], start=True, stop=True),
                            deps=[o_r2, o_ps2_last])
                        o_t0 = add("vector", lambda v, h=h, qc=qc, dstD=dstD: v.tensor_tensor(
                            dstD[:, (2 * h) * T + qc * 512:(2 * h) * T + (qc + 1) * 512],
                            cvo[0][:, :], ps[2][:, :], ALU.mult),
                            deps=[o_bc] + (dlast if h == 0 and qc == 0 else []))
                        o_t1 = add("vector", lambda v, h=h, qc=qc, dstD=dstD: v.tensor_tensor(
                            dstD[:, (2 * h + 1) * T + qc * 512:(2 * h + 1) * T + (qc + 1) * 512],
                            cvo[1][:, :], ps[2][:, :], ALU.mult), deps=[o_bc])
                        o_ps2_last = o_t1
                        o_l2tail = o_t1
                        dn_src[(s_i, 2 * h, qc)] = o_t0
                        dn_src[(s_i, 2 * h + 1, qc)] = o_t1
        last_qD = []
        last_kD = []
        last_vD = []

        dbg_ops = []
        if DEBUG_SPILLS and b == 0:
            alldn = [dn_src[(s2, m2, q2)] for s2 in range(3) for m2 in range(KT)
                     for q2 in range(NQ)]
            dbg_ops.append(add("gpsimd", lambda g: g.dma_start(out=dbgq[:, :], in_=qD[:, :]),
                               deps=alldn, dma=True))
            dbg_ops.append(add("gpsimd", lambda g: g.dma_start(out=dbgk[:, :], in_=kD[:, :]),
                               deps=alldn, dma=True))
            dbg_ops.append(add("gpsimd", lambda g: g.dma_start(out=dbgv[:, :], in_=vD[:, :]),
                               deps=alldn, dma=True))
        o_pAe = None
        o_pBe = None
        o_pCe = o_ps2_last and None
        o_pDe = None
        o_ps23_init = o_ps2_last
        o_ps2_war = None
        o_ps3_war = None
        o_ps3f_war = None
        o_pb16_war = None
        o_tt_war = []
        o_b6_war = []
        o_om_prev = None
        o_usb_war = None
        o_setK_war = [[], []]
        o_setV_war = [None, None]
        o_setTV_war = [None, None]
        o_setTK_war = [None, None]
        o_dn_last = []
        o_trV2_list = []
        o_prev_head_end = None
        o_odt_war = [None, None]
        o_ps01_init = o_l2tail
        Pb16 = QtT[2][:, 0:64].bitcast(BF16)
        Tt_sb = QtT[3][:, 0:64].bitcast(BF16)
        for h in range(H_DN):
            m0, m1 = 2 * h, 2 * h + 1
            o_Su = add("vector", lambda v: [
                v.memset(S1[:, :], 0.0), v.memset(Sb1[:, :], 0.0)][-1],
                deps=[dn_src[(0, 2 * h, 0)], o_prev_head_end])
            o_oTev = []
            dep_k = [dn_src[(1, m0, qc)] for qc in range(NQ)] + \
                    [dn_src[(1, m1, qc)] for qc in range(NQ)]
            dep_q = [dn_src[(0, m0, qc)] for qc in range(NQ)] + \
                    [dn_src[(0, m1, qc)] for qc in range(NQ)]
            dep_v = [dn_src[(2, m0, qc)] for qc in range(NQ)] + \
                    [dn_src[(2, m1, qc)] for qc in range(NQ)]
            for ci in range(NCH):
                t0 = ci * 128
                par = (h * NCH + ci) % 2
                sbase = 1024 * par
                KTs = slabT[:, sbase + 0:sbase + 256]
                VTs = slabT[:, sbase + 256:sbase + 512]
                TKs = slabT[:, sbase + 512:sbase + 768]
                TVs2 = slabT[:, sbase + 768:sbase + 1024]
                kd0 = kD[:, m0 * T + t0:m0 * T + t0 + 128]
                kd1 = kD[:, m1 * T + t0:m1 * T + t0 + 128]
                qd0 = qD[:, m0 * T + t0:m0 * T + t0 + 128]
                qd1 = qD[:, m1 * T + t0:m1 * T + t0 + 128]
                vd0 = vD[:, m0 * T + t0:m0 * T + t0 + 128]
                vd1 = vD[:, m1 * T + t0:m1 * T + t0 + 128]
                last_ch = (ci == NCH - 1)
                ps2b0 = ps[2][:, 0:64].bitcast(BF16)
                ps2b1 = ps[2][:, 64:128].bitcast(BF16)
                ps2b01 = ps[2][:, 0:128].bitcast(BF16)
                ps3b0 = ps[3][:, 0:64].bitcast(BF16)
                ps3b1 = ps[3][:, 64:128].bitcast(BF16)
                ps3b01 = ps[3][:, 0:128].bitcast(BF16)
                ps3f = ps[3][:, 256:384]
                o_ptK1 = add("tensor", lambda t, kd0=kd0, ps2b0=ps2b0: t.transpose(
                    ps2b0, kd0, ident[:, :]),
                    deps=dep_k + [o_ps2_war] + o_setK_war[par] +
                         ([o_ps23_init] if o_ps23_init else []))
                o_ptK2 = add("tensor", lambda t, kd1=kd1, ps2b1=ps2b1: t.transpose(
                    ps2b1, kd1, ident[:, :]), deps=[o_ptK1])
                o_evK = add("vector", lambda v, KTs=KTs, ps2b01=ps2b01: v.tensor_copy(
                    KTs, ps2b01), deps=[o_ptK2])
                o_ptV1 = add("tensor", lambda t, vd0=vd0, ps3b0=ps3b0: t.transpose(
                    ps3b0, vd0, ident[:, :]),
                    deps=dep_v + [o_ps3_war, o_setV_war[par]] +
                         ([o_ps23_init] if o_ps23_init else []))
                o_ps23_init = None
                o_ptV2 = add("tensor", lambda t, vd1=vd1, ps3b1=ps3b1: t.transpose(
                    ps3b1, vd1, ident[:, :]), deps=[o_ptV1])
                o_evV = add("vector", lambda v, VTs=VTs, ps3b01=ps3b01: v.tensor_copy(
                    VTs, ps3b01), deps=[o_ptV2])
                o_trV2_list.append(o_evV)
                o_A = add("tensor", lambda t, kd0=kd0, kd1=kd1: [
                    t.matmul(pA[:, :], kd0, kd0, start=True, stop=False),
                    t.matmul(pA[:, :], kd1, kd1, start=False, stop=True)][-1],
                    deps=dep_k + [o_pAe])
                o_Q0 = add("vector", lambda v: v.tensor_tensor(QpR[0], pA[:, :],
                                                               m_ltn[:, :], ALU.mult),
                           deps=[o_A] + ([o_spA.get((b - 1, TT - 2)), o_spA.get((b - 1, TT - 1))]
                                         if h == 0 and ci == 0 else []))
                o_Qt0 = add("vector", lambda v: v.tensor_tensor(QtR[0], pA[:, :],
                                                                m_utn[:, :], ALU.mult),
                            deps=[o_Q0, o_A])
                o_pAe = o_Qt0
                o_P0 = add("vector", lambda v: v.tensor_tensor(
                    QpR[2], QpT[0][:, :].bitcast(F32), ident[:, :], ALU.add),
                    deps=[o_Q0, o_A])
                o_qp_prev, o_qt_prev, o_p_prev = o_Q0, o_Qt0, o_P0
                for j in range(1, 7):
                    c0, c1 = (j - 1) % 2, j % 2
                    o_mq = add("tensor", lambda t, c0=c0: t.matmul(
                        pA[:, :], QtR[c0], QpR[c0], start=True, stop=True),
                        deps=[o_qp_prev, o_qt_prev, o_pAe])
                    o_qp = add("vector", lambda v, c1=c1: v.tensor_copy(QpR[c1], pA[:, :]),
                               deps=[o_mq])
                    o_pAe = o_qp
                    o_mqt = add("tensor", lambda t, c0=c0: t.matmul(
                        pB[:, :], QpR[c0], QtR[c0], start=True, stop=True),
                        deps=[o_qp_prev, o_qt_prev, o_pBe])
                    o_qt = add("vector", lambda v, c1=c1: v.tensor_copy(QtR[c1], pB[:, :]),
                               deps=[o_mqt])
                    o_pBe = o_qt
                    o_mp = add("tensor", lambda t, c1=c1, c0=c0, ps3f=ps3f: t.matmul(
                        ps3f, QtR[c1], QpR[2 + c0], start=True, stop=True),
                        deps=[o_qt, o_p_prev, o_ps3f_war, o_evV if j == 1 else None])
                    o_pev = add("vector", lambda v, c1=c1, c0=c0, ps3f=ps3f: v.tensor_tensor(
                        QpR[2 + c1], QpT[2 + c0][:, :].bitcast(F32), ps3f, ALU.add),
                        deps=[o_mp])
                    o_ps3f_war = o_pev
                    o_ps3_war = o_pev
                    o_qp_prev, o_qt_prev, o_p_prev = o_qp, o_qt, o_pev
                o_pb16 = add("vector", lambda v: v.tensor_copy(
                    Pb16, QpT[2][:, :].bitcast(F32)), deps=[o_p_prev, o_pb16_war])
                o_ttr = add("tensor", lambda t, ps2b0=ps2b0: t.transpose(
                    ps2b0, Pb16, ident[:, :]), deps=[o_pb16, o_evK])
                o_evTt = add("vector", lambda v, ps2b0=ps2b0: v.tensor_copy(Tt_sb, ps2b0),
                             deps=[o_ttr] + o_tt_war)
                o_ps2_war = o_evTt
                o_TV = add("tensor", lambda t, VTs=VTs: t.matmul(
                    bank4[:, 0:256], Tt_sb, VTs, start=True, stop=True),
                    deps=[o_evTt, o_evV, o_pAe])
                o_TVev = add("vector", lambda v, TVs2=TVs2: v.tensor_copy(
                    TVs2, bank4[:, 0:256]), deps=[o_TV, o_setTV_war[par]])
                o_pAe = o_TVev
                o_TK1 = add("tensor", lambda t, KTs=KTs: t.matmul(
                    bank5[:, 0:128], KTs[:, 0:128], Tt_sb, start=True, stop=True),
                    deps=[o_evTt, o_evK, o_pBe])
                o_TK2 = add("tensor", lambda t, KTs=KTs: t.matmul(
                    bank5[:, 128:256], KTs[:, 128:256], Tt_sb, start=True, stop=True),
                    deps=[o_TK1])
                o_TKev = add("vector", lambda v, TKs=TKs: v.tensor_copy(
                    TKs, bank5[:, 0:256]), deps=[o_TK2, o_setTK_war[par]])
                o_pBe = o_TKev
                o_tt_war = [o_TV, o_TK2]
                o_pb16_war = o_ttr
                o_G = add("tensor", lambda t, kd0=kd0, kd1=kd1, qd0=qd0, qd1=qd1: [
                    t.matmul(bank6[:, 256:384], kd0, qd0, start=True, stop=False),
                    t.matmul(bank6[:, 256:384], kd1, qd1, start=False, stop=True)][-1],
                    deps=dep_q + o_b6_war)
                o_Gm = add("vector", lambda v: v.tensor_tensor(Gm[:, :], bank6[:, 256:384],
                                                               m_uti[:, :], ALU.mult),
                           deps=[o_G, o_om_prev])
                o_U12 = add("tensor", lambda t, TKs=TKs: [
                    t.matmul(pC[:, :], TKs[:, 0:128], Sb1[:, 0:DH], start=True, stop=False),
                    t.matmul(pC[:, :], TKs[:, 128:256], Sb1[:, DH:2 * DH],
                             start=False, stop=True)][-1],
                    deps=[o_TKev, o_Su, o_pCe, o_Gm])
                o_Uev = add("vector", lambda v, TVs2=TVs2: v.tensor_tensor(
                    Usb[:, :], TVs2, pC[:, :], ALU.subtract),
                    deps=[o_U12, o_TVev, o_usb_war])
                o_pCe = o_Uev
                o_setTV_war[par] = o_Uev
                o_setTK_war[par] = o_U12
                o_setV_war[par] = o_TV
                o_om = add("tensor", lambda t, qd0=qd0, qd1=qd1: [
                    t.matmul(pC[:, :], qd0, Sb1[:, 0:DH], start=True, stop=False),
                    t.matmul(pC[:, :], qd1, Sb1[:, DH:2 * DH], start=False, stop=False),
                    t.matmul(pC[:, :], Gm[:, :], Usb[:, :], start=False, stop=True)][-1],
                    deps=dep_q + [o_Uev, o_Gm, o_Su, o_pCe])
                o_om_prev = o_om
                o_oe = add("vector", lambda v, ci=ci: v.tensor_copy(
                    oT[:, ci * DH:(ci + 1) * DH], pC[:, :]),
                    deps=[o_om] + (last_oT if ci == 0 and h == 0 else []))
                o_pCe = o_oe
                o_b6_war = [o_om, o_oe]
                o_oTev.append(o_oe)
                if not last_ch:
                    o_su1 = add("tensor", lambda t, KTs=KTs: t.matmul(
                        pD[:, :], KTs[:, 0:128], Usb[:, :], start=True, stop=True),
                        deps=[o_Uev, o_evK, o_pDe, o_om])
                    o_sa1 = add("vector", lambda v: v.tensor_tensor(
                        S1[:, 0:DH], S1[:, 0:DH], pD[:, :], ALU.add),
                        deps=[o_su1, o_Su])
                    o_sb1 = add("vector", lambda v: v.tensor_copy(
                        Sb1[:, 0:DH], S1[:, 0:DH]), deps=[o_sa1])
                    o_su2 = add("tensor", lambda t, KTs=KTs: t.matmul(
                        pD[:, :], KTs[:, 128:256], Usb[:, :], start=True, stop=True),
                        deps=[o_sa1, o_evK, o_Uev])
                    o_sa2 = add("vector", lambda v: v.tensor_tensor(
                        S1[:, DH:2 * DH], S1[:, DH:2 * DH], pD[:, :], ALU.add),
                        deps=[o_su2, o_Su])
                    o_sb2 = add("vector", lambda v: v.tensor_copy(
                        Sb1[:, DH:2 * DH], S1[:, DH:2 * DH]), deps=[o_sa2])
                    o_pDe = o_sa2
                    o_usb_war = o_su2
                    o_setK_war[par] = [o_su2, o_TK2]
                    o_Su = add("vector", lambda v: v.engine_nop(), deps=[o_sb1, o_sb2])
                else:
                    o_usb_war = o_om
                    o_setK_war[par] = [o_TK2]
            o_dn_last.append(o_oe)
            o_prev_head_end = o_oe
            if DEBUG_SPILLS and b == 0 and h == 0:
                dbg_ops.append(add("gpsimd", lambda g: g.dma_start(out=dbgo[:, :], in_=oT[:, :]),
                                   deps=o_oTev, dma=True))
            o_sq = add("scalar", lambda sc: sc.activation(rmsq[:, :], oT[:, :], AF.Square),
                       deps=o_oTev + (dbg_ops if DEBUG_SPILLS else []))
            o_rd = add("vector", lambda v: v.tensor_reduce(
                rmsr[:, :], rmsq.ap().rearrange("p (i d) -> p i d", d=DH), AX.X, ALU.add),
                deps=[o_sq])
            o_r1 = add("scalar", lambda sc: sc.activation(
                rmsr[:, :], rmsr[:, :], AF.Ln, bias=cst[:, 2:3], scale=1.0 / DH), deps=[o_rd])
            o_r2 = add("scalar", lambda sc: sc.activation(
                rmsr[:, :], rmsr[:, :], AF.Exp, scale=-0.5), deps=[o_r1])
            o_on = []
            for ci in range(NCH):
                o_on.append(add("vector", lambda v, ci=ci: v.scalar_tensor_tensor(
                    rmsq[:, ci * DH:(ci + 1) * DH], oT[:, ci * DH:(ci + 1) * DH],
                    rmsr[:, ci:ci + 1], onwB[:, :], ALU.mult, ALU.mult), deps=[o_r2, o_sq]))
            for ci in range(NCH):
                for dvt in range(2):
                    pso = ps[dvt][:, 0:64].bitcast(BF16)
                    o_pto = add("tensor", lambda t, ci=ci, dvt=dvt, pso=pso: t.transpose(
                        pso,
                        rmsq[:, ci * DH + dvt * 128:ci * DH + (dvt + 1) * 128], ident[:, :]),
                        deps=[o_on[ci], o_om, o_odt_war[dvt]] +
                             ([o_ps01_init] if o_ps01_init else []))
                    o_ps01_init = None
                    o_odt = add("vector", lambda v, ci=ci, dvt=dvt, h=h, pso=pso: v.tensor_copy(
                        qD[:, (2 * h + dvt) * T + ci * 128:(2 * h + dvt) * T + (ci + 1) * 128],
                        pso),
                        deps=[o_pto] + [dn_src[(0, 2 * h + dvt, qc)] for qc in range(NQ)])
                    o_odt_war[dvt] = o_odt
                    last_qD.append(o_odt)
        o_oD_done = list(last_qD)
        last_oT = []

        o_wol = {}
        o_wol[0] = add("gpsimd", lambda g: g.dma_start(
            out=oT[:, 0:4096].rearrange("p (k m) -> p k m", k=KT),
            in_=wo[:, 0:512].rearrange("(k p) m -> p k m", p=128)),
            deps=[o_sq, o_on[NCH - 1]] + o_oD_done[-2:], dma=True)
        o_wol[1] = add("gpsimd", lambda g: g.dma_start(
            out=rmsq[:, 0:4096].rearrange("p (k m) -> p k m", k=KT),
            in_=wo[:, 512:1024].rearrange("(k p) m -> p k m", p=128)),
            deps=o_oD_done, dma=True)
        o_evw = [None, None]
        o_wo_mm_last = None
        for tt in range(TT):
            for chh in range(2):
                gi = tt * 2 + chh
                pb = ps[gi % 2]
                rbuf = oT if chh == 0 else rmsq
                o_mm = add("tensor", lambda t, tt=tt, pb=pb, rbuf=rbuf: [
                    t.matmul(pb[:, :], qD[:, k * T + tt * 128:k * T + (tt + 1) * 128],
                             rbuf[:, k * 512:(k + 1) * 512],
                             start=(k == 0), stop=(k == KT - 1))
                    for k in range(KT)][-1],
                    deps=o_oD_done + [o_wol[chh], o_evw[gi % 2]])
                o_wo_mm_last = o_mm
                evb = ev0 if tt % 2 == 0 else ev1
                o_evw[gi % 2] = add("vector", lambda v, chh=chh, pb=pb, evb=evb: v.tensor_copy(
                    evb[:, chh * 512:(chh + 1) * 512], pb[:, :]),
                    deps=[o_mm, o_spA.get((b, tt - 2))] +
                         (o_zlist if b == 0 and tt < 2 else []))
            o_spA[(b, tt)] = add("gpsimd", lambda g, tt=tt, b=b,
                                 evb=(ev0 if tt % 2 == 0 else ev1): g.dma_start(
                out=sp_attn[b, tt * 128:(tt + 1) * 128, :], in_=evb[:, :]),
                deps=[o_evw[0], o_evw[1]], dma=True)
        o_attn_done = [o_spA[(b, tt)] for tt in range(TT)]

        o_fw = {}
        o_fq_done = {}
        o_evf = [None, None]
        o_first_evf = {}
        for s_i2, (dstD2, bcc) in enumerate([(qD, bqc), (kD, bkc)]):
            scw = sc_w[s_i2]
            for mi in range(KT):
                o_fw[(s_i2, mi)] = add("gpsimd", lambda g, scw=scw, mi=mi: g.dma_start(
                    out=wbufA[mi % 2][:, 0:1024].rearrange("p (k m) -> p k m", k=KT),
                    in_=scw[:, mi * 128:(mi + 1) * 128].rearrange("(k p) m -> p k m", p=128)),
                    deps=[last_wbufA[mi % 2]] +
                         ([o_wol[0], o_wol[1]] if s_i2 == 0 and mi < 2 else []), dma=True)
                last_wbufA[mi % 2] = o_fw[(s_i2, mi)]
                for qc in range(NQ):
                    gi = mi * NQ + qc
                    pb = ps[gi % 2]
                    o_mm = add("tensor", lambda t, mi=mi, qc=qc, pb=pb: [
                        t.matmul(pb[:, :], wbufA[mi % 2][:, k * 128:(k + 1) * 128],
                                 xnD[:, k * T + qc * 512:k * T + (qc + 1) * 512],
                                 start=(k == 0), stop=(k == KT - 1))
                        for k in range(KT)][-1],
                        deps=[o_fw[(s_i2, mi)], o_evf[gi % 2],
                              o_evw[0] if s_i2 == 0 and mi == 0 and qc == 0 else None,
                              o_evw[1] if s_i2 == 0 and mi == 0 and qc == 0 else None])
                    last_wbufA[mi % 2] = o_mm
                    extra = []
                    if s_i2 == 0 and mi == 0 and qc == 0:
                        extra = [o_wo_mm_last]
                    if s_i2 == 1 and mi == 0 and qc == 0:
                        extra = o_dn_last + [o_wo_mm_last]
                    o_evf[gi % 2] = add("vector", lambda v, mi=mi, qc=qc, pb=pb,
                                        dstD2=dstD2, bcc=bcc: v.tensor_scalar(
                        dstD2[:, mi * T + qc * 512:mi * T + (qc + 1) * 512], pb[:, :],
                        bcc[:, mi:mi + 1], None, ALU.add),
                        deps=[o_mm] + extra +
                             ([o_parm_last] if b == 0 and s_i2 == 0 and mi == 0 and qc == 0 else []))
                    o_fq_done[(s_i2, mi, qc)] = o_evf[gi % 2]
        all_fqk = [o_fq_done[(si2, mi, qc)] for si2 in range(2) for mi in range(KT)
                   for qc in range(NQ)]
        vwb = [oT, rmsq]
        o_vw = {}
        for hc in range(2):
            o_vw[hc] = add("gpsimd", lambda g, hc=hc: g.dma_start(
                out=vwb[hc][:, 0:4096].rearrange("p (k f) -> p k f", k=KT),
                in_=sc_w[2][:, hc * 512:(hc + 1) * 512].rearrange("(k p) f -> p k f", p=128)),
                deps=[o_wo_mm_last], dma=True)
        o_evv = [None, None]
        o_vt_last = None
        first_vt_deps = [o_trV2_list[-1]] + (list(o_scale) if b > 0 else list(last_vD))
        for tt in range(TT):
            for hc in range(2):
                gi = tt * 2 + hc
                pb = ps[2 + gi % 2]
                o_mm = add("tensor", lambda t, tt=tt, hc=hc, pb=pb: [
                    t.matmul(pb[:, :], xnD[:, k * T + tt * 128:k * T + (tt + 1) * 128],
                             vwb[hc][:, k * 512:(k + 1) * 512],
                             start=(k == 0), stop=(k == KT - 1))
                    for k in range(KT)][-1],
                    deps=[o_vw[hc], o_tr_xn, o_evv[gi % 2], o_ps2_last if tt == 0 else None])
                o_evv[gi % 2] = add("vector", lambda v, tt=tt, hc=hc, pb=pb: v.tensor_tensor(
                    vD[:, tt * 1024 + hc * 512:tt * 1024 + (hc + 1) * 512], pb[:, :],
                    bvB[:, hc * 512:(hc + 1) * 512], ALU.add),
                    deps=[o_mm] + (first_vt_deps if tt == 0 and hc == 0 else []))
                o_vt_last = o_evv[gi % 2]
        o_vt_all = [o_evv[0], o_evv[1]]

        o_hd_parts = []
        o_den_dma = []
        o_poE = None
        o_poO = None
        o_ts_ev_prev = None
        o_slab_war = {}
        o_sh_prev = [None, None]
        o_denr_war = [None, None]
        for hp in range(8):
            hE, hO = 2 * hp, 2 * hp + 1
            o_vg = []
            for par, hh in enumerate((hE, hO)):
                o_v1 = add("vector", lambda v, par=par, hh=hh: v.tensor_copy(
                    vaug[par].ap().rearrange("p (i c) -> p i c", c=65)[:, :, 0:64],
                    vD.ap().rearrange("p (i c) -> p i c", c=1024)[:, :, hh * 64:(hh + 1) * 64]),
                    deps=(o_vt_all if hp == 0 else []) + [o_pv_war[par]] +
                         ([last_wbufA[par]] if hp == 0 else []))
                o_v2 = add("vector", lambda v, par=par: v.memset(
                    vaug[par].ap().rearrange("p (i c) -> p i c", c=65)[:, :, 64:65], 1.0),
                    deps=[o_v1])
                o_vg.append(o_v2)
            o_tsm = add("tensor", lambda t: [
                t.matmul(ps[2][0:65, par * 16 + kt:par * 16 + kt + 1],
                         vaug[par][:, kt * 65:(kt + 1) * 65], onescol[:, :],
                         start=True, stop=True)
                for par in range(2) for kt in range(TT)][-1],
                deps=o_vg + [o_poE, o_ts_ev_prev])
            o_tse = add("vector", lambda v: v.tensor_copy(tsum[:, :], ps[2][0:65, 0:32]),
                        deps=[o_tsm])
            o_ts_ev_prev = o_tse
            o_suf = None
            for par in range(2):
                for qc in range(NQ):
                    lo = 4 * qc + 4
                    if lo < TT:
                        o_suf = add("vector", lambda v, par=par, qc=qc, lo=lo: v.tensor_reduce(
                            sufc[:, par * NQ + qc:par * NQ + qc + 1],
                            tsum[:, par * 16 + lo:par * 16 + 16], AX.X, ALU.add), deps=[o_tse, o_suf])
                    else:
                        o_suf = add("vector", lambda v, par=par, qc=qc: v.memset(
                            sufc[:, par * NQ + qc:par * NQ + qc + 1], 0.0), deps=[o_tse, o_suf])
            for qc in range(NQ):
                nkt = 4 * qc + 4
                o_pv_lastE = None
                o_pv_lastO = None
                for kt in range(nkt):
                    sl_e = slab[kt % 2]
                    sl_o = slab[2 + kt % 2]
                    kslE = kD[0:64, hp * T + kt * 128:hp * T + (kt + 1) * 128]
                    kslO = kD[64:128, hp * T + kt * 128:hp * T + (kt + 1) * 128]
                    qslE = qD[0:64, hp * T + qc * 512:hp * T + (qc + 1) * 512]
                    qslO = qD[64:128, hp * T + qc * 512:hp * T + (qc + 1) * 512]
                    o_sE = add("tensor", lambda t, kslE=kslE, qslE=qslE: t.matmul(
                        ps[0][:, :], kslE, qslE, start=True, stop=True),
                        deps=[o_psE] + (all_fqk if hp == 0 and qc == 0 and kt == 0 else []))
                    o_sO = add("tensor", lambda t, kslO=kslO, qslO=qslO: t.matmul(
                        ps[1][:, :], kslO, qslO, start=True, stop=True),
                        deps=[o_psO, o_sE])
                    o_xE = add("scalar", lambda sc, sl_e=sl_e: sc.activation(
                        sl_e[:, :], ps[0][:, :], AF.Exp, scale=1.0 / 32.0),
                        deps=[o_sE, o_slab_war.get((0, kt % 2))])
                    o_psE = o_xE
                    o_xO = add("scalar", lambda sc, sl_o=sl_o: sc.activation(
                        sl_o[:, :], ps[1][:, :], AF.Exp, scale=1.0 / 32.0),
                        deps=[o_sO, o_slab_war.get((1, kt % 2))])
                    o_psO = o_xO
                    r = kt - 4 * qc
                    if r >= 0:
                        def f_fix(v, sl=sl_e, r=r):
                            if r > 0:
                                v.memset(sl[:, 0:r * 128], 1.0)
                            return v.copy_predicated(sl[:, r * 128:(r + 1) * 128],
                                                     m_fut[:, :], onesb[:, :])
                        o_fE = add("vector", f_fix, deps=[o_xE])

                        def f_fixO(v, sl=sl_o, r=r):
                            if r > 0:
                                v.memset(sl[:, 0:r * 128], 1.0)
                            return v.copy_predicated(sl[:, r * 128:(r + 1) * 128],
                                                     m_fut[:, :], onesb[:, :])
                        o_fO = add("vector", f_fixO, deps=[o_xO])
                    else:
                        o_fE, o_fO = o_xE, o_xO
                    o_pv_lastE = add("tensor", lambda t, kt=kt, sl_e=sl_e, nkt=nkt: t.matmul(
                        ps[2][0:65, :], vaug[0][:, kt * 65:(kt + 1) * 65], sl_e[:, :],
                        start=(kt == 0), stop=(kt == nkt - 1)),
                        deps=[o_fE, o_poE if kt == 0 else None, o_tse if kt == 0 else None])
                    o_slab_war[(0, kt % 3)] = o_pv_lastE
                    o_pv_lastO = add("tensor", lambda t, kt=kt, sl_o=sl_o, nkt=nkt: t.matmul(
                        ps[3][0:65, :], vaug[1][:, kt * 65:(kt + 1) * 65], sl_o[:, :],
                        start=(kt == 0), stop=(kt == nkt - 1)),
                        deps=[o_fO, o_poO if kt == 0 else None])
                    o_slab_war[(1, kt % 3)] = o_pv_lastO
                o_pv_war[0] = o_pv_lastE
                o_pv_war[1] = o_pv_lastO
                hd_dep = [o_vt_last] + all_fqk if hp == 0 and qc == 0 else []
                o_heE = add("vector", lambda v, hp=hp, qc=qc: v.tensor_scalar(
                    xnD[0:64, hp * T + qc * 512:hp * T + (qc + 1) * 512],
                    ps[2][0:64, :], sufc[0:64, qc:qc + 1], None, ALU.add),
                    deps=[o_pv_lastE, o_suf] + hd_dep)
                o_dnE = add("vector", lambda v, qc=qc: v.tensor_scalar(
                    denr[0][64:65, :], ps[2][64:65, :], sufc[64:65, qc:qc + 1], None, ALU.add),
                    deps=[o_pv_lastE, o_suf, o_denr_war[0]])
                o_poE = o_dnE
                o_hoO = add("vector", lambda v, qc=qc: v.tensor_scalar(
                    oddst[qc % 2][:, :], ps[3][0:64, :], sufc[0:64, NQ + qc:NQ + qc + 1],
                    None, ALU.add),
                    deps=[o_pv_lastO, o_suf, o_sh_prev[qc % 2]])
                o_dnO = add("vector", lambda v, qc=qc: v.tensor_scalar(
                    denr[1][64:65, :], ps[3][64:65, :], sufc[64:65, NQ + qc:NQ + qc + 1],
                    None, ALU.add),
                    deps=[o_pv_lastO, o_suf, o_denr_war[1]])
                o_poO = o_dnO
                o_sh = add("gpsimd", lambda g, hp=hp, qc=qc: g.dma_start(
                    out=xnD[64:128, hp * T + qc * 512:hp * T + (qc + 1) * 512],
                    in_=oddst[qc % 2][:, :]),
                    deps=[o_hoO] + hd_dep, dma=True)
                o_sh_prev[qc % 2] = o_sh
                o_hd_parts += [o_heE, o_sh]
                o_d1 = add("gpsimd", lambda g, hE=hE, qc=qc: g.dma_start(
                    out=denM[hE:hE + 1, qc * 512:(qc + 1) * 512], in_=denr[0][64:65, :]),
                    deps=[o_dnE], dma=True)
                o_d2 = add("gpsimd", lambda g, hO=hO, qc=qc: g.dma_start(
                    out=denM[hO:hO + 1, qc * 512:(qc + 1) * 512], in_=denr[1][64:65, :]),
                    deps=[o_dnO], dma=True)
                o_denr_war[0] = o_d1
                o_denr_war[1] = o_d2
                o_den_dma += [o_d1, o_d2]
        last_wbufA = [o_pv_war[0], o_pv_war[1]]
        o_scale = []
        o_pbc = None
        o_bm_prev = None
        o_rc = None
        for qc in range(NQ):
            o_dl = add("gpsimd", lambda g, qc=qc: g.dma_start(
                out=denQ, in_=denM[0:16, qc * 512:(qc + 1) * 512]),
                deps=o_den_dma + [o_rc, o_pv_war[0], o_pv_war[1]], dma=True)
            o_rc = add("vector", lambda v: v.reciprocal(denr[0][0:16, :], denQ),
                       deps=[o_dl, o_denr_war[0] if qc == 0 else None, o_bm_prev])
            o_rowE = None
            o_rowO = None
            for hp in range(8):
                o_rowE = add("gpsimd", lambda g, hp=hp: g.dma_start(
                    out=cvo[1][0:1, 0:512], in_=denr[0][2 * hp:2 * hp + 1, :]),
                    deps=[o_rc, o_bm_prev, o_rowE], dma=True)
                o_rowO = add("gpsimd", lambda g, hp=hp: g.dma_start(
                    out=rowst[0:1, 0:512], in_=denr[0][2 * hp + 1:2 * hp + 2, :]),
                    deps=[o_rc, o_bm_prev, o_rowO, o_ps2_last if hp == 0 and qc == 0 else None],
                    dma=True)
                o_bm = add("tensor", lambda t: [
                    t.matmul(ps[0][0:64, :], ones1f[0:1, 0:64], cvo[1][0:1, 0:512],
                             start=True, stop=True),
                    t.matmul(ps[0][64:128, :], ones1f[0:1, 0:64], rowst[0:1, 0:512],
                             start=True, stop=True)][-1],
                    deps=[o_rowE, o_rowO, o_pbc,
                          o_psE if hp == 0 and qc == 0 else None])
                o_bm_prev = o_bm
                o_pbc = add("vector", lambda v, hp=hp, qc=qc: v.tensor_tensor(
                    xnD[:, hp * T + qc * 512:hp * T + (qc + 1) * 512],
                    xnD[:, hp * T + qc * 512:hp * T + (qc + 1) * 512],
                    ps[0][:, :], ALU.mult),
                    deps=[o_bm] + (o_hd_parts if hp == 0 and qc == 0 else []))
                o_scale.append(o_pbc)
        o_tr_hd = []
        for tt in range(TT):
            for k in range(KT):
                o_tr_hd.append(add("sync", lambda sc, tt=tt, k=k: sc.dma_start_transpose(
                    qD[:, tt * D + k * 128:tt * D + (k + 1) * 128],
                    xnD[:, k * T + tt * 128:k * T + (tt + 1) * 128]),
                    deps=o_scale + ([o_psE, o_psO, o_wo_mm_last] if tt == 0 and k == 0 else []),
                    dma=True))
        for tt in range(TT):
            o_spHT[(b, tt)] = add("gpsimd", lambda g, tt=tt, b=b: g.dma_start(
                out=sp_hdt[b * T + tt * 128:b * T + (tt + 1) * 128, :],
                in_=qD[:, tt * D:(tt + 1) * D]),
                deps=[o_tr_hd[tt * KT + KT - 1]], dma=True)
        all_last_A7 = o_scale + [o_spHT[(b, tt)] for tt in range(TT)]

        o_pR_last = None
        for tt in range(TT):
            o_lg = add("tensor", lambda t, tt=tt: [
                t.matmul(pR[:, :], xnD[:, k * T + tt * 128:k * T + (tt + 1) * 128],
                         wswT[:, k * NE:(k + 1) * NE], start=(k == 0), stop=(k == KT - 1))
                for k in range(KT)][-1],
                deps=o_scale + [o_pR_last, o_route_tail] +
                     ([o_parm_last] if b == 0 and tt == 0 else []))
            o_1 = add("vector", lambda v: v.tensor_tensor(lgt[:, :], pR[:, :], bswB[:, :],
                                                          ALU.add), deps=[o_lg, o_route_tail])
            o_2 = add("vector", lambda v: v.tensor_reduce(rt1[:, :], lgt[:, :], AX.X, ALU.max,
                                                          negate=True), deps=[o_1])
            o_3 = add("scalar", lambda sc: sc.activation(prb[:, :], lgt[:, :], AF.Exp,
                                                         bias=rt1[:, :]), deps=[o_2])
            o_4 = add("vector", lambda v: v.tensor_reduce(rt2[:, :], prb[:, :], AX.X, ALU.add),
                      deps=[o_3])
            o_5 = add("vector", lambda v, b=b, tt=tt: v.reciprocal(
                selP[:, b * TT + tt:b * TT + tt + 1], rt2[:, :]), deps=[o_4])
            o_6 = add("vector", lambda v: v.tensor_reduce(rt3[:, :], prb[:, :], AX.X, ALU.max),
                      deps=[o_3])
            o_7 = add("vector", lambda v: v.tensor_scalar(rt0[:, :], prb[:, :], rt3[:, :],
                                                          None, ALU.is_ge), deps=[o_6])
            o_8 = add("vector", lambda v: v.scalar_tensor_tensor(
                rt0[:, :], rt0[:, :], cst[:, 3:4], iota8[:, :], ALU.mult, ALU.add), deps=[o_7])
            o_9 = add("vector", lambda v: v.tensor_reduce(rt3[:, :], rt0[:, :], AX.X, ALU.min),
                      deps=[o_8])
            rcol = routesF[:, b * TT + tt:b * TT + tt + 1]
            o_10 = add("vector", lambda v, rcol=rcol: v.tensor_scalar(
                rcol, rt3[:, :], 1000.0, None, ALU.add), deps=[o_9])
            o_rts[b * TT + tt] = o_10
            ohsl = oneh8[:, :]
            o_11 = add("vector", lambda v, ohsl=ohsl, rcol=rcol: v.tensor_scalar(
                ohsl, iota8[:, :], rcol[:, 0:1], None, ALU.is_equal),
                deps=[o_10, o_cm_prev, o_tt2_prev])
            o_cm = add("tensor", lambda t, ohsl=ohsl: t.matmul(
                pR[:, :], m_uti[:, :], ohsl, start=True, stop=False), deps=[o_11, o_1])
            o_cm_prev = o_cm
            o_cb = add("tensor", lambda t: t.matmul(
                pR[:, :], ones1f[:, :], carry[:, :], start=False, stop=True),
                deps=[o_cm, o_route_tail])
            o_sel = add("vector", lambda v, ohsl=ohsl: v.tensor_tensor(
                rt0[:, :], pR[:, :], ohsl, ALU.mult), deps=[o_cb])
            o_pR_last = o_sel
            o_lp = add("vector", lambda v, b=b, tt=tt: v.tensor_reduce(
                lpos[:, b * TT + tt:b * TT + tt + 1], rt0[:, :], AX.X, ALU.add), deps=[o_sel])
            o_tt2 = add("tensor", lambda t, ohsl=ohsl: t.matmul(
                pRc[:, :], onescol[:, :], ohsl, start=True, stop=True),
                deps=[o_11, o_route_tail, o_sel])
            o_tt2_prev = o_tt2
            o_route_tail = add("vector", lambda v: v.tensor_tensor(
                carry[:, :], carry[:, :], pRc[:, :], ALU.add), deps=[o_tt2, o_lp])
        last_xnD = list(all_last_A7) + [o_pR_last, o_route_tail]
        last_qD = [o_spHT[(b, tt)] for tt in range(TT)]
        last_kD = [o_psE, o_psO]
        last_vD = [o_pv_war[0], o_pv_war[1]]
        last_oT = [o_vt_all[0], o_vt_all[1], o_vt_last, o_wo_mm_last]

    o_ci = add("gpsimd", lambda g: g.dma_start(out=cc_in[:, :], in_=carry[:, :]),
               deps=[o_route_tail], dma=True)

    o_b2a = add("gpsimd", lambda g: g.dma_start(out=rt1[0:8, 0:1], in_=carry[0:1, 0:NE]),
                deps=[o_route_tail], dma=True)
    o_b2b = add("vector", lambda v: v.tensor_scalar(rt2[0:8, 0:1], rt1[0:8, 0:1],
                                                    float(GR - 1), None, ALU.add), deps=[o_b2a])
    o_b2c1 = add("vector", lambda v: v.tensor_copy(slotu[0:8, 0:1], rt2[0:8, 0:1]),
                 deps=[o_b2b])
    o_b2c2 = add("vector", lambda v: v.tensor_scalar(slotu[0:8, 0:1], slotu[0:8, 0:1],
                                                     9, None, ALU.logical_shift_right),
                 deps=[o_b2c1])
    o_b2c3 = add("vector", lambda v: v.tensor_scalar(slotu[0:8, 0:1], slotu[0:8, 0:1],
                                                     9, None, ALU.logical_shift_left),
                 deps=[o_b2c2])
    o_b2d = add("vector", lambda v: v.tensor_copy(rt1[0:8, 0:1], slotu[0:8, 0:1]),
                deps=[o_b2c3])
    o_b2e = add("vector", lambda v: v.tensor_copy(b2cb[0:8, 0:1], rt1[0:8, 0:1]), deps=[o_b2d])
    o_b2f = add("tensor", lambda t: t.matmul(pA[0:8, 0:1], m_utn[0:8, 0:8], b2cb[0:8, 0:1],
                                             start=True, stop=True),
                deps=[o_b2e, o_route_tail, o_pR_last])
    o_b2g = add("vector", lambda v: v.tensor_scalar(rt2[0:8, 0:1], pA[0:8, 0:1], -1.0,
                                                    None, ALU.mult), deps=[o_b2f])
    o_b2h = add("vector", lambda v: v.tensor_copy(slotu[0:8, 0:1], rt2[0:8, 0:1]), deps=[o_b2g])
    o_aod1 = add("gpsimd", lambda g: g.dma_start(out=aoD[:, 0:1], in_=slotu[0:8, 0:1],
                                                 allow_slow_non_contiguous=True),
                 deps=[o_b2h], dma=True)
    o_aod2 = add("gpsimd", lambda g: g.dma_start(out=aoD[:, 1:2], in_=slotu[0:8, 0:1],
                                                 allow_slow_non_contiguous=True),
                 deps=[o_b2h], dma=True)
    o_b2i = add("vector", lambda v: v.tensor_tensor(rt3[0:8, 0:1], rt2[0:8, 0:1],
                                                    rt1[0:8, 0:1], ALU.add), deps=[o_b2g, o_b2d])
    o_b2i2 = add("vector", lambda v: v.tensor_scalar(rt3[0:8, 0:1], rt3[0:8, 0:1],
                                                     1.0 / GR, None, ALU.mult), deps=[o_b2i])
    o_b2j = add("gpsimd", lambda g: g.dma_start(out=b2row[0:1, 0:NE], in_=rt3[0:8, 0:1]),
                deps=[o_b2i2], dma=True)
    o_b2k = add("tensor", lambda t: t.matmul(pA[0:NG, 0:NE], ones1f[0:1, 0:NG],
                                             b2row[0:1, 0:NE], start=True, stop=True),
                deps=[o_b2j, o_b2g])
    o_b2l = add("vector", lambda v: v.tensor_scalar(egsb[0:NG, 0:NE], pA[0:NG, 0:NE],
                                                    widxP[0:NG, 0:1], None, ALU.is_le),
                deps=[o_b2k, o_ikp])
    o_b2m = add("vector", lambda v: v.tensor_reduce(egcol[0:NG, 0:1], egsb[0:NG, 0:NE],
                                                    AX.X, ALU.add), deps=[o_b2l])
    o_b2n = add("vector", lambda v: v.tensor_scalar(egcol[0:NG, 0:1], egcol[0:NG, 0:1],
                                                    7.0, None, ALU.min), deps=[o_b2m])
    o_b2o = add("gpsimd", lambda g: g.dma_start(out=b2row[0:1, 8:8 + NG],
                                                in_=egcol[0:NG, 0:1]),
                deps=[o_b2n], dma=True)

    o_aor = add("gpsimd", lambda g: g.dma_start(out=b2row[0:1, 8 + NG:8 + NG + NE],
                                                in_=rt2[0:8, 0:1]),
                deps=[o_b2g, o_b2o], dma=True)
    o_aom = add("tensor", lambda t: t.matmul(pA[:, 0:NE], ones1f[:, :],
                                             b2row[0:1, 8 + NG:8 + NG + NE],
                                             start=True, stop=True),
                deps=[o_aor, o_b2k, o_b2l])
    o_aob = add("vector", lambda v: v.tensor_copy(prb[:, :], pA[:, 0:NE]),
                deps=[o_aom])

    o_p2_prev = None
    o_p3_prev = None
    o_scT_prev = []
    o_slotT = {}
    for c in range(TT * B_LOC):
        o_p1 = add("vector", lambda v, c=c: v.tensor_scalar(
            rt0[:, :], iota8[:, :], routesF[:, c:c + 1], None, ALU.is_equal),
            deps=[o_rts[c], o_aob, o_p3_prev])
        o_p2 = add("vector", lambda v: v.tensor_tensor(rt0[:, :], rt0[:, :], prb[:, :],
                                                       ALU.mult), deps=[o_p1])
        o_p3 = add("vector", lambda v: v.tensor_reduce(rt1[:, 0:1], rt0[:, :], AX.X,
                                                       ALU.add), deps=[o_p2])
        o_p3_prev = o_p3
        o_p2_prev = o_p2
        o_p4 = add("vector", lambda v, c=c: v.tensor_scalar(
            rt2[:, :], lpos[:, c:c + 1], rt1[:, 0:1], -1.0, ALU.add, ALU.add),
            deps=[o_p3])
        o_p5 = add("vector", lambda v: v.tensor_copy(slotu[:, :], rt2[:, :]),
                   deps=[o_p4] + o_scT_prev + ([o_aod1, o_aod2] if c == 0 else []))
        o_sc2 = add("gpsimd", lambda g, c=c: g.indirect_dma_start(
            out=idxT[:, :], out_offset=bass.IndirectOffsetOnAxis(ap=slotu[:, :], axis=0),
            in_=idxC[:, 2 * c:2 * c + 2], in_offset=None),
            deps=[o_p5, o_idc, o_pref if c == 0 else None], ind=True)
        o_st2 = add("sync", lambda s, c=c: s.dma_start(
            out=slotT[c:c + 1, :], in_=slotu[:, :]),
            deps=[o_p5], dma=True)
        o_scT_prev = [o_sc2, o_st2]
        o_idx_all.append(o_sc2)
        o_slotT[c] = o_st2
        o_p3_last = o_p3
        o_p2_last = o_p2

    def f_cc(g):
        g.collective_compute("AllGather", ALU.bypass, replica_groups=[list(range(NCORES))],
                             ins=[cc_in_t.ap().opt()], outs=[cc_out_t.ap().opt()]).then_inc(cc_sem)
        g.wait_ge(cc_sem, 1)
        return g.engine_nop()

    o_cc = add("gpsimd", f_cc, deps=[o_ci])
    o_cl = add("gpsimd", lambda g: g.dma_start(out=cnts[:, :], in_=cc_out[:, :]),
               deps=[o_cc], dma=True)
    o_pm = add("tensor", lambda t: t.matmul(pRc[:, :], pmrow[:, :], cnts[:, :],
                                            start=True, stop=True),
               deps=[o_cl, o_route_tail, o_parm_last, o_b2g])
    o_pe = add("vector", lambda v: v.tensor_copy(rowst[0:1, 0:NE], pRc[:, :]),
               deps=[o_pm, o_ps2_last])
    o_pb = add("tensor", lambda t: t.matmul(pR[:, :], ones1f[:, :], rowst[0:1, 0:NE],
                                            start=True, stop=True),
               deps=[o_pe, o_pR_last, o_b2l])
    o_base = add("vector", lambda v: v.tensor_copy(base8[:, :], pR[:, :]), deps=[o_pb])

    o_sprev = o_base
    o_selp = {}
    for b in range(B_LOC):
        for tt in range(TT):
            o_a0 = add("vector", lambda v, b=b, tt=tt: v.tensor_scalar(
                rt0[:, :], iota8[:, :], routesF[:, b * TT + tt:b * TT + tt + 1],
                None, ALU.is_equal), deps=[o_base, o_sprev])
            o_a = add("vector", lambda v: v.tensor_tensor(
                rt0[:, :], rt0[:, :], base8[:, :], ALU.mult), deps=[o_a0])
            o_b2 = add("vector", lambda v: v.tensor_reduce(rt1[:, :], rt0[:, :], AX.X, ALU.add),
                       deps=[o_a])
            o_c2 = add("vector", lambda v, b=b, tt=tt: v.tensor_tensor(
                rt2[:, :], lpos[:, b * TT + tt:b * TT + tt + 1], rt1[:, :], ALU.add),
                deps=[o_b2])
            o_d2 = add("vector", lambda v: v.tensor_scalar(
                rt2[:, :], rt2[:, :], cst[:, 5:6], None, ALU.is_le), deps=[o_c2])
            o_e2 = add("vector", lambda v, b=b, tt=tt: v.tensor_scalar(
                selYb[:, b * TT + tt:b * TT + tt + 1], rt2[:, :],
                selP[:, b * TT + tt:b * TT + tt + 1], None, ALU.mult), deps=[o_d2])
            o_g2 = add("vector", lambda v, b=b, tt=tt: v.scalar_tensor_tensor(
                selZb[:, b * TT + tt:b * TT + tt + 1],
                selYb[:, b * TT + tt:b * TT + tt + 1], negb[:, 0:1],
                selP[:, b * TT + tt:b * TT + tt + 1], ALU.mult, ALU.add), deps=[o_e2])
            o_selp[(b, tt)] = o_g2
            o_sprev = o_g2

    sp_hdt_flat = sp_hdt
    sc_we_flat = sc_we.rearrange("e p k f -> (e p) (k f)")
    gbuf = [kD[:, 0:1024], kD[:, 1024:2048], kD[:, 4096:5120], kD[:, 5120:6144]]
    gXd = [kD[:, 2048:3072], kD[:, 3072:4096], kD[:, 6144:7168], kD[:, 7168:8192]]
    idxS4 = [idxS[0][:, :], idxS[1][:, :], widxU[:, 2:4], widxU[:, 4:6]]
    ybuf = [ev0, ev1]
    wEb = [vD[:, 0:9 * 1024], qD[:, 0:9 * 1024]]
    all_spHT = [o_spHT[(bb, tt)] for bb in range(B_LOC) for tt in range(TT)]
    o_wpv = None
    for g2 in range(NG):
        o_eg1p = add("tensor", lambda t, g2=g2: t.matmul(
            pA[:, 0:1], ones1f[0:1, 0:128], b2row[0:1, 8 + g2:9 + g2],
            start=True, stop=True),
            deps=[o_b2o, o_wpv] + ([o_pe, o_b2l, o_aob] if g2 == 0 else []))
        o_wpv = add("vector", lambda v, g2=g2: v.tensor_scalar(
            widx24F[:, g2:g2 + 1], pA[:, 0:1], 128.0, None, ALU.mult),
            deps=[o_eg1p] + ([o_b2m, o_p3_last, o_p2_last] if g2 == 0 else []))
    o_wadd = add("vector", lambda v: v.tensor_scalar(
        widx24F[:, :], widx24F[:, :], widxP[:, 0:1], None, ALU.add),
        deps=[o_wpv, o_ikp])
    o_wcast = add("vector", lambda v: v.tensor_copy(widx24U[:, :], widx24F[:, :]),
                  deps=[o_wadd])
    o_g_prev = [None, None, None, None]
    o_tr_prev = [None, None, None, None]
    o_mm_prev4 = [None, None, None, None]
    o_y_prev = [None, None]
    o_yst_prev = [None, None]
    o_pTe = None
    pTrot = [pT, bank6[:, 0:64].bitcast(BF16),
             ps[2][:, 0:64].bitcast(BF16), ps[3][:, 0:64].bitcast(BF16)]
    o_pTe_rot = [None, None, None, None]
    o_mm_prev = [None, None]
    o_wread = [None, None]
    o_wg_last = [None, None]
    o_eg_prev = []
    o_il_prev = [None, None]
    o_yst_all = []
    for g in range(NG):
        wp = g % 2
        o_wg = add("gpsimd", lambda gg, wp=wp, g=g: gg.indirect_dma_start(
            out=wEb[wp][:, 0:9 * 1024], out_offset=None,
            in_=sc_we_flat,
            in_offset=bass.IndirectOffsetOnAxis(ap=widx24U[:, g:g + 1], axis=0)),
            deps=[o_wcast, o_wread[wp]] +
                 ((last_vD if wp == 0 else last_qD) if g < 2 else []), ind=True)
        o_wg_last[wp] = o_wg
        o_bg = o_wg
        for t4 in range(4):
            s4 = g * 4 + t4
            par = s4 % 4
            par2 = t4 % 2
            r0 = g * GR + t4 * 128
            o_il = add("gpsimd", lambda gg, r0=r0, par=par: gg.dma_start(
                out=idxS4[par],
                in_=idxT[r0:r0 + 128, :]),
                deps=[o_il_prev[0], o_g_prev[par]] +
                     (o_idx_all if g == 0 and t4 == 0 else []), dma=True)
            o_il_prev[0] = o_il
            o_g = add("gpsimd", lambda gg, par=par: gg.indirect_dma_start(
                out=gbuf[par], out_offset=None,
                in_=sp_hdt_flat,
                in_offset=bass.IndirectOffsetOnAxis(ap=idxS4[par][:, 0:1], axis=0)),
                deps=[o_il, o_tr_prev[par]] +
                     (all_spHT + o_zlist if g == 0 else []), ind=True)
            o_g_prev[par] = o_g
            o_trs = None
            for k in range(KT):
                pTk = pTrot[k % 4]
                o_trk = add("tensor", lambda t, par=par, k=k, pTk=pTk: t.transpose(
                    pTk, gbuf[par][:, k * 128:(k + 1) * 128], ident[:, :]),
                    deps=[o_g, o_pTe_rot[k % 4]] +
                         ([o_pCe, o_pDe, o_b6_war] if g == 0 and t4 == 0 else []))
                o_ev = add("vector", lambda v, par=par, k=k, pTk=pTk: v.tensor_copy(
                    gXd[par][:, k * 128:(k + 1) * 128], pTk),
                    deps=[o_trk, o_mm_prev4[par] if k == 0 else None])
                o_pTe_rot[k % 4] = o_ev
                o_pTe = o_ev
                o_trs = o_ev
            o_tr_prev[par] = o_trs
            for fc in range(2):
                pb = ps[fc]
                o_mm = add("tensor", lambda t, par=par, fc=fc, pb=pb, wp=wp: [
                    t.matmul(pb[:, :], gXd[par][:, k * 128:(k + 1) * 128],
                             wEb[wp][:, k * D + fc * 512:k * D + fc * 512 + 512],
                             start=(k == 0), stop=(k == KT - 1))
                    for k in range(KT)][-1],
                    deps=[o_trs, o_wg_last[wp], o_y_prev[fc]] +
                         ([o_bm_prev, o_pR_last] if g == 0 and t4 == 0 else []))
                o_mm_prev4[par] = o_mm
                o_wread[wp] = o_mm
                o_ye = add("vector", lambda v, par2=par2, fc=fc, pb=pb, wp=wp: v.tensor_tensor(
                    ybuf[par2][:, fc * 512:(fc + 1) * 512], pb[:, :],
                    wEb[wp][:, 8 * 1024 + fc * 512:8 * 1024 + (fc + 1) * 512], ALU.add),
                    deps=[o_mm, o_bg, o_yst_prev[par2]] +
                         ([o_spA.get((B_LOC - 1, TT - 2)), o_spA.get((B_LOC - 1, TT - 1))]
                          if g == 0 and t4 < 2 else []))
                o_y_prev[fc] = o_ye
            o_yst = add("sync", lambda s, r0=r0, par2=par2: s.dma_start(
                out=yP[r0:r0 + 128, :], in_=ybuf[par2][:, :]),
                deps=[o_y_prev[0], o_y_prev[1]], dma=True)
            o_yst_prev[par2] = o_yst
            o_yst_all.append(o_yst)

    ybufs = [ev0[:, :], ev1[:, :],
             rmsq[:, 0:2048].bitcast(F32), rmsq[:, 2048:4096].bitcast(F32)]
    gbufs = [kD[:, 0:1024], kD[:, 1024:2048], oT[:, 0:1024], oT[:, 1024:2048]]
    xtbs = [xtb[0][:, :], xtb[1][:, :], slabT[:, :].bitcast(F32)]
    slotSs = [idxS[0][:, 0:1], idxS[1][:, 0:1], widxU[:, 0:1], widxU[:, 1:2]]
    NT = B_LOC * TT
    o_st_d = {}
    o_y2_d = {}
    o_yl_d = {}
    o_xl_d = {}
    o_hdl_d = {}
    o_sll_d = {}
    o_out_final = []
    o_prev_store = None
    o_prev_ln = None
    o_hdl_prev = None
    o_yl_prev = None
    o_xl_prev = None
    o_sll_prev = None
    o_al_prev = None

    def em_sll_yl(k):
        global_unused = None
        b, tt = divmod(k, TT)
        p4 = k % 4
        sS = slotSs[p4]
        yb = ybufs[p4]
        o_sll = add("sync", lambda s, k=k, sS=sS: s.dma_start(
            out=sS, in_=slotT[k:k + 1, :]),
            deps=[o_slotT[k], o_yl_d.get(k - 4), o_sll_d.get(k - 1)] +
                 ([o_g_prev[p4]] if k < 2 else []) +
                 ([o_wg_last[0], o_wg_last[1]] if k in (2, 3) else []),
            dma=True, key="sync_s")
        o_sll_d[k] = o_sll
        o_yl = add("gpsimd", lambda g, yb=yb, sS=sS: g.indirect_dma_start(
            out=yb, out_offset=None, in_=yP,
            in_offset=bass.IndirectOffsetOnAxis(ap=sS, axis=0)),
            deps=[o_sll, o_st_d.get(k - 4), o_yl_d.get(k - 1)] + o_yst_all[-1:] +
                 ([o_y_prev[0], o_y_prev[1]] if k < 2 else []) +
                 (o_vt_all if k in (2, 3) else []), ind=True)
        o_yl_d[k] = o_yl

    def em_hdl(k):
        b, tt = divmod(k, TT)
        p4 = k % 4
        gb = gbufs[p4]
        o_hdl = add("gpsimd", lambda g, b=b, tt=tt, gb=gb: g.dma_start(
            out=gb, in_=sp_hdt[b * T + tt * 128:b * T + (tt + 1) * 128, :]),
            deps=[o_hdl_d.get(k - 1), o_y2_d.get(k - 4)] +
                 ([o_tr_prev[p4], o_g_prev[p4]] if k < 2 else []) +
                 (o_vt_all if k in (2, 3) else []),
            dma=True, key="g_hd")
        o_hdl_d[k] = o_hdl

    def em_xl(k):
        b, tt = divmod(k, TT)
        p3 = k % 3
        xb = xtbs[p3]
        o_xl = add("sync", lambda s, b=b, tt=tt, xb=xb: s.dma_start(
            out=xb, in_=x[b, tt * 128:(tt + 1) * 128, :]),
            deps=[o_st_d.get(k - 3), o_xl_d.get(k - 1)] +
                 ([o_y_prev[0], o_y_prev[1]] if k < 3 else []),
            dma=True, key="sync_x")
        o_xl_d[k] = o_xl

    for k in range(3):
        em_hdl(k)
        em_sll_yl(k)
    for k in range(2):
        em_xl(k)
    for c in range(NT):
        if c + 3 < NT:
            em_hdl(c + 3)
            em_sll_yl(c + 3)
        if c + 2 < NT:
            em_xl(c + 2)
        b, tt = divmod(c, TT)
        p4 = c % 4
        p3 = c % 3
        yb = ybufs[p4]
        gb = gbufs[p4]
        xb = xtbs[p3]
        o_xl = o_xl_d[c]
        o_hdl = o_hdl_d[c]
        o_yl = o_yl_d[c]
        mcol = mrs[:, c * 2:c * 2 + 1]
        rcol = mrs[:, c * 2 + 1:c * 2 + 2]
        sycol = selYb[:, c:c + 1]
        szcol = selZb[:, c:c + 1]
        o_xn = add("vector", lambda v, xb=xb, mcol=mcol, rcol=rcol: v.tensor_scalar(
            cvo[0][:, :], xb[:, 0:512], mcol, rcol,
            ALU.subtract, ALU.mult), deps=[o_xl, o_prev_ln])
        o_xn2 = add("vector", lambda v, xb=xb, mcol=mcol, rcol=rcol: v.tensor_scalar(
            cvo[1][:, :], xb[:, 512:1024], mcol, rcol,
            ALU.subtract, ALU.mult), deps=[o_xl, o_xn])
        o_y1 = add("vector", lambda v, yb=yb, sycol=sycol: v.tensor_scalar(
            yb, yb, sycol, None, ALU.mult),
            deps=[o_yl, o_selp[(b, tt)]])
        o_y2 = add("vector", lambda v, yb=yb, gb=gb, szcol=szcol: v.scalar_tensor_tensor(
            yb, gb, szcol, yb, ALU.mult, ALU.add), deps=[o_y1, o_hdl])
        o_y2_d[c] = o_y2
        o_x2 = add("vector", lambda v, yb=yb: v.tensor_tensor(
            yb[:, 0:512], yb[:, 0:512], cvo[0][:, :], ALU.add),
            deps=[o_y2, o_xn])
        o_x2b = add("vector", lambda v, yb=yb: v.tensor_tensor(
            yb[:, 512:1024], yb[:, 512:1024], cvo[1][:, :], ALU.add),
            deps=[o_x2, o_xn2])
        o_b1 = add("vector", lambda v, yb=yb: v.bn_stats(bnb[:, 0:6], yb[:, 0:512]),
                   deps=[o_x2b])
        o_b22 = add("vector", lambda v, yb=yb: v.bn_stats(bnb[:, 6:12], yb[:, 512:1024]),
                    deps=[o_b1])
        o_b3 = add("vector", lambda v: v.bn_aggr(bnm[:, 0:2], bnb[:, 0:12]), deps=[o_b22])
        o_b4 = add("scalar", lambda sc: sc.activation(bnb[:, 2:3], bnm[:, 1:2],
                                                      AF.Ln, bias=cst[:, 0:1]), deps=[o_b3])
        o_b5 = add("scalar", lambda sc: sc.activation(bnb[:, 3:4], bnb[:, 2:3], AF.Exp,
                                                      scale=-0.5), deps=[o_b4])
        o_n1 = add("vector", lambda v, yb=yb: v.tensor_scalar(
            yb, yb, bnm[:, 0:1], bnb[:, 3:4],
            ALU.subtract, ALU.mult), deps=[o_b5])
        o_n2 = add("gpsimd", lambda g, yb=yb: g.tensor_tensor(
            yb, yb, gamB[:, :], ALU.mult), deps=[o_n1])
        o_n3 = add("gpsimd", lambda g, yb=yb: g.tensor_tensor(
            yb, yb, betB[:, :], ALU.add), deps=[o_n2])
        o_n4 = add("gpsimd", lambda g, yb=yb, xb=xb: g.tensor_tensor(
            yb, yb, xb, ALU.add), deps=[o_n3])
        o_al = add("gpsimd", lambda g, b=b, tt=tt, yb=yb: g.dma_start(
            out=yb, in_=sp_attn[b, tt * 128:(tt + 1) * 128, :],
            accum_op=ALU.add), deps=[o_n4, o_al_prev], dma=True, key="g_al")
        o_al_prev = o_al
        o_prev_ln = o_n4
        o_st = add("sync", lambda s, b=b, tt=tt, yb=yb: s.dma_start(
            out=out[b, tt * 128:(tt + 1) * 128, :], in_=yb),
            deps=[o_al, o_prev_store], dma=True)
        o_prev_store = o_st
        o_st_d[c] = o_st
        o_out_final.append(o_st)

    add("gpsimd", lambda g: g.engine_nop(), deps=o_out_final)
    add("vector", lambda v: v.engine_nop(), deps=o_out_final)
    add("sync", lambda s: s.nop(), deps=o_out_final)
    add("tensor", lambda t: t.nop(), deps=o_out_final)
    add("scalar", lambda sc: sc.nop(), deps=o_out_final)

    S.emit()
    return nc


_NC_CACHE = [None]


def kernel(**inputs):
    def f(name):
        return np.ascontiguousarray(np.asarray(inputs[name], dtype=np.float32))

    xs = f("L0_attn_hidden_states")
    base = {
        "wq": f("L0_attn_Wq"), "wk": f("L0_attn_Wk"), "wv": f("L0_attn_Wv"),
        "wo": f("L0_attn_Wo"), "cwq": f("L0_attn_conv_wq"), "cwk": f("L0_attn_conv_wk"),
        "cwv": f("L0_attn_conv_wv"), "onw": f("L0_attn_onorm_w"),
        "fwq": f("L0_ffn_Wq").reshape(D, D), "fwk": f("L0_ffn_Wk").reshape(D, D),
        "fwv": f("L0_ffn_Wv").reshape(D, D),
        "fbq": f("L0_ffn_bq").reshape(D), "fbk": f("L0_ffn_bk").reshape(D),
        "fbv": f("L0_ffn_bv").reshape(D),
        "fwe": f("L0_ffn_We"), "fbe": f("L0_ffn_be"), "fwsw": f("L0_ffn_Wsw"),
        "fbsw": f("L0_ffn_bsw"), "fgam": f("L0_ffn_gamma"), "fbet": f("L0_ffn_beta"),
    }
    in_maps = []
    for c in range(NCORES):
        m = dict(base)
        m["x"] = xs[c * B_LOC:(c + 1) * B_LOC]
        pm = np.zeros((NE, 1), dtype=np.float32)
        pm[:c, 0] = 1.0
        m["pmask"] = pm
        in_maps.append(m)
    if _NC_CACHE[0] is None:
        _NC_CACHE[0] = build_program()
    res = run_bass_kernel_spmd(_NC_CACHE[0], in_maps, core_ids=list(range(NCORES)))
    outs = [np.asarray(r["out"]) for r in res.results]
    return np.concatenate(outs, axis=0)


# version 37 (final)
# speedup vs baseline: 1.1787x; 1.1787x over previous
import numpy as np
from contextlib import ExitStack

import concourse.bass as bass
import concourse.mybir as mybir
from concourse.bass_utils import run_bass_kernel_spmd

F32 = mybir.dt.float32
F32R = mybir.dt.float32r
BF16 = mybir.dt.bfloat16
I8 = mybir.dt.int8
AF = mybir.ActivationFunctionType
ALU = mybir.AluOpType
AX = mybir.AxisListType

NCORES = 8
B_LOC = 4
T = 2048
D = 1024
KT = D // 128
TT = T // 128
NQ = T // 512
H_DN = 4
DH = 256
NCH = 16
NE = 8
CAP = 9830
GR = 512
NG = 24
NSLOT = NG * GR
EPS_LN = 1e-5
EPS_L2 = 1e-6
EPS_RMS = 1e-5
LN16 = 2.772588722239781
DEBUG_SPILLS = False

ENGINES = ("sync", "scalar", "tensor", "vector", "gpsimd")


class Op:
    __slots__ = ("idx", "eng", "fn", "deps", "dma", "need_inc", "val", "key")

    def __init__(self, idx, eng, fn, deps, dma, ind, key):
        self.idx = idx
        self.eng = eng
        self.fn = fn
        self.deps = deps
        self.dma = dma
        self.need_inc = False
        self.val = None
        self.key = key if key else eng + ("_i" if ind else "_d" if dma else "")


class Sched:
    def __init__(self, nc, ctx):
        self.nc = nc
        self.ctx = ctx
        self.ops = []

    def add(self, eng, fn, deps=(), dma=False, ind=False, key=None):
        flat = []
        for d in deps:
            if d is None:
                continue
            if isinstance(d, (list, tuple)):
                flat.extend(q for q in d if q is not None)
            else:
                flat.append(d)
        o = Op(len(self.ops), eng, fn, flat, dma or ind, ind, key)
        self.ops.append(o)
        return o

    def emit(self):
        for o in self.ops:
            if o.dma:
                o.need_inc = True
            for d in o.deps:
                assert d.idx < o.idx, "dependency must precede op"
                d.need_inc = True
        counters = {}
        for o in self.ops:
            if o.need_inc:
                counters[o.key] = counters.get(o.key, 0) + (16 if o.dma else 1)
                o.val = counters[o.key]
        sems = {k: self.ctx.enter_context(self.nc.semaphore("sem_" + k)) for k in counters}
        per_eng = {e: [o for o in self.ops if o.eng == e] for e in ENGINES}

        def stream(eng_name, eng):
            for o in per_eng[eng_name]:
                waits = {}
                for d in o.deps:
                    waits[d.key] = max(waits.get(d.key, 0), d.val)
                for key, v in waits.items():
                    eng.wait_ge(sems[key], v)
                r = o.fn(eng)
                if o.need_inc:
                    instrs = r if isinstance(r, (list, tuple)) else [r]
                    if o.dma:
                        assert len(instrs) == 1
                        instrs[0].then_inc(sems[o.key], 16)
                    else:
                        instrs[-1].then_inc(sems[o.key], 1)

        with self.nc.Block() as blk:
            for name in ENGINES:
                getattr(blk, name)(lambda e, n=name: stream(n, e))


def build_program():
    nc = bass.Bass("TRN2", target_bir_lowering=False)
    ctx = ExitStack()
    ap = lambda t: t.ap()

    x = ap(nc.dram_tensor("x", [B_LOC, T, D], F32, kind="ExternalInput"))
    wq = ap(nc.dram_tensor("wq", [D, D], F32, kind="ExternalInput"))
    wk = ap(nc.dram_tensor("wk", [D, D], F32, kind="ExternalInput"))
    wv = ap(nc.dram_tensor("wv", [D, D], F32, kind="ExternalInput"))
    wo = ap(nc.dram_tensor("wo", [D, D], F32, kind="ExternalInput"))
    cwq = ap(nc.dram_tensor("cwq", [D, 4], F32, kind="ExternalInput"))
    cwk = ap(nc.dram_tensor("cwk", [D, 4], F32, kind="ExternalInput"))
    cwv = ap(nc.dram_tensor("cwv", [D, 4], F32, kind="ExternalInput"))
    onw = ap(nc.dram_tensor("onw", [DH], F32, kind="ExternalInput"))
    fwq = ap(nc.dram_tensor("fwq", [D, D], F32, kind="ExternalInput"))
    fwk = ap(nc.dram_tensor("fwk", [D, D], F32, kind="ExternalInput"))
    fwv = ap(nc.dram_tensor("fwv", [D, D], F32, kind="ExternalInput"))
    fbq = ap(nc.dram_tensor("fbq", [D], F32, kind="ExternalInput"))
    fbk = ap(nc.dram_tensor("fbk", [D], F32, kind="ExternalInput"))
    fbv = ap(nc.dram_tensor("fbv", [D], F32, kind="ExternalInput"))
    fwe = ap(nc.dram_tensor("fwe", [NE, D, D], F32, kind="ExternalInput"))
    fbe = ap(nc.dram_tensor("fbe", [NE, D], F32, kind="ExternalInput"))
    fwsw = ap(nc.dram_tensor("fwsw", [NE, D], F32, kind="ExternalInput"))
    fbsw = ap(nc.dram_tensor("fbsw", [NE], F32, kind="ExternalInput"))
    fgam = ap(nc.dram_tensor("fgam", [D], F32, kind="ExternalInput"))
    fbet = ap(nc.dram_tensor("fbet", [D], F32, kind="ExternalInput"))
    pmask = ap(nc.dram_tensor("pmask", [NE, 1], F32, kind="ExternalInput"))
    out = ap(nc.dram_tensor("out", [B_LOC, T, D], F32, kind="ExternalOutput"))

    dbg = dict(kind="ExternalOutput") if DEBUG_SPILLS else {}
    sc_w = [ap(nc.dram_tensor(n, [D, D], BF16)) for n in ("scfwq", "scfwk", "scfwv")]
    sc_we = ap(nc.dram_tensor("scfwe", [NE, 128, 9, 1024], BF16))
    sp_attn = ap(nc.dram_tensor("spattn", [B_LOC, T, D], F32, **dbg))
    sp_hd = ap(nc.dram_tensor("sphd", [B_LOC, D, T], BF16, **dbg))
    sp_hdt = ap(nc.dram_tensor("sphdt", [B_LOC * T + 128, D], BF16, **dbg))
    denM = ap(nc.dram_tensor("denM", [16, T], F32))
    if DEBUG_SPILLS:
        dbgq = ap(nc.dram_tensor("dbgq", [128, KT * T], BF16, kind="ExternalOutput"))
        dbgk = ap(nc.dram_tensor("dbgk", [128, KT * T], BF16, kind="ExternalOutput"))
        dbgv = ap(nc.dram_tensor("dbgv", [128, KT * T], BF16, kind="ExternalOutput"))
        dbgo = ap(nc.dram_tensor("dbgo", [128, TT * DH], BF16, kind="ExternalOutput"))
        dbgon = ap(nc.dram_tensor("dbgon", [128, TT * DH], BF16, kind="ExternalOutput"))
        dbgx = ap(nc.dram_tensor("dbgx", [128, KT * T], BF16, kind="ExternalOutput"))
        dbgs = ap(nc.dram_tensor("dbgs", [128, 512], BF16, kind="ExternalOutput"))
        dbgc = ap(nc.dram_tensor("dbgc", [128, 512], F32, kind="ExternalOutput"))
    idxT = ap(nc.dram_tensor("idxT", [NSLOT, 2], mybir.dt.uint32))
    yP = ap(nc.dram_tensor("yP", [NSLOT, D], F32))
    slotT = ap(nc.dram_tensor("slotT", [TT * B_LOC, 128], mybir.dt.uint32))
    aoD = ap(nc.dram_tensor("aoD", [NE, 2], mybir.dt.uint32))
    cc_in_t = nc.dram_tensor("ccin", [1, NE], F32)
    cc_out_t = nc.dram_tensor("ccout", [NCORES, NE], F32)
    cc_in = ap(cc_in_t)
    cc_out = ap(cc_out_t)

    sb = lambda name, shape, dt: ctx.enter_context(nc.sbuf_tensor(name, shape, dt))
    xnD = sb("xnD", [128, KT * T], BF16)
    qD = sb("qD", [128, KT * T], BF16)
    kD = sb("kD", [128, KT * T], BF16)
    vD = sb("vD", [128, KT * T], BF16)
    stg = [sb(f"stg{i}", [128, 515], BF16) for i in range(2)]
    cvo = [sb(f"cvo{i}", [128, 512], F32) for i in range(2)]
    denQ = cvo[0][0:16, 0:512]
    xtb = [sb(f"xtb{i}", [128, D], F32) for i in range(2)]

    bnb = sb("bnb", [128, 12], F32)
    bnm = sb("bnm", [128, 2], F32)
    mrs = sb("mrs", [128, 2 * TT * B_LOC], F32)
    Sb1 = sb("Sb1", [128, 2 * DH], BF16)
    oT = sb("oT", [128, TT * DH], BF16)
    rmsq = sb("rmsq", [128, TT * DH], BF16)
    rmsr = sb("rmsr", [128, TT], F32)
    ev0 = sb("ev0", [128, D], F32)
    ev1 = sb("ev1", [128, D], F32)
    QpT = [sb(f"QpT{j}", [128, 128], F32R) for j in range(6)]
    QtT = [sb(f"QtT{j}", [128, 128], F32R) for j in range(7)]
    CabT = [sb(f"CabT{i}", [128, DH], F32R) for i in range(2)]
    QpR = [t[:, :] for t in QpT]
    QtR = [t[:, :] for t in QtT]
    CabR = [t[:, :] for t in CabT]
    CabF = [t[:, :].bitcast(F32) for t in CabT]
    S1 = xtb[1][:, 0:512]
    vaug = [sb(f"vaug{i}", [128, TT * 65], BF16) for i in range(2)]
    wbufA = [vaug[0], vaug[1]]
    slabT = sb("slabT", [128, 2048], BF16)
    slab = [slabT[:, i * 512:(i + 1) * 512] for i in range(4)]
    xnT = [slabT[:, 0:1024], slabT[:, 1024:2048]]
    KTt = slabT[:, 1024:1280]
    VTt = slabT[:, 1280:1536]
    Usb = vaug[0][:, 0:256]
    Gm = vaug[0][:, 256:384]
    tsum = sb("tsum", [65, 2 * TT], F32)
    sufc = sb("sufc", [65, 2 * NQ], F32)
    denr = [sb(f"denr{i}", [65, 512], F32) for i in range(2)]
    rowst = denr[1][0:1, 0:512]
    oddst = [sb(f"odd{i}", [64, 512], BF16) for i in range(2)]
    beS = [oddst[0][0:1, 0:512], oddst[1][0:1, 0:512]]
    lgt = sb("lgt", [128, NE], F32)
    prb = sb("prb", [128, NE], F32)
    rt0 = sb("rt0", [128, NE], F32)
    rt1 = sb("rt1", [128, 1], F32)
    rt2 = sb("rt2", [128, 1], F32)
    rt3 = sb("rt3", [128, 1], F32)
    oneh8 = sb("oneh8", [128, NE], BF16)
    selP = sb("selP", [128, TT * B_LOC], F32)
    lpos = sb("lpos", [128, TT * B_LOC], F32)
    carry = sb("carry", [1, NE], F32)
    base8 = sb("base8", [128, NE], F32)
    selZb = sb("selZb", [128, TT * B_LOC], F32)
    selYb = sb("selYb", [128, TT * B_LOC], F32)
    slotu = sb("slotu", [128, 1], mybir.dt.uint32)
    idxC = sb("idxC", [128, 2 * TT * B_LOC], mybir.dt.uint32)
    idxS = [sb(f"idxS{i}", [128, 2], mybir.dt.uint32) for i in range(2)]
    oobU = sb("oobU", [128, 16], mybir.dt.uint32)
    routesF = sb("routesF", [128, TT * B_LOC], F32)
    widxP = sb("widxP", [128, 1], F32)
    egsb = sb("egsb", [128, NE], F32)
    widxF = egsb
    widxU = sb("widxU", [128, KT], mybir.dt.uint32)
    egcol = sb("egcol", [128, 1], F32)
    b2row = sb("b2row", [1, 48], F32)
    b2cb = sb("b2cb", [128, 1], BF16)
    egbU = sb("egbU", [128, 1], mybir.dt.uint32)
    widx24F = sb("widx24F", [128, 24], F32)
    widx24U = sb("widx24U", [128, 24], mybir.dt.uint32)
    slotS = idxS
    aosel = widxU[:, 0:2]
    ridxU = sb("ridxU", [128, 1], mybir.dt.uint32)
    cnts = sb("cnts", [NCORES, NE], F32)
    pmrow = sb("pmrow", [NE, 1], F32)
    ident = sb("ident", [128, 128], BF16)
    m_ltn = sb("m_ltn", [128, 128], BF16)
    m_utn = sb("m_utn", [128, 128], BF16)
    m_uti = sb("m_uti", [128, 128], BF16)
    m_fut = sb("m_fut", [128, 128], I8)
    onesb = sb("onesb", [128, 128], BF16)
    negb = sb("negb", [128, 128], F32)
    ones1 = sb("ones1", [1, 128], BF16)
    ones1f = sb("ones1f", [1, 128], F32)
    onescol = sb("onescol", [128, 1], BF16)
    iota8 = sb("iota8", [128, NE], F32)
    cst = sb("cst", [128, 10], F32)
    convw = sb("convw", [128, 3 * KT * 4], F32)
    onwB = sb("onwB", [128, DH], F32)
    gamB = sb("gamB", [128, D], F32)
    betB = sb("betB", [128, D], F32)
    bvB = sb("bvB", [128, D], BF16)
    bqc = sb("bqc", [128, KT], F32)
    bkc = sb("bkc", [128, KT], F32)
    wswT = sb("wswT", [128, KT * NE], BF16)
    bswB = sb("bswB", [128, NE], F32)

    psf = lambda name, shape: ctx.enter_context(nc.psum_tensor(name, shape, F32))
    ps = [psf(f"ps{i}", [128, 512]) for i in range(4)]
    bank4 = psf("bank4", [128, 512])
    bank5 = psf("bank5", [128, 512])
    bank6 = psf("bank6", [128, 512])
    bank7 = psf("bank7", [128, 512])
    pA = bank4[:, 0:128]
    pR = bank4[:, 128:136]
    pRc = bank4[0:1, 136:144]
    pB = bank5[:, 0:128]
    pL = bank5[0:1, 128:512]
    pL = bank5[0:1, 0:512]
    pC = bank6[:, 0:DH]
    pD = bank7[:, 0:DH]
    pT = bank7[:, 0:64].bitcast(BF16)
    cc_sem = ctx.enter_context(nc.semaphore("cc_sem"))

    S = Sched(nc, ctx)
    add = S.add

    def f_memsets(v):
        v.memset(onesb[:, :], 1.0)
        v.memset(ones1[:, :], 1.0)
        v.memset(ones1f[:, :], 1.0)
        v.memset(onescol[:, :], 1.0)
        v.memset(negb[:, :], -1.0)
        v.memset(cst[:, 0:1], EPS_LN)
        v.memset(cst[:, 1:2], EPS_L2)
        v.memset(cst[:, 2:3], EPS_RMS)
        v.memset(cst[:, 3:4], -1000.0)
        v.memset(cst[:, 4:5], 1.0)
        v.memset(cst[:, 5:6], float(CAP))
        v.memset(cst[:, 6:7], -LN16)
        v.memset(cst[:, 7:8], 0.0)
        v.memset(cst[:, 8:9], float(GR))
        v.memset(cst[:, 9:10], 65535.0)
        v.memset(carry[:, :], 0.0)
        return v.memset(cst[:, 7:8], 0.0)

    o_ms = add("vector", f_memsets)

    def f_consts(g):
        g.iota(iota8[:, :], [[1, NE]], base=0, channel_multiplier=0,
               allow_small_or_imprecise_dtypes=True)
        g.affine_select(ident[:, :], onesb[:, :], [[-1, 128]], ALU.is_equal, 0.0,
                        base=0, channel_multiplier=1)
        g.affine_select(m_uti[:, :], onesb[:, :], [[1, 128]], ALU.is_ge, 0.0,
                        base=0, channel_multiplier=-1)
        g.affine_select(m_ltn[:, :], negb[:, :], [[-1, 128]], ALU.is_gt, 0.0,
                        base=0, channel_multiplier=1)
        r = g.affine_select(m_utn[:, :], negb[:, :], [[1, 128]], ALU.is_gt, 0.0,
                            base=0, channel_multiplier=-1)
        return r

    o_cst = add("gpsimd", f_consts, deps=[o_ms])
    o_idc = add("gpsimd", lambda g: g.iota(idxC[:, :], [[128, TT * B_LOC], [0, 2]], base=0,
                                           channel_multiplier=1), deps=[o_ms])
    o_oob = add("gpsimd", lambda g: g.iota(oobU[:, :], [[0, 16]], base=B_LOC * T,
                                           channel_multiplier=1), deps=[o_ms])
    o_ikp = add("gpsimd", lambda g: g.iota(widxP[:, :], [[0, 1]], base=0,
                                           channel_multiplier=1,
                                           allow_small_or_imprecise_dtypes=True), deps=[o_ms])
    idxTv = idxT[:, :].rearrange("(a b) c -> b a c", b=128)
    o_pref = None
    for pi in range(12):
        o_pref = add("gpsimd", lambda g, pi=pi: g.dma_start(
            out=idxTv[:, pi * 8:(pi + 1) * 8, :],
            in_=oobU[:, :].rearrange("p (a c) -> p a c", c=2),
            allow_slow_non_contiguous=True), deps=[o_oob, o_pref], dma=True)
    o_z0 = add("vector", lambda v: v.memset(ev0[:, :], 0.0), deps=[])
    o_zhd = add("gpsimd", lambda g: g.dma_start(
        out=sp_hdt[B_LOC * T:B_LOC * T + 128, :], in_=ev0[:, :]), deps=[o_z0], dma=True)
    o_zlist = [o_zhd]
    o_f0 = add("gpsimd", lambda g: g.memset(m_fut[:, :], 1), deps=[o_cst])
    o_cst = add("gpsimd", lambda g: g.affine_select(
        m_fut[:, :], m_fut[:, :], [[-1, 128]], ALU.is_gt, 0, base=0, channel_multiplier=1),
        deps=[o_f0])

    for si, cw in enumerate([cwq, cwk, cwv]):
        for k in range(KT):
            add("gpsimd", lambda g, si=si, k=k, cw=cw: g.dma_start(
                out=convw[:, (si * KT + k) * 4:(si * KT + k) * 4 + 4],
                in_=cw[k * 128:(k + 1) * 128, :]), dma=True)
    for k in range(KT):
        add("gpsimd", lambda g, k=k: g.dma_start(
            out=bqc[:, k:k + 1], in_=fbq[k * 128:(k + 1) * 128].rearrange("(a b) -> a b", b=1)), dma=True)
        add("gpsimd", lambda g, k=k: g.dma_start(
            out=bkc[:, k:k + 1], in_=fbk[k * 128:(k + 1) * 128].rearrange("(a b) -> a b", b=1)), dma=True)
    for k in range(KT):
        add("gpsimd", lambda g, k=k: g.dma_start(
            out=wswT[:, k * NE:(k + 1) * NE],
            in_=fwsw.rearrange("e d -> d e")[k * 128:(k + 1) * 128, :],
            allow_slow_non_contiguous=True), dma=True)
    o_parm_last = add("gpsimd", lambda g: g.dma_start(out=pmrow[:, :], in_=pmask[:, :]), dma=True)

    o_prev = o_ms
    for srcv, dst, n in [(onw, onwB, DH), (fgam, gamB, D), (fbet, betB, D),
                         (fbv, bvB, D), (fbsw, bswB, NE)]:
        for cc in range(0, n, 512):
            w = min(512, n - cc)
            o_row = add("gpsimd", lambda g, srcv=srcv, cc=cc, w=w: g.dma_start(
                out=rowst[0:1, 0:w], in_=srcv[cc:cc + w].rearrange("(b a) -> b a", b=1)),
                deps=[o_prev], dma=True)
            o_m = add("tensor", lambda t, w=w: t.matmul(
                ps[0][:, 0:w], ones1f[:, :], rowst[0:1, 0:w], start=True, stop=True),
                deps=[o_row, o_prev])
            o_prev = add("vector", lambda v, dst=dst, cc=cc, w=w: v.tensor_copy(
                dst[:, cc:cc + w], ps[0][:, 0:w]), deps=[o_m])
    o_bc_done = o_prev

    o_bstage_prev = None
    p0rot = [pT, bank6[:, 0:64].bitcast(BF16),
             ps[2][:, 0:64].bitcast(BF16), ps[3][:, 0:64].bitcast(BF16),
             bank4[:, 0:64].bitcast(BF16), bank5[:, 0:64].bitcast(BF16),
             ps[0][:, 0:64].bitcast(BF16), ps[1][:, 0:64].bitcast(BF16)]
    o_p0rot = [None] * 8
    o_prev_ld = [None, None]
    o_prev_ev = None
    o_st_by_dk = {}
    o_prev_stores = []
    for src, dst, e in ([(fwq, sc_w[0], None), (fwk, sc_w[1], None), (fwv, sc_w[2], None)] +
                        [(fwe, sc_we, ee) for ee in range(NE)]):
        o_evs = []
        evs_by_dk = {dk: [] for dk in range(8)}
        for r in range(8):
            inap = src[r * 128:(r + 1) * 128, :] if e is None else src[e, r * 128:(r + 1) * 128, :]
            o_ld = add("gpsimd", lambda g, r=r, inap=inap: g.dma_start(
                out=wbufA[r % 2][:, 0:1024], in_=inap), deps=[o_prev_ld[r % 2]], dma=True)
            for dk in range(8):
                rk = (r * 8 + dk) % 8
                prt = p0rot[rk]
                o_tr = add("tensor", lambda t, r=r, dk=dk, prt=prt: t.transpose(
                    prt, wbufA[r % 2][:, dk * 128:(dk + 1) * 128], ident[:, :]),
                    deps=[o_ld, o_cst, o_p0rot[rk]] +
                         ([o_bc_done] if rk >= 4 and o_p0rot[rk] is None else []))
                o_prev_ev = add("vector", lambda v, r=r, dk=dk, prt=prt: v.tensor_copy(
                    kD[:, dk * 1024 + r * 128:dk * 1024 + (r + 1) * 128], prt),
                    deps=[o_tr, o_st_by_dk.get(dk)])
                o_p0rot[rk] = o_prev_ev
                o_evs.append(o_prev_ev)
                evs_by_dk[dk].append(o_prev_ev)
            o_prev_ld[r % 2] = o_evs[-1]
        o_prev_stores = []
        for dk in range(8):
            if e is None:
                outap = dst[dk * 128:(dk + 1) * 128, :]
            else:
                outap = dst[e, :, dk, :]
            o_stk = add("gpsimd", lambda g, outap=outap, dk=dk: g.dma_start(
                out=outap, in_=kD[:, dk * 1024:(dk + 1) * 1024]), deps=[evs_by_dk[dk]],
                dma=True)
            o_st_by_dk[dk] = o_stk
            o_prev_stores.append(o_stk)
        if e is not None:
            o_br1 = add("gpsimd", lambda g, e=e: g.dma_start(
                out=rowst[0:1, 0:512], in_=fbe[e, 0:512].rearrange("(b a) -> b a", b=1)),
                deps=[o_bc_done, o_bstage_prev], dma=True)
            o_bm1 = add("tensor", lambda t: t.matmul(ps[0][:, 0:512], ones1f[:, :],
                                                     rowst[0:1, 0:512], start=True, stop=True),
                        deps=[o_br1, o_p0rot[6]])
            o_be1 = add("vector", lambda v: v.tensor_copy(kD[:, 8192:8704], ps[0][:, 0:512]),
                        deps=[o_bm1, o_bstage_prev])
            o_br2 = add("gpsimd", lambda g, e=e: g.dma_start(
                out=rowst[0:1, 0:512], in_=fbe[e, 512:1024].rearrange("(b a) -> b a", b=1)),
                deps=[o_be1], dma=True)
            o_bm2 = add("tensor", lambda t: t.matmul(ps[0][:, 0:512], ones1f[:, :],
                                                     rowst[0:1, 0:512], start=True, stop=True),
                        deps=[o_br2])
            o_be2 = add("vector", lambda v: v.tensor_copy(kD[:, 8704:9216], ps[0][:, 0:512]),
                        deps=[o_bm2])
            o_p0rot[6] = o_be2
            o_bst = add("gpsimd", lambda g, e=e: g.dma_start(
                out=sc_we[e, :, 8, :], in_=kD[:, 8192:9216]), deps=[o_be2], dma=True)
            o_bstage_prev = o_bst
            o_prev_stores.append(o_bst)
    o_wt_stores = o_prev_stores

    last_xnD = list(o_wt_stores)
    last_qD = [o_prev_ld[0], o_prev_ld[1]] + list(o_wt_stores)
    last_kD = list(o_wt_stores)
    last_vD = []
    last_oT = []
    o_route_tail = o_ms
    o_idx_all = []
    o_rts = {}
    o_cm_prev = None
    o_tt2_prev = None
    o_conv_dump = None
    o_psE = None
    o_psO = None
    o_pv_war = [None, None]
    o_rc = None
    o_spA = {}
    o_spH = {}
    o_spHT = {}
    last_wbufA = [None, None]
    o_scale = []
    all_last_A7 = []

    for b in range(B_LOC):
        o_tr_xn = None
        o_first_trs = []
        o_prev_ts = [None, None]
        o_ln_tail = None
        for tt in range(TT):
            o_xl = add("sync", lambda s, tt=tt, b=b: s.dma_start(
                out=xtb[tt % 2][:, :], in_=x[b, tt * 128:(tt + 1) * 128, :]),
                deps=[o_prev_ts[tt % 2]] +
                     ([o_route_tail] if tt < 2 and b > 0 else []), dma=True)
            o_s1 = add("vector", lambda v, tt=tt: v.bn_stats(bnb[:, 0:6], xtb[tt % 2][:, 0:512]),
                       deps=[o_xl, o_ln_tail])
            o_s2 = add("vector", lambda v, tt=tt: v.bn_stats(bnb[:, 6:12], xtb[tt % 2][:, 512:1024]),
                       deps=[o_s1])
            o_ag = add("vector", lambda v: v.bn_aggr(bnb[:, 0:2], bnb[:, 0:12]), deps=[o_s2])
            mcol = mrs[:, (b * TT + tt) * 2:(b * TT + tt) * 2 + 1]
            rcol = mrs[:, (b * TT + tt) * 2 + 1:(b * TT + tt) * 2 + 2]
            o_mc = add("vector", lambda v, mcol=mcol: v.tensor_copy(mcol, bnb[:, 0:1]),
                       deps=[o_ag])
            o_l1 = add("scalar", lambda sc: sc.activation(rt1[:, :], bnb[:, 1:2], AF.Ln,
                                                          bias=cst[:, 0:1]),
                       deps=[o_ag, o_bc_done if b == 0 and tt == 0 else None])
            o_l2 = add("scalar", lambda sc, rcol=rcol: sc.activation(rcol, rt1[:, :], AF.Exp,
                                                                     scale=-0.5), deps=[o_l1])
            o_ts = add("vector", lambda v, tt=tt, mcol=mcol, rcol=rcol: v.tensor_scalar(
                xnT[tt % 2][:, :], xtb[tt % 2][:, :], mcol, rcol, ALU.subtract, ALU.mult),
                deps=[o_l2, o_mc, o_tr_xn] +
                     ((o_pv_war + [o_route_tail]) if tt < 2 else []))
            o_prev_ts[tt % 2] = o_ts
            o_ln_tail = o_ts
            for k in range(KT):
                o_tr_xn = add("scalar", lambda sc, tt=tt, k=k: sc.dma_start_transpose(
                    xnD[:, k * T + tt * 128:k * T + (tt + 1) * 128],
                    xnT[tt % 2][:, k * 128:(k + 1) * 128]),
                    deps=[o_ts] + (last_xnD if tt == 0 else []), dma=True)
                if tt == 0:
                    o_first_trs.append(o_tr_xn)
        last_xnD = []

        if DEBUG_SPILLS and b == 0:
            o_dbgx = add("gpsimd", lambda g: g.dma_start(out=dbgx[:, :], in_=xnD[:, :]),
                         deps=[o_tr_xn], dma=True)
        dn_src = {}
        o_pl_last = None
        o_ps2_last = None
        o_lastmm_all = []
        for s_i, (wsrc, dstD) in enumerate([(wq, qD), (wk, kD), (wv, vD)]):
            dlast = [last_qD, last_kD, last_vD][s_i]
            cwoff = s_i * KT * 4
            o_wld = {}
            o_prev_ev = [None, None]
            o_silu = {}
            o_zs = {}
            o_conv_prev = {}
            for h in range(H_DN):
                for mi2, mi in enumerate((2 * h, 2 * h + 1)):
                    o_wld[mi] = add("gpsimd", lambda g, mi=mi, wsrc=wsrc: g.dma_start(
                        out=wbufA[mi % 2][:, 0:1024].rearrange("p (k m) -> p k m", k=KT),
                        in_=wsrc[:, mi * 128:(mi + 1) * 128].rearrange("(k p) m -> p k m", p=128)),
                        deps=[last_wbufA[mi % 2]], dma=True)
                    last_wbufA[mi % 2] = o_wld[mi]
                for mi2 in range(2):
                    o_zs[mi2] = add("vector", lambda v, mi2=mi2: v.memset(stg[mi2][:, 0:3], 0.0),
                                    deps=[o_conv_prev.get(mi2)])
                o_l2tail = None
                for qc in range(NQ):
                    for mi2, mi in enumerate((2 * h, 2 * h + 1)):
                        gi = mi * NQ + qc
                        pb = ps[gi % 2]
                        o_mm = add("tensor", lambda t, mi=mi, qc=qc, pb=pb: [
                            t.matmul(pb[:, :], wbufA[mi % 2][:, k * 128:(k + 1) * 128],
                                     xnD[:, k * T + qc * 512:k * T + (qc + 1) * 512],
                                     start=(k == 0), stop=(k == KT - 1))
                            for k in range(KT)][-1],
                            deps=[o_wld[mi], o_tr_xn, o_prev_ev[gi % 2]])
                        o_lastmm_all.append(o_mm)
                        last_wbufA[mi % 2] = o_mm
                        o_ev = add("vector", lambda v, mi2=mi2, pb=pb: v.tensor_copy(
                            stg[mi2][:, 3:515], pb[:, :]),
                            deps=[o_mm, o_zs[mi2]])
                        if DEBUG_SPILLS and b == 0 and s_i == 2 and mi == 0 and qc == 0:
                            o_dmp = add("gpsimd", lambda g: g.dma_start(
                                out=dbgs[:, :], in_=stg[0][:, 3:515]), deps=[o_ev], dma=True)
                            o_ev = o_dmp
                        o_prev_ev[gi % 2] = o_ev
                        wtap = [convw[:, cwoff + mi * 4 + tap:cwoff + mi * 4 + tap + 1]
                                for tap in range(4)]
                        o_c = add("vector", lambda v, mi2=mi2, w3=wtap[3]: v.tensor_scalar(
                            cvo[mi2][:, :], stg[mi2][:, 3:515], w3, None, ALU.mult),
                            deps=[o_ev, o_silu.get((mi2, qc - 1))] +
                                 ([o_parm_last] if b == 0 and s_i == 0 and h == 0 and qc == 0 else []) +
                                 ([o_rc] if b > 0 and s_i == 0 and h == 0 and qc == 0 else []))
                        for tap in (2, 1, 0):
                            o_c = add("vector", lambda v, mi2=mi2, tap=tap, wt=wtap[tap]:
                                      v.scalar_tensor_tensor(
                                          cvo[mi2][:, :], stg[mi2][:, tap:tap + 512], wt,
                                          cvo[mi2][:, :], ALU.mult, ALU.add), deps=[o_c])
                        o_cc = add("vector", lambda v, mi2=mi2: v.tensor_copy(
                            stg[mi2][:, 0:3], stg[mi2][:, 512:515]), deps=[o_c])
                        if DEBUG_SPILLS and b == 0 and s_i == 2 and mi == 0 and qc == 0:
                            o_dm2 = add("gpsimd", lambda g: g.dma_start(
                                out=dbgc[:, :], in_=cvo[0][:, :]), deps=[o_c], dma=True)
                            o_conv_dump = o_dm2
                        o_zs[mi2] = o_cc
                        o_conv_prev[mi2] = o_cc
                        dbg_cd = [o_conv_dump] if (DEBUG_SPILLS and b == 0 and s_i == 2
                                                    and mi == 0 and qc == 0) else []
                        if s_i < 2:
                            o_si = add("scalar", lambda sc, mi2=mi2: sc.activation(
                                cvo[mi2][:, :], cvo[mi2][:, :], AF.Silu), deps=[o_c] + dbg_cd)
                        else:
                            o_si = add("scalar", lambda sc, mi=mi, qc=qc, mi2=mi2: sc.activation(
                                dstD[:, mi * T + qc * 512:mi * T + (qc + 1) * 512],
                                cvo[mi2][:, :], AF.Silu),
                                deps=[o_c] + dbg_cd + (dlast if mi == 0 and qc == 0 else []))
                            dn_src[(s_i, mi, qc)] = o_si
                        o_silu[(mi2, qc)] = o_si
                    if s_i < 2:
                        o_q0 = add("scalar", lambda sc: sc.activation(
                            slab[0][:, :], cvo[0][:, :], AF.Square),
                            deps=[o_silu[(0, qc)], o_l2tail] +
                                 ([o_tr_xn] if s_i == 0 and h == 0 and qc == 0 else []))
                        o_q1 = add("scalar", lambda sc: sc.activation(
                            slab[1][:, :], cvo[1][:, :], AF.Square),
                            deps=[o_silu[(1, qc)], o_l2tail])
                        o_sm = add("tensor", lambda t: [
                            t.matmul(pL[:, :], onescol[:, :], slab[0][:, :], start=True, stop=False),
                            t.matmul(pL[:, :], onescol[:, :], slab[1][:, :], start=False, stop=True)][-1],
                            deps=[o_q0, o_q1, o_pl_last])
                        o_r1 = add("scalar", lambda sc: sc.activation(
                            rowst[0:1, 0:512], pL[:, :], AF.Ln, bias=cst[0:1, 1:2]), deps=[o_sm])
                        o_pl_last = o_r1
                        b_ap = cst[0:1, 6:7] if s_i == 0 else cst[0:1, 7:8]
                        o_r2 = add("scalar", lambda sc, b_ap=b_ap: sc.activation(
                            rowst[0:1, 0:512], rowst[0:1, 0:512], AF.Exp, bias=b_ap,
                            scale=-0.5), deps=[o_r1])
                        o_bc = add("tensor", lambda t: t.matmul(
                            ps[2][:, :], ones1f[:, :], rowst[0:1, 0:512], start=True, stop=True),
                            deps=[o_r2, o_ps2_last])
                        o_t0 = add("vector", lambda v, h=h, qc=qc, dstD=dstD: v.tensor_tensor(
                            dstD[:, (2 * h) * T + qc * 512:(2 * h) * T + (qc + 1) * 512],
                            cvo[0][:, :], ps[2][:, :], ALU.mult),
                            deps=[o_bc] + (dlast if h == 0 and qc == 0 else []))
                        o_t1 = add("vector", lambda v, h=h, qc=qc, dstD=dstD: v.tensor_tensor(
                            dstD[:, (2 * h + 1) * T + qc * 512:(2 * h + 1) * T + (qc + 1) * 512],
                            cvo[1][:, :], ps[2][:, :], ALU.mult), deps=[o_bc])
                        o_ps2_last = o_t1
                        o_l2tail = o_t1
                        dn_src[(s_i, 2 * h, qc)] = o_t0
                        dn_src[(s_i, 2 * h + 1, qc)] = o_t1
        last_qD = []
        last_kD = []
        last_vD = []

        dbg_ops = []
        if DEBUG_SPILLS and b == 0:
            alldn = [dn_src[(s2, m2, q2)] for s2 in range(3) for m2 in range(KT)
                     for q2 in range(NQ)]
            dbg_ops.append(add("gpsimd", lambda g: g.dma_start(out=dbgq[:, :], in_=qD[:, :]),
                               deps=alldn, dma=True))
            dbg_ops.append(add("gpsimd", lambda g: g.dma_start(out=dbgk[:, :], in_=kD[:, :]),
                               deps=alldn, dma=True))
            dbg_ops.append(add("gpsimd", lambda g: g.dma_start(out=dbgv[:, :], in_=vD[:, :]),
                               deps=alldn, dma=True))
        o_pAe = None
        o_pBe = None
        o_pCe = o_ps2_last and None
        o_pDe = None
        o_ps23_init = o_ps2_last
        o_ps2_war = None
        o_ps3_war = None
        o_ps3f_war = None
        o_pb16_war = None
        o_tt_war = []
        o_b6_war = []
        o_om_prev = None
        o_usb_war = None
        o_setK_war = [[], []]
        o_setV_war = [None, None]
        o_setTV_war = [None, None]
        o_setTK_war = [None, None]
        o_dn_last = []
        o_trV2_list = []
        o_prev_head_end = None
        o_odt_war = [None, None]
        o_ps01_init = o_l2tail
        Pb16 = QtT[2][:, 0:64].bitcast(BF16)
        Tt_sb = QtT[3][:, 0:64].bitcast(BF16)
        for h in range(H_DN):
            m0, m1 = 2 * h, 2 * h + 1
            o_Su = add("vector", lambda v: [
                v.memset(S1[:, :], 0.0), v.memset(Sb1[:, :], 0.0)][-1],
                deps=[dn_src[(0, 2 * h, 0)], o_prev_head_end])
            o_oTev = []
            dep_k = [dn_src[(1, m0, qc)] for qc in range(NQ)] + \
                    [dn_src[(1, m1, qc)] for qc in range(NQ)]
            dep_q = [dn_src[(0, m0, qc)] for qc in range(NQ)] + \
                    [dn_src[(0, m1, qc)] for qc in range(NQ)]
            dep_v = [dn_src[(2, m0, qc)] for qc in range(NQ)] + \
                    [dn_src[(2, m1, qc)] for qc in range(NQ)]
            for ci in range(NCH):
                t0 = ci * 128
                par = (h * NCH + ci) % 2
                sbase = 1024 * par
                KTs = slabT[:, sbase + 0:sbase + 256]
                VTs = slabT[:, sbase + 256:sbase + 512]
                TKs = slabT[:, sbase + 512:sbase + 768]
                TVs2 = slabT[:, sbase + 768:sbase + 1024]
                kd0 = kD[:, m0 * T + t0:m0 * T + t0 + 128]
                kd1 = kD[:, m1 * T + t0:m1 * T + t0 + 128]
                qd0 = qD[:, m0 * T + t0:m0 * T + t0 + 128]
                qd1 = qD[:, m1 * T + t0:m1 * T + t0 + 128]
                vd0 = vD[:, m0 * T + t0:m0 * T + t0 + 128]
                vd1 = vD[:, m1 * T + t0:m1 * T + t0 + 128]
                last_ch = (ci == NCH - 1)
                ps2b0 = ps[2][:, 0:64].bitcast(BF16)
                ps2b1 = ps[2][:, 64:128].bitcast(BF16)
                ps2b01 = ps[2][:, 0:128].bitcast(BF16)
                ps3b0 = ps[3][:, 0:64].bitcast(BF16)
                ps3b1 = ps[3][:, 64:128].bitcast(BF16)
                ps3b01 = ps[3][:, 0:128].bitcast(BF16)
                ps3f = ps[3][:, 256:384]
                o_ptK1 = add("tensor", lambda t, kd0=kd0, ps2b0=ps2b0: t.transpose(
                    ps2b0, kd0, ident[:, :]),
                    deps=dep_k + [o_ps2_war] + o_setK_war[par] +
                         ([o_ps23_init] if o_ps23_init else []))
                o_ptK2 = add("tensor", lambda t, kd1=kd1, ps2b1=ps2b1: t.transpose(
                    ps2b1, kd1, ident[:, :]), deps=[o_ptK1])
                o_evK = add("vector", lambda v, KTs=KTs, ps2b01=ps2b01: v.tensor_copy(
                    KTs, ps2b01), deps=[o_ptK2])
                o_ptV1 = add("tensor", lambda t, vd0=vd0, ps3b0=ps3b0: t.transpose(
                    ps3b0, vd0, ident[:, :]),
                    deps=dep_v + [o_ps3_war, o_setV_war[par]] +
                         ([o_ps23_init] if o_ps23_init else []))
                o_ps23_init = None
                o_ptV2 = add("tensor", lambda t, vd1=vd1, ps3b1=ps3b1: t.transpose(
                    ps3b1, vd1, ident[:, :]), deps=[o_ptV1])
                o_evV = add("vector", lambda v, VTs=VTs, ps3b01=ps3b01: v.tensor_copy(
                    VTs, ps3b01), deps=[o_ptV2])
                o_trV2_list.append(o_evV)
                o_A = add("tensor", lambda t, kd0=kd0, kd1=kd1: [
                    t.matmul(pA[:, :], kd0, kd0, start=True, stop=False),
                    t.matmul(pA[:, :], kd1, kd1, start=False, stop=True)][-1],
                    deps=dep_k + [o_pAe])
                o_Q0 = add("vector", lambda v: v.tensor_tensor(QpR[0], pA[:, :],
                                                               m_ltn[:, :], ALU.mult),
                           deps=[o_A] + ([o_spA.get((b - 1, TT - 2)), o_spA.get((b - 1, TT - 1))]
                                         if h == 0 and ci == 0 else []))
                o_Qt0 = add("vector", lambda v: v.tensor_tensor(QtR[0], pA[:, :],
                                                                m_utn[:, :], ALU.mult),
                            deps=[o_Q0, o_A])
                o_pAe = o_Qt0
                o_P0 = add("vector", lambda v: v.tensor_tensor(
                    QpR[2], QpT[0][:, :].bitcast(F32), ident[:, :], ALU.add),
                    deps=[o_Q0, o_A])
                o_qp_prev, o_qt_prev, o_p_prev = o_Q0, o_Qt0, o_P0
                for j in range(1, 7):
                    c0, c1 = (j - 1) % 2, j % 2
                    o_mq = add("tensor", lambda t, c0=c0: t.matmul(
                        pA[:, :], QtR[c0], QpR[c0], start=True, stop=True),
                        deps=[o_qp_prev, o_qt_prev, o_pAe])
                    o_qp = add("vector", lambda v, c1=c1: v.tensor_copy(QpR[c1], pA[:, :]),
                               deps=[o_mq])
                    o_pAe = o_qp
                    o_mqt = add("tensor", lambda t, c0=c0: t.matmul(
                        pB[:, :], QpR[c0], QtR[c0], start=True, stop=True),
                        deps=[o_qp_prev, o_qt_prev, o_pBe])
                    o_qt = add("vector", lambda v, c1=c1: v.tensor_copy(QtR[c1], pB[:, :]),
                               deps=[o_mqt])
                    o_pBe = o_qt
                    o_mp = add("tensor", lambda t, c1=c1, c0=c0, ps3f=ps3f: t.matmul(
                        ps3f, QtR[c1], QpR[2 + c0], start=True, stop=True),
                        deps=[o_qt, o_p_prev, o_ps3f_war, o_evV if j == 1 else None])
                    o_pev = add("vector", lambda v, c1=c1, c0=c0, ps3f=ps3f: v.tensor_tensor(
                        QpR[2 + c1], QpT[2 + c0][:, :].bitcast(F32), ps3f, ALU.add),
                        deps=[o_mp])
                    o_ps3f_war = o_pev
                    o_ps3_war = o_pev
                    o_qp_prev, o_qt_prev, o_p_prev = o_qp, o_qt, o_pev
                o_pb16 = add("vector", lambda v: v.tensor_copy(
                    Pb16, QpT[2][:, :].bitcast(F32)), deps=[o_p_prev, o_pb16_war])
                o_ttr = add("tensor", lambda t, ps2b0=ps2b0: t.transpose(
                    ps2b0, Pb16, ident[:, :]), deps=[o_pb16, o_evK])
                o_evTt = add("vector", lambda v, ps2b0=ps2b0: v.tensor_copy(Tt_sb, ps2b0),
                             deps=[o_ttr] + o_tt_war)
                o_ps2_war = o_evTt
                o_TV = add("tensor", lambda t, VTs=VTs: t.matmul(
                    bank4[:, 0:256], Tt_sb, VTs, start=True, stop=True),
                    deps=[o_evTt, o_evV, o_pAe])
                o_TVev = add("vector", lambda v, TVs2=TVs2: v.tensor_copy(
                    TVs2, bank4[:, 0:256]), deps=[o_TV, o_setTV_war[par]])
                o_pAe = o_TVev
                o_TK1 = add("tensor", lambda t, KTs=KTs: t.matmul(
                    bank5[:, 0:128], KTs[:, 0:128], Tt_sb, start=True, stop=True),
                    deps=[o_evTt, o_evK, o_pBe])
                o_TK2 = add("tensor", lambda t, KTs=KTs: t.matmul(
                    bank5[:, 128:256], KTs[:, 128:256], Tt_sb, start=True, stop=True),
                    deps=[o_TK1])
                o_TKev = add("vector", lambda v, TKs=TKs: v.tensor_copy(
                    TKs, bank5[:, 0:256]), deps=[o_TK2, o_setTK_war[par]])
                o_pBe = o_TKev
                o_tt_war = [o_TV, o_TK2]
                o_pb16_war = o_ttr
                o_G = add("tensor", lambda t, kd0=kd0, kd1=kd1, qd0=qd0, qd1=qd1: [
                    t.matmul(bank6[:, 256:384], kd0, qd0, start=True, stop=False),
                    t.matmul(bank6[:, 256:384], kd1, qd1, start=False, stop=True)][-1],
                    deps=dep_q + o_b6_war)
                o_Gm = add("vector", lambda v: v.tensor_tensor(Gm[:, :], bank6[:, 256:384],
                                                               m_uti[:, :], ALU.mult),
                           deps=[o_G, o_om_prev])
                o_U12 = add("tensor", lambda t, TKs=TKs: [
                    t.matmul(pC[:, :], TKs[:, 0:128], Sb1[:, 0:DH], start=True, stop=False),
                    t.matmul(pC[:, :], TKs[:, 128:256], Sb1[:, DH:2 * DH],
                             start=False, stop=True)][-1],
                    deps=[o_TKev, o_Su, o_pCe, o_Gm])
                o_Uev = add("vector", lambda v, TVs2=TVs2: v.tensor_tensor(
                    Usb[:, :], TVs2, pC[:, :], ALU.subtract),
                    deps=[o_U12, o_TVev, o_usb_war])
                o_pCe = o_Uev
                o_setTV_war[par] = o_Uev
                o_setTK_war[par] = o_U12
                o_setV_war[par] = o_TV
                o_om = add("tensor", lambda t, qd0=qd0, qd1=qd1: [
                    t.matmul(pC[:, :], qd0, Sb1[:, 0:DH], start=True, stop=False),
                    t.matmul(pC[:, :], qd1, Sb1[:, DH:2 * DH], start=False, stop=False),
                    t.matmul(pC[:, :], Gm[:, :], Usb[:, :], start=False, stop=True)][-1],
                    deps=dep_q + [o_Uev, o_Gm, o_Su, o_pCe])
                o_om_prev = o_om
                o_oe = add("vector", lambda v, ci=ci: v.tensor_copy(
                    oT[:, ci * DH:(ci + 1) * DH], pC[:, :]),
                    deps=[o_om] + (last_oT if ci == 0 and h == 0 else []))
                o_pCe = o_oe
                o_b6_war = [o_om, o_oe]
                o_oTev.append(o_oe)
                if not last_ch:
                    o_su1 = add("tensor", lambda t, KTs=KTs: t.matmul(
                        pD[:, :], KTs[:, 0:128], Usb[:, :], start=True, stop=True),
                        deps=[o_Uev, o_evK, o_pDe, o_om])
                    o_sa1 = add("vector", lambda v: v.tensor_tensor(
                        S1[:, 0:DH], S1[:, 0:DH], pD[:, :], ALU.add),
                        deps=[o_su1, o_Su])
                    o_sb1 = add("vector", lambda v: v.tensor_copy(
                        Sb1[:, 0:DH], S1[:, 0:DH]), deps=[o_sa1])
                    o_su2 = add("tensor", lambda t, KTs=KTs: t.matmul(
                        pD[:, :], KTs[:, 128:256], Usb[:, :], start=True, stop=True),
                        deps=[o_sa1, o_evK, o_Uev])
                    o_sa2 = add("vector", lambda v: v.tensor_tensor(
                        S1[:, DH:2 * DH], S1[:, DH:2 * DH], pD[:, :], ALU.add),
                        deps=[o_su2, o_Su])
                    o_sb2 = add("vector", lambda v: v.tensor_copy(
                        Sb1[:, DH:2 * DH], S1[:, DH:2 * DH]), deps=[o_sa2])
                    o_pDe = o_sa2
                    o_usb_war = o_su2
                    o_setK_war[par] = [o_su2, o_TK2]
                    o_Su = add("vector", lambda v: v.engine_nop(), deps=[o_sb1, o_sb2])
                else:
                    o_usb_war = o_om
                    o_setK_war[par] = [o_TK2]
            o_dn_last.append(o_oe)
            o_prev_head_end = o_oe
            if DEBUG_SPILLS and b == 0 and h == 0:
                dbg_ops.append(add("gpsimd", lambda g: g.dma_start(out=dbgo[:, :], in_=oT[:, :]),
                                   deps=o_oTev, dma=True))
            o_sq = add("scalar", lambda sc: sc.activation(rmsq[:, :], oT[:, :], AF.Square),
                       deps=o_oTev + (dbg_ops if DEBUG_SPILLS else []))
            o_rd = add("vector", lambda v: v.tensor_reduce(
                rmsr[:, :], rmsq.ap().rearrange("p (i d) -> p i d", d=DH), AX.X, ALU.add),
                deps=[o_sq])
            o_r1 = add("scalar", lambda sc: sc.activation(
                rmsr[:, :], rmsr[:, :], AF.Ln, bias=cst[:, 2:3], scale=1.0 / DH), deps=[o_rd])
            o_r2 = add("scalar", lambda sc: sc.activation(
                rmsr[:, :], rmsr[:, :], AF.Exp, scale=-0.5), deps=[o_r1])
            o_on = []
            for ci in range(NCH):
                o_on.append(add("vector", lambda v, ci=ci: v.scalar_tensor_tensor(
                    rmsq[:, ci * DH:(ci + 1) * DH], oT[:, ci * DH:(ci + 1) * DH],
                    rmsr[:, ci:ci + 1], onwB[:, :], ALU.mult, ALU.mult), deps=[o_r2, o_sq]))
            for ci in range(NCH):
                for dvt in range(2):
                    pso = ps[dvt][:, 0:64].bitcast(BF16)
                    o_pto = add("tensor", lambda t, ci=ci, dvt=dvt, pso=pso: t.transpose(
                        pso,
                        rmsq[:, ci * DH + dvt * 128:ci * DH + (dvt + 1) * 128], ident[:, :]),
                        deps=[o_on[ci], o_om, o_odt_war[dvt]] +
                             ([o_ps01_init] if o_ps01_init else []))
                    o_ps01_init = None
                    o_odt = add("vector", lambda v, ci=ci, dvt=dvt, h=h, pso=pso: v.tensor_copy(
                        qD[:, (2 * h + dvt) * T + ci * 128:(2 * h + dvt) * T + (ci + 1) * 128],
                        pso),
                        deps=[o_pto] + [dn_src[(0, 2 * h + dvt, qc)] for qc in range(NQ)])
                    o_odt_war[dvt] = o_odt
                    last_qD.append(o_odt)
        o_oD_done = list(last_qD)
        last_oT = []

        o_wol = {}
        o_wol[0] = add("gpsimd", lambda g: g.dma_start(
            out=oT[:, 0:4096].rearrange("p (k m) -> p k m", k=KT),
            in_=wo[:, 0:512].rearrange("(k p) m -> p k m", p=128)),
            deps=[o_sq, o_on[NCH - 1]] + o_oD_done[-2:], dma=True)
        o_wol[1] = add("gpsimd", lambda g: g.dma_start(
            out=rmsq[:, 0:4096].rearrange("p (k m) -> p k m", k=KT),
            in_=wo[:, 512:1024].rearrange("(k p) m -> p k m", p=128)),
            deps=o_oD_done, dma=True)
        o_evw = [None, None]
        o_wo_mm_last = None
        for tt in range(TT):
            for chh in range(2):
                gi = tt * 2 + chh
                pb = ps[gi % 2]
                rbuf = oT if chh == 0 else rmsq
                o_mm = add("tensor", lambda t, tt=tt, pb=pb, rbuf=rbuf: [
                    t.matmul(pb[:, :], qD[:, k * T + tt * 128:k * T + (tt + 1) * 128],
                             rbuf[:, k * 512:(k + 1) * 512],
                             start=(k == 0), stop=(k == KT - 1))
                    for k in range(KT)][-1],
                    deps=o_oD_done + [o_wol[chh], o_evw[gi % 2]])
                o_wo_mm_last = o_mm
                evb = ev0 if tt % 2 == 0 else ev1
                o_evw[gi % 2] = add("vector", lambda v, chh=chh, pb=pb, evb=evb: v.tensor_copy(
                    evb[:, chh * 512:(chh + 1) * 512], pb[:, :]),
                    deps=[o_mm, o_spA.get((b, tt - 2))] +
                         (o_zlist if b == 0 and tt < 2 else []))
            o_spA[(b, tt)] = add("gpsimd", lambda g, tt=tt, b=b,
                                 evb=(ev0 if tt % 2 == 0 else ev1): g.dma_start(
                out=sp_attn[b, tt * 128:(tt + 1) * 128, :], in_=evb[:, :]),
                deps=[o_evw[0], o_evw[1]], dma=True)
        o_attn_done = [o_spA[(b, tt)] for tt in range(TT)]

        o_fw = {}
        o_fq_done = {}
        o_evf = [None, None]
        o_first_evf = {}
        for s_i2, (dstD2, bcc) in enumerate([(qD, bqc), (kD, bkc)]):
            scw = sc_w[s_i2]
            for mi in range(KT):
                o_fw[(s_i2, mi)] = add("gpsimd", lambda g, scw=scw, mi=mi: g.dma_start(
                    out=wbufA[mi % 2][:, 0:1024].rearrange("p (k m) -> p k m", k=KT),
                    in_=scw[:, mi * 128:(mi + 1) * 128].rearrange("(k p) m -> p k m", p=128)),
                    deps=[last_wbufA[mi % 2]] +
                         ([o_wol[0], o_wol[1]] if s_i2 == 0 and mi < 2 else []), dma=True)
                last_wbufA[mi % 2] = o_fw[(s_i2, mi)]
                for qc in range(NQ):
                    gi = mi * NQ + qc
                    pb = ps[gi % 2]
                    o_mm = add("tensor", lambda t, mi=mi, qc=qc, pb=pb: [
                        t.matmul(pb[:, :], wbufA[mi % 2][:, k * 128:(k + 1) * 128],
                                 xnD[:, k * T + qc * 512:k * T + (qc + 1) * 512],
                                 start=(k == 0), stop=(k == KT - 1))
                        for k in range(KT)][-1],
                        deps=[o_fw[(s_i2, mi)], o_evf[gi % 2],
                              o_evw[0] if s_i2 == 0 and mi == 0 and qc == 0 else None,
                              o_evw[1] if s_i2 == 0 and mi == 0 and qc == 0 else None])
                    last_wbufA[mi % 2] = o_mm
                    extra = []
                    if s_i2 == 0 and mi == 0 and qc == 0:
                        extra = [o_wo_mm_last]
                    if s_i2 == 1 and mi == 0 and qc == 0:
                        extra = o_dn_last + [o_wo_mm_last]
                    o_evf[gi % 2] = add("vector", lambda v, mi=mi, qc=qc, pb=pb,
                                        dstD2=dstD2, bcc=bcc: v.tensor_scalar(
                        dstD2[:, mi * T + qc * 512:mi * T + (qc + 1) * 512], pb[:, :],
                        bcc[:, mi:mi + 1], None, ALU.add),
                        deps=[o_mm] + extra +
                             ([o_parm_last] if b == 0 and s_i2 == 0 and mi == 0 and qc == 0 else []))
                    o_fq_done[(s_i2, mi, qc)] = o_evf[gi % 2]
        all_fqk = [o_fq_done[(si2, mi, qc)] for si2 in range(2) for mi in range(KT)
                   for qc in range(NQ)]
        vwb = [oT, rmsq]
        o_vw = {}
        for hc in range(2):
            o_vw[hc] = add("gpsimd", lambda g, hc=hc: g.dma_start(
                out=vwb[hc][:, 0:4096].rearrange("p (k f) -> p k f", k=KT),
                in_=sc_w[2][:, hc * 512:(hc + 1) * 512].rearrange("(k p) f -> p k f", p=128)),
                deps=[o_wo_mm_last], dma=True)
        o_evv = [None, None]
        o_vt_last = None
        first_vt_deps = [o_trV2_list[-1]] + (list(o_scale) if b > 0 else list(last_vD))
        for tt in range(TT):
            for hc in range(2):
                gi = tt * 2 + hc
                pb = ps[2 + gi % 2]
                o_mm = add("tensor", lambda t, tt=tt, hc=hc, pb=pb: [
                    t.matmul(pb[:, :], xnD[:, k * T + tt * 128:k * T + (tt + 1) * 128],
                             vwb[hc][:, k * 512:(k + 1) * 512],
                             start=(k == 0), stop=(k == KT - 1))
                    for k in range(KT)][-1],
                    deps=[o_vw[hc], o_tr_xn, o_evv[gi % 2], o_ps2_last if tt == 0 else None])
                o_evv[gi % 2] = add("vector", lambda v, tt=tt, hc=hc, pb=pb: v.tensor_tensor(
                    vD[:, tt * 1024 + hc * 512:tt * 1024 + (hc + 1) * 512], pb[:, :],
                    bvB[:, hc * 512:(hc + 1) * 512], ALU.add),
                    deps=[o_mm] + (first_vt_deps if tt == 0 and hc == 0 else []))
                o_vt_last = o_evv[gi % 2]
        o_vt_all = [o_evv[0], o_evv[1]]

        o_hd_parts = []
        o_den_dma = []
        o_poE = None
        o_poO = None
        o_ts_ev_prev = None
        o_slab_war = {}
        o_sh_prev = [None, None]
        o_denr_war = [None, None]
        for hp in range(8):
            hE, hO = 2 * hp, 2 * hp + 1
            o_vg = []
            for par, hh in enumerate((hE, hO)):
                o_v1 = add("vector", lambda v, par=par, hh=hh: v.tensor_copy(
                    vaug[par].ap().rearrange("p (i c) -> p i c", c=65)[:, :, 0:64],
                    vD.ap().rearrange("p (i c) -> p i c", c=1024)[:, :, hh * 64:(hh + 1) * 64]),
                    deps=(o_vt_all if hp == 0 else []) + [o_pv_war[par]] +
                         ([last_wbufA[par]] if hp == 0 else []))
                o_v2 = add("vector", lambda v, par=par: v.memset(
                    vaug[par].ap().rearrange("p (i c) -> p i c", c=65)[:, :, 64:65], 1.0),
                    deps=[o_v1])
                o_vg.append(o_v2)
            o_tsm = add("tensor", lambda t: [
                t.matmul(ps[2][0:65, par * 16 + kt:par * 16 + kt + 1],
                         vaug[par][:, kt * 65:(kt + 1) * 65], onescol[:, :],
                         start=True, stop=True)
                for par in range(2) for kt in range(TT)][-1],
                deps=o_vg + [o_poE, o_ts_ev_prev])
            o_tse = add("vector", lambda v: v.tensor_copy(tsum[:, :], ps[2][0:65, 0:32]),
                        deps=[o_tsm])
            o_ts_ev_prev = o_tse
            o_suf = None
            for par in range(2):
                for qc in range(NQ):
                    lo = 4 * qc + 4
                    if lo < TT:
                        o_suf = add("vector", lambda v, par=par, qc=qc, lo=lo: v.tensor_reduce(
                            sufc[:, par * NQ + qc:par * NQ + qc + 1],
                            tsum[:, par * 16 + lo:par * 16 + 16], AX.X, ALU.add), deps=[o_tse, o_suf])
                    else:
                        o_suf = add("vector", lambda v, par=par, qc=qc: v.memset(
                            sufc[:, par * NQ + qc:par * NQ + qc + 1], 0.0), deps=[o_tse, o_suf])
            for qc in range(NQ):
                nkt = 4 * qc + 4
                o_pv_lastE = None
                o_pv_lastO = None
                for kt in range(nkt):
                    sl_e = slab[kt % 2]
                    sl_o = slab[2 + kt % 2]
                    kslE = kD[0:64, hp * T + kt * 128:hp * T + (kt + 1) * 128]
                    kslO = kD[64:128, hp * T + kt * 128:hp * T + (kt + 1) * 128]
                    qslE = qD[0:64, hp * T + qc * 512:hp * T + (qc + 1) * 512]
                    qslO = qD[64:128, hp * T + qc * 512:hp * T + (qc + 1) * 512]
                    o_sE = add("tensor", lambda t, kslE=kslE, qslE=qslE: t.matmul(
                        ps[0][:, :], kslE, qslE, start=True, stop=True),
                        deps=[o_psE] + (all_fqk if hp == 0 and qc == 0 and kt == 0 else []))
                    o_sO = add("tensor", lambda t, kslO=kslO, qslO=qslO: t.matmul(
                        ps[1][:, :], kslO, qslO, start=True, stop=True),
                        deps=[o_psO, o_sE])
                    o_xE = add("scalar", lambda sc, sl_e=sl_e: sc.activation(
                        sl_e[:, :], ps[0][:, :], AF.Exp, scale=1.0 / 32.0),
                        deps=[o_sE, o_slab_war.get((0, kt % 2))])
                    o_psE = o_xE
                    o_xO = add("scalar", lambda sc, sl_o=sl_o: sc.activation(
                        sl_o[:, :], ps[1][:, :], AF.Exp, scale=1.0 / 32.0),
                        deps=[o_sO, o_slab_war.get((1, kt % 2))])
                    o_psO = o_xO
                    r = kt - 4 * qc
                    if r >= 0:
                        def f_fix(v, sl=sl_e, r=r):
                            if r > 0:
                                v.memset(sl[:, 0:r * 128], 1.0)
                            return v.copy_predicated(sl[:, r * 128:(r + 1) * 128],
                                                     m_fut[:, :], onesb[:, :])
                        o_fE = add("vector", f_fix, deps=[o_xE])

                        def f_fixO(v, sl=sl_o, r=r):
                            if r > 0:
                                v.memset(sl[:, 0:r * 128], 1.0)
                            return v.copy_predicated(sl[:, r * 128:(r + 1) * 128],
                                                     m_fut[:, :], onesb[:, :])
                        o_fO = add("vector", f_fixO, deps=[o_xO])
                    else:
                        o_fE, o_fO = o_xE, o_xO
                    o_pv_lastE = add("tensor", lambda t, kt=kt, sl_e=sl_e, nkt=nkt: t.matmul(
                        ps[2][0:65, :], vaug[0][:, kt * 65:(kt + 1) * 65], sl_e[:, :],
                        start=(kt == 0), stop=(kt == nkt - 1)),
                        deps=[o_fE, o_poE if kt == 0 else None, o_tse if kt == 0 else None])
                    o_slab_war[(0, kt % 3)] = o_pv_lastE
                    o_pv_lastO = add("tensor", lambda t, kt=kt, sl_o=sl_o, nkt=nkt: t.matmul(
                        ps[3][0:65, :], vaug[1][:, kt * 65:(kt + 1) * 65], sl_o[:, :],
                        start=(kt == 0), stop=(kt == nkt - 1)),
                        deps=[o_fO, o_poO if kt == 0 else None])
                    o_slab_war[(1, kt % 3)] = o_pv_lastO
                o_pv_war[0] = o_pv_lastE
                o_pv_war[1] = o_pv_lastO
                hd_dep = [o_vt_last] + all_fqk if hp == 0 and qc == 0 else []
                o_heE = add("vector", lambda v, hp=hp, qc=qc: v.tensor_scalar(
                    xnD[0:64, hp * T + qc * 512:hp * T + (qc + 1) * 512],
                    ps[2][0:64, :], sufc[0:64, qc:qc + 1], None, ALU.add),
                    deps=[o_pv_lastE, o_suf] + hd_dep)
                o_dnE = add("vector", lambda v, qc=qc: v.tensor_scalar(
                    denr[0][64:65, :], ps[2][64:65, :], sufc[64:65, qc:qc + 1], None, ALU.add),
                    deps=[o_pv_lastE, o_suf, o_denr_war[0]])
                o_poE = o_dnE
                o_hoO = add("vector", lambda v, qc=qc: v.tensor_scalar(
                    oddst[qc % 2][:, :], ps[3][0:64, :], sufc[0:64, NQ + qc:NQ + qc + 1],
                    None, ALU.add),
                    deps=[o_pv_lastO, o_suf, o_sh_prev[qc % 2]])
                o_dnO = add("vector", lambda v, qc=qc: v.tensor_scalar(
                    denr[1][64:65, :], ps[3][64:65, :], sufc[64:65, NQ + qc:NQ + qc + 1],
                    None, ALU.add),
                    deps=[o_pv_lastO, o_suf, o_denr_war[1]])
                o_poO = o_dnO
                o_sh = add("gpsimd", lambda g, hp=hp, qc=qc: g.dma_start(
                    out=xnD[64:128, hp * T + qc * 512:hp * T + (qc + 1) * 512],
                    in_=oddst[qc % 2][:, :]),
                    deps=[o_hoO] + hd_dep, dma=True)
                o_sh_prev[qc % 2] = o_sh
                o_hd_parts += [o_heE, o_sh]
                o_d1 = add("gpsimd", lambda g, hE=hE, qc=qc: g.dma_start(
                    out=denM[hE:hE + 1, qc * 512:(qc + 1) * 512], in_=denr[0][64:65, :]),
                    deps=[o_dnE], dma=True)
                o_d2 = add("gpsimd", lambda g, hO=hO, qc=qc: g.dma_start(
                    out=denM[hO:hO + 1, qc * 512:(qc + 1) * 512], in_=denr[1][64:65, :]),
                    deps=[o_dnO], dma=True)
                o_denr_war[0] = o_d1
                o_denr_war[1] = o_d2
                o_den_dma += [o_d1, o_d2]
        last_wbufA = [o_pv_war[0], o_pv_war[1]]
        o_scale = []
        o_pbc = None
        o_bm_prev = None
        o_rc = None
        for qc in range(NQ):
            o_dl = add("gpsimd", lambda g, qc=qc: g.dma_start(
                out=denQ, in_=denM[0:16, qc * 512:(qc + 1) * 512]),
                deps=o_den_dma + [o_rc, o_pv_war[0], o_pv_war[1]], dma=True)
            o_rc = add("vector", lambda v: v.reciprocal(denr[0][0:16, :], denQ),
                       deps=[o_dl, o_denr_war[0] if qc == 0 else None, o_bm_prev])
            o_rowE = None
            o_rowO = None
            for hp in range(8):
                o_rowE = add("gpsimd", lambda g, hp=hp: g.dma_start(
                    out=cvo[1][0:1, 0:512], in_=denr[0][2 * hp:2 * hp + 1, :]),
                    deps=[o_rc, o_bm_prev, o_rowE], dma=True)
                o_rowO = add("gpsimd", lambda g, hp=hp: g.dma_start(
                    out=rowst[0:1, 0:512], in_=denr[0][2 * hp + 1:2 * hp + 2, :]),
                    deps=[o_rc, o_bm_prev, o_rowO, o_ps2_last if hp == 0 and qc == 0 else None],
                    dma=True)
                o_bm = add("tensor", lambda t: [
                    t.matmul(ps[0][0:64, :], ones1f[0:1, 0:64], cvo[1][0:1, 0:512],
                             start=True, stop=True),
                    t.matmul(ps[0][64:128, :], ones1f[0:1, 0:64], rowst[0:1, 0:512],
                             start=True, stop=True)][-1],
                    deps=[o_rowE, o_rowO, o_pbc,
                          o_psE if hp == 0 and qc == 0 else None])
                o_bm_prev = o_bm
                o_pbc = add("vector", lambda v, hp=hp, qc=qc: v.tensor_tensor(
                    xnD[:, hp * T + qc * 512:hp * T + (qc + 1) * 512],
                    xnD[:, hp * T + qc * 512:hp * T + (qc + 1) * 512],
                    ps[0][:, :], ALU.mult),
                    deps=[o_bm] + (o_hd_parts if hp == 0 and qc == 0 else []))
                o_scale.append(o_pbc)
        o_tr_hd = []
        for tt in range(TT):
            for k in range(KT):
                o_tr_hd.append(add("sync", lambda sc, tt=tt, k=k: sc.dma_start_transpose(
                    qD[:, tt * D + k * 128:tt * D + (k + 1) * 128],
                    xnD[:, k * T + tt * 128:k * T + (tt + 1) * 128]),
                    deps=o_scale + ([o_psE, o_psO, o_wo_mm_last] if tt == 0 and k == 0 else []),
                    dma=True))
        for tt in range(TT):
            o_spHT[(b, tt)] = add("gpsimd", lambda g, tt=tt, b=b: g.dma_start(
                out=sp_hdt[b * T + tt * 128:b * T + (tt + 1) * 128, :],
                in_=qD[:, tt * D:(tt + 1) * D]),
                deps=[o_tr_hd[tt * KT + KT - 1]], dma=True)
        all_last_A7 = o_scale + [o_spHT[(b, tt)] for tt in range(TT)]

        o_pR_last = None
        for tt in range(TT):
            o_lg = add("tensor", lambda t, tt=tt: [
                t.matmul(pR[:, :], xnD[:, k * T + tt * 128:k * T + (tt + 1) * 128],
                         wswT[:, k * NE:(k + 1) * NE], start=(k == 0), stop=(k == KT - 1))
                for k in range(KT)][-1],
                deps=o_scale + [o_pR_last, o_route_tail] +
                     ([o_parm_last] if b == 0 and tt == 0 else []))
            o_1 = add("vector", lambda v: v.tensor_tensor(lgt[:, :], pR[:, :], bswB[:, :],
                                                          ALU.add), deps=[o_lg, o_route_tail])
            o_2 = add("vector", lambda v: v.tensor_reduce(rt1[:, :], lgt[:, :], AX.X, ALU.max,
                                                          negate=True), deps=[o_1])
            o_3 = add("scalar", lambda sc: sc.activation(prb[:, :], lgt[:, :], AF.Exp,
                                                         bias=rt1[:, :]), deps=[o_2])
            o_4 = add("vector", lambda v: v.tensor_reduce(rt2[:, :], prb[:, :], AX.X, ALU.add),
                      deps=[o_3])
            o_5 = add("vector", lambda v, b=b, tt=tt: v.reciprocal(
                selP[:, b * TT + tt:b * TT + tt + 1], rt2[:, :]), deps=[o_4])
            o_6 = add("vector", lambda v: v.tensor_reduce(rt3[:, :], prb[:, :], AX.X, ALU.max),
                      deps=[o_3])
            o_7 = add("vector", lambda v: v.tensor_scalar(rt0[:, :], prb[:, :], rt3[:, :],
                                                          None, ALU.is_ge), deps=[o_6])
            o_8 = add("vector", lambda v: v.scalar_tensor_tensor(
                rt0[:, :], rt0[:, :], cst[:, 3:4], iota8[:, :], ALU.mult, ALU.add), deps=[o_7])
            o_9 = add("vector", lambda v: v.tensor_reduce(rt3[:, :], rt0[:, :], AX.X, ALU.min),
                      deps=[o_8])
            rcol = routesF[:, b * TT + tt:b * TT + tt + 1]
            o_10 = add("vector", lambda v, rcol=rcol: v.tensor_scalar(
                rcol, rt3[:, :], 1000.0, None, ALU.add), deps=[o_9])
            o_rts[b * TT + tt] = o_10
            ohsl = oneh8[:, :]
            o_11 = add("vector", lambda v, ohsl=ohsl, rcol=rcol: v.tensor_scalar(
                ohsl, iota8[:, :], rcol[:, 0:1], None, ALU.is_equal),
                deps=[o_10, o_cm_prev, o_tt2_prev])
            o_cm = add("tensor", lambda t, ohsl=ohsl: t.matmul(
                pR[:, :], m_uti[:, :], ohsl, start=True, stop=False), deps=[o_11, o_1])
            o_cm_prev = o_cm
            o_cb = add("tensor", lambda t: t.matmul(
                pR[:, :], ones1f[:, :], carry[:, :], start=False, stop=True),
                deps=[o_cm, o_route_tail])
            o_sel = add("vector", lambda v, ohsl=ohsl: v.tensor_tensor(
                rt0[:, :], pR[:, :], ohsl, ALU.mult), deps=[o_cb])
            o_pR_last = o_sel
            o_lp = add("vector", lambda v, b=b, tt=tt: v.tensor_reduce(
                lpos[:, b * TT + tt:b * TT + tt + 1], rt0[:, :], AX.X, ALU.add), deps=[o_sel])
            o_tt2 = add("tensor", lambda t, ohsl=ohsl: t.matmul(
                pRc[:, :], onescol[:, :], ohsl, start=True, stop=True),
                deps=[o_11, o_route_tail, o_sel])
            o_tt2_prev = o_tt2
            o_route_tail = add("vector", lambda v: v.tensor_tensor(
                carry[:, :], carry[:, :], pRc[:, :], ALU.add), deps=[o_tt2, o_lp])
        last_xnD = list(all_last_A7) + [o_pR_last, o_route_tail]
        last_qD = [o_spHT[(b, tt)] for tt in range(TT)]
        last_kD = [o_psE, o_psO]
        last_vD = [o_pv_war[0], o_pv_war[1]]
        last_oT = [o_vt_all[0], o_vt_all[1], o_vt_last, o_wo_mm_last]

    o_ci = add("gpsimd", lambda g: g.dma_start(out=cc_in[:, :], in_=carry[:, :]),
               deps=[o_route_tail], dma=True)

    o_b2a = add("gpsimd", lambda g: g.dma_start(out=rt1[0:8, 0:1], in_=carry[0:1, 0:NE]),
                deps=[o_route_tail], dma=True)
    o_b2b = add("vector", lambda v: v.tensor_scalar(rt2[0:8, 0:1], rt1[0:8, 0:1],
                                                    float(GR - 1), None, ALU.add), deps=[o_b2a])
    o_b2c1 = add("vector", lambda v: v.tensor_copy(slotu[0:8, 0:1], rt2[0:8, 0:1]),
                 deps=[o_b2b])
    o_b2c2 = add("vector", lambda v: v.tensor_scalar(slotu[0:8, 0:1], slotu[0:8, 0:1],
                                                     9, None, ALU.logical_shift_right),
                 deps=[o_b2c1])
    o_b2c3 = add("vector", lambda v: v.tensor_scalar(slotu[0:8, 0:1], slotu[0:8, 0:1],
                                                     9, None, ALU.logical_shift_left),
                 deps=[o_b2c2])
    o_b2d = add("vector", lambda v: v.tensor_copy(rt1[0:8, 0:1], slotu[0:8, 0:1]),
                deps=[o_b2c3])
    o_b2e = add("vector", lambda v: v.tensor_copy(b2cb[0:8, 0:1], rt1[0:8, 0:1]), deps=[o_b2d])
    o_b2f = add("tensor", lambda t: t.matmul(pA[0:8, 0:1], m_utn[0:8, 0:8], b2cb[0:8, 0:1],
                                             start=True, stop=True),
                deps=[o_b2e, o_route_tail, o_pR_last])
    o_b2g = add("vector", lambda v: v.tensor_scalar(rt2[0:8, 0:1], pA[0:8, 0:1], -1.0,
                                                    None, ALU.mult), deps=[o_b2f])
    o_b2h = add("vector", lambda v: v.tensor_copy(slotu[0:8, 0:1], rt2[0:8, 0:1]), deps=[o_b2g])
    o_aod1 = add("gpsimd", lambda g: g.dma_start(out=aoD[:, 0:1], in_=slotu[0:8, 0:1],
                                                 allow_slow_non_contiguous=True),
                 deps=[o_b2h], dma=True)
    o_aod2 = add("gpsimd", lambda g: g.dma_start(out=aoD[:, 1:2], in_=slotu[0:8, 0:1],
                                                 allow_slow_non_contiguous=True),
                 deps=[o_b2h], dma=True)
    o_b2i = add("vector", lambda v: v.tensor_tensor(rt3[0:8, 0:1], rt2[0:8, 0:1],
                                                    rt1[0:8, 0:1], ALU.add), deps=[o_b2g, o_b2d])
    o_b2i2 = add("vector", lambda v: v.tensor_scalar(rt3[0:8, 0:1], rt3[0:8, 0:1],
                                                     1.0 / GR, None, ALU.mult), deps=[o_b2i])
    o_b2j = add("gpsimd", lambda g: g.dma_start(out=b2row[0:1, 0:NE], in_=rt3[0:8, 0:1]),
                deps=[o_b2i2], dma=True)
    o_b2k = add("tensor", lambda t: t.matmul(pA[0:NG, 0:NE], ones1f[0:1, 0:NG],
                                             b2row[0:1, 0:NE], start=True, stop=True),
                deps=[o_b2j, o_b2g])
    o_b2l = add("vector", lambda v: v.tensor_scalar(egsb[0:NG, 0:NE], pA[0:NG, 0:NE],
                                                    widxP[0:NG, 0:1], None, ALU.is_le),
                deps=[o_b2k, o_ikp])
    o_b2m = add("vector", lambda v: v.tensor_reduce(egcol[0:NG, 0:1], egsb[0:NG, 0:NE],
                                                    AX.X, ALU.add), deps=[o_b2l])
    o_b2n = add("vector", lambda v: v.tensor_scalar(egcol[0:NG, 0:1], egcol[0:NG, 0:1],
                                                    7.0, None, ALU.min), deps=[o_b2m])
    o_b2o = add("gpsimd", lambda g: g.dma_start(out=b2row[0:1, 8:8 + NG],
                                                in_=egcol[0:NG, 0:1]),
                deps=[o_b2n], dma=True)

    o_aor = add("gpsimd", lambda g: g.dma_start(out=b2row[0:1, 8 + NG:8 + NG + NE],
                                                in_=rt2[0:8, 0:1]),
                deps=[o_b2g, o_b2o], dma=True)
    o_aom = add("tensor", lambda t: t.matmul(pA[:, 0:NE], ones1f[:, :],
                                             b2row[0:1, 8 + NG:8 + NG + NE],
                                             start=True, stop=True),
                deps=[o_aor, o_b2k, o_b2l])
    o_aob = add("vector", lambda v: v.tensor_copy(prb[:, :], pA[:, 0:NE]),
                deps=[o_aom])

    o_p2_prev = None
    o_p3_prev = None
    o_scT_prev = []
    o_slotT = {}
    for c in range(TT * B_LOC):
        o_p1 = add("vector", lambda v, c=c: v.tensor_scalar(
            rt0[:, :], iota8[:, :], routesF[:, c:c + 1], None, ALU.is_equal),
            deps=[o_rts[c], o_aob, o_p3_prev])
        o_p2 = add("vector", lambda v: v.tensor_tensor(rt0[:, :], rt0[:, :], prb[:, :],
                                                       ALU.mult), deps=[o_p1])
        o_p3 = add("vector", lambda v: v.tensor_reduce(rt1[:, 0:1], rt0[:, :], AX.X,
                                                       ALU.add), deps=[o_p2])
        o_p3_prev = o_p3
        o_p2_prev = o_p2
        o_p4 = add("vector", lambda v, c=c: v.tensor_scalar(
            rt2[:, :], lpos[:, c:c + 1], rt1[:, 0:1], -1.0, ALU.add, ALU.add),
            deps=[o_p3])
        o_p5 = add("vector", lambda v: v.tensor_copy(slotu[:, :], rt2[:, :]),
                   deps=[o_p4] + o_scT_prev + ([o_aod1, o_aod2] if c == 0 else []))
        o_sc2 = add("gpsimd", lambda g, c=c: g.indirect_dma_start(
            out=idxT[:, :], out_offset=bass.IndirectOffsetOnAxis(ap=slotu[:, :], axis=0),
            in_=idxC[:, 2 * c:2 * c + 2], in_offset=None),
            deps=[o_p5, o_idc, o_pref if c == 0 else None], ind=True)
        o_st2 = add("sync", lambda s, c=c: s.dma_start(
            out=slotT[c:c + 1, :], in_=slotu[:, :]),
            deps=[o_p5], dma=True)
        o_scT_prev = [o_sc2, o_st2]
        o_idx_all.append(o_sc2)
        o_slotT[c] = o_st2
        o_p3_last = o_p3
        o_p2_last = o_p2

    def f_cc(g):
        g.collective_compute("AllGather", ALU.bypass, replica_groups=[list(range(NCORES))],
                             ins=[cc_in_t.ap().opt()], outs=[cc_out_t.ap().opt()]).then_inc(cc_sem)
        g.wait_ge(cc_sem, 1)
        return g.engine_nop()

    o_cc = add("gpsimd", f_cc, deps=[o_ci])
    o_cl = add("gpsimd", lambda g: g.dma_start(out=cnts[:, :], in_=cc_out[:, :]),
               deps=[o_cc], dma=True)
    o_pm = add("tensor", lambda t: t.matmul(pRc[:, :], pmrow[:, :], cnts[:, :],
                                            start=True, stop=True),
               deps=[o_cl, o_route_tail, o_parm_last, o_b2g])
    o_pe = add("vector", lambda v: v.tensor_copy(rowst[0:1, 0:NE], pRc[:, :]),
               deps=[o_pm, o_ps2_last])
    o_pb = add("tensor", lambda t: t.matmul(pR[:, :], ones1f[:, :], rowst[0:1, 0:NE],
                                            start=True, stop=True),
               deps=[o_pe, o_pR_last, o_b2l])
    o_base = add("vector", lambda v: v.tensor_copy(base8[:, :], pR[:, :]), deps=[o_pb])

    o_sprev = o_base
    o_selp = {}
    for b in range(B_LOC):
        for tt in range(TT):
            o_a0 = add("vector", lambda v, b=b, tt=tt: v.tensor_scalar(
                rt0[:, :], iota8[:, :], routesF[:, b * TT + tt:b * TT + tt + 1],
                None, ALU.is_equal), deps=[o_base, o_sprev])
            o_a = add("vector", lambda v: v.tensor_tensor(
                rt0[:, :], rt0[:, :], base8[:, :], ALU.mult), deps=[o_a0])
            o_b2 = add("vector", lambda v: v.tensor_reduce(rt1[:, :], rt0[:, :], AX.X, ALU.add),
                       deps=[o_a])
            o_c2 = add("vector", lambda v, b=b, tt=tt: v.tensor_tensor(
                rt2[:, :], lpos[:, b * TT + tt:b * TT + tt + 1], rt1[:, :], ALU.add),
                deps=[o_b2])
            o_d2 = add("vector", lambda v: v.tensor_scalar(
                rt2[:, :], rt2[:, :], cst[:, 5:6], None, ALU.is_le), deps=[o_c2])
            o_e2 = add("vector", lambda v, b=b, tt=tt: v.tensor_scalar(
                selYb[:, b * TT + tt:b * TT + tt + 1], rt2[:, :],
                selP[:, b * TT + tt:b * TT + tt + 1], None, ALU.mult), deps=[o_d2])
            o_g2 = add("vector", lambda v, b=b, tt=tt: v.scalar_tensor_tensor(
                selZb[:, b * TT + tt:b * TT + tt + 1],
                selYb[:, b * TT + tt:b * TT + tt + 1], negb[:, 0:1],
                selP[:, b * TT + tt:b * TT + tt + 1], ALU.mult, ALU.add), deps=[o_e2])
            o_selp[(b, tt)] = o_g2
            o_sprev = o_g2

    sp_hdt_flat = sp_hdt
    sc_we_flat = sc_we.rearrange("e p k f -> (e p) (k f)")
    gbuf = [kD[:, 0:1024], kD[:, 1024:2048], kD[:, 4096:5120], kD[:, 5120:6144]]
    gXd = [kD[:, 2048:3072], kD[:, 3072:4096], kD[:, 6144:7168], kD[:, 7168:8192]]
    idxS4 = [idxS[0][:, :], idxS[1][:, :], widxU[:, 2:4], widxU[:, 4:6]]
    ybuf = [ev0, ev1]
    wEb = [vD[:, 0:9 * 1024], qD[:, 0:9 * 1024]]
    all_spHT = [o_spHT[(bb, tt)] for bb in range(B_LOC) for tt in range(TT)]
    o_wpv = None
    for g2 in range(NG):
        o_eg1p = add("tensor", lambda t, g2=g2: t.matmul(
            pA[:, 0:1], ones1f[0:1, 0:128], b2row[0:1, 8 + g2:9 + g2],
            start=True, stop=True),
            deps=[o_b2o, o_wpv] + ([o_pe, o_b2l, o_aob] if g2 == 0 else []))
        o_wpv = add("vector", lambda v, g2=g2: v.tensor_scalar(
            widx24F[:, g2:g2 + 1], pA[:, 0:1], 128.0, None, ALU.mult),
            deps=[o_eg1p] + ([o_b2m, o_p3_last, o_p2_last] if g2 == 0 else []))
    o_wadd = add("vector", lambda v: v.tensor_scalar(
        widx24F[:, :], widx24F[:, :], widxP[:, 0:1], None, ALU.add),
        deps=[o_wpv, o_ikp])
    o_wcast = add("vector", lambda v: v.tensor_copy(widx24U[:, :], widx24F[:, :]),
                  deps=[o_wadd])
    o_g_prev = [None, None, None, None]
    o_tr_prev = [None, None, None, None]
    o_mm_prev4 = [None, None, None, None]
    o_y_prev = [None, None]
    o_yst_prev = [None, None]
    o_pTe = None
    pTrot = [pT, bank6[:, 0:64].bitcast(BF16),
             ps[2][:, 0:64].bitcast(BF16), ps[3][:, 0:64].bitcast(BF16)]
    o_pTe_rot = [None, None, None, None]
    o_mm_prev = [None, None]
    o_wread = [None, None]
    o_wg_last = [None, None]
    o_eg_prev = []
    o_il_prev = [None, None]
    o_yst_all = []
    for g in range(NG):
        wp = g % 2
        o_wg = add("gpsimd", lambda gg, wp=wp, g=g: gg.indirect_dma_start(
            out=wEb[wp][:, 0:9 * 1024], out_offset=None,
            in_=sc_we_flat,
            in_offset=bass.IndirectOffsetOnAxis(ap=widx24U[:, g:g + 1], axis=0)),
            deps=[o_wcast, o_wread[wp]] +
                 ((last_vD if wp == 0 else last_qD) if g < 2 else []), ind=True)
        o_wg_last[wp] = o_wg
        o_bg = o_wg
        for t4 in range(4):
            s4 = g * 4 + t4
            par = s4 % 4
            par2 = t4 % 2
            r0 = g * GR + t4 * 128
            o_il = add("gpsimd", lambda gg, r0=r0, par=par: gg.dma_start(
                out=idxS4[par],
                in_=idxT[r0:r0 + 128, :]),
                deps=[o_il_prev[0], o_g_prev[par]] +
                     (o_idx_all if g == 0 and t4 == 0 else []), dma=True)
            o_il_prev[0] = o_il
            o_g = add("gpsimd", lambda gg, par=par: gg.indirect_dma_start(
                out=gbuf[par], out_offset=None,
                in_=sp_hdt_flat,
                in_offset=bass.IndirectOffsetOnAxis(ap=idxS4[par][:, 0:1], axis=0)),
                deps=[o_il, o_tr_prev[par]] +
                     (all_spHT + o_zlist if g == 0 else []), ind=True)
            o_g_prev[par] = o_g
            o_trs = None
            for k in range(KT):
                pTk = pTrot[k % 4]
                o_trk = add("tensor", lambda t, par=par, k=k, pTk=pTk: t.transpose(
                    pTk, gbuf[par][:, k * 128:(k + 1) * 128], ident[:, :]),
                    deps=[o_g, o_pTe_rot[k % 4]] +
                         ([o_pCe, o_pDe, o_b6_war] if g == 0 and t4 == 0 else []))
                o_ev = add("vector", lambda v, par=par, k=k, pTk=pTk: v.tensor_copy(
                    gXd[par][:, k * 128:(k + 1) * 128], pTk),
                    deps=[o_trk, o_mm_prev4[par] if k == 0 else None])
                o_pTe_rot[k % 4] = o_ev
                o_pTe = o_ev
                o_trs = o_ev
            o_tr_prev[par] = o_trs
            for fc in range(2):
                pb = ps[fc]
                o_mm = add("tensor", lambda t, par=par, fc=fc, pb=pb, wp=wp: [
                    t.matmul(pb[:, :], gXd[par][:, k * 128:(k + 1) * 128],
                             wEb[wp][:, k * D + fc * 512:k * D + fc * 512 + 512],
                             start=(k == 0), stop=(k == KT - 1))
                    for k in range(KT)][-1],
                    deps=[o_trs, o_wg_last[wp], o_y_prev[fc]] +
                         ([o_bm_prev, o_pR_last] if g == 0 and t4 == 0 else []))
                o_mm_prev4[par] = o_mm
                o_wread[wp] = o_mm
                o_ye = add("vector", lambda v, par2=par2, fc=fc, pb=pb, wp=wp: v.tensor_tensor(
                    ybuf[par2][:, fc * 512:(fc + 1) * 512], pb[:, :],
                    wEb[wp][:, 8 * 1024 + fc * 512:8 * 1024 + (fc + 1) * 512], ALU.add),
                    deps=[o_mm, o_bg, o_yst_prev[par2]] +
                         ([o_spA.get((B_LOC - 1, TT - 2)), o_spA.get((B_LOC - 1, TT - 1))]
                          if g == 0 and t4 < 2 else []))
                o_y_prev[fc] = o_ye
            o_yst = add("sync", lambda s, r0=r0, par2=par2: s.dma_start(
                out=yP[r0:r0 + 128, :], in_=ybuf[par2][:, :]),
                deps=[o_y_prev[0], o_y_prev[1]], dma=True)
            o_yst_prev[par2] = o_yst
            o_yst_all.append(o_yst)

    ybufs = [ev0[:, :], ev1[:, :],
             rmsq[:, 0:2048].bitcast(F32), rmsq[:, 2048:4096].bitcast(F32)]
    gbufs = [kD[:, 0:1024], kD[:, 1024:2048], oT[:, 0:1024], oT[:, 1024:2048]]
    xtbs = [xtb[0][:, :], xtb[1][:, :], slabT[:, :].bitcast(F32)]
    slotSs = [idxS[0][:, 0:1], idxS[1][:, 0:1], widxU[:, 0:1], widxU[:, 1:2]]
    NT = B_LOC * TT
    o_st_d = {}
    o_y2_d = {}
    o_yl_d = {}
    o_xl_d = {}
    o_hdl_d = {}
    o_sll_d = {}
    o_out_final = []
    o_prev_store = None
    o_prev_ln = None
    o_hdl_prev = None
    o_yl_prev = None
    o_xl_prev = None
    o_sll_prev = None
    o_al_prev = None

    def em_sll_yl(k):
        global_unused = None
        b, tt = divmod(k, TT)
        p4 = k % 4
        sS = slotSs[p4]
        yb = ybufs[p4]
        o_sll = add("sync", lambda s, k=k, sS=sS: s.dma_start(
            out=sS, in_=slotT[k:k + 1, :]),
            deps=[o_slotT[k], o_yl_d.get(k - 4), o_sll_d.get(k - 1)] +
                 ([o_g_prev[p4]] if k < 2 else []) +
                 ([o_wg_last[0], o_wg_last[1]] if k in (2, 3) else []),
            dma=True, key="sync_s")
        o_sll_d[k] = o_sll
        o_yl = add("gpsimd", lambda g, yb=yb, sS=sS: g.indirect_dma_start(
            out=yb, out_offset=None, in_=yP,
            in_offset=bass.IndirectOffsetOnAxis(ap=sS, axis=0)),
            deps=[o_sll, o_st_d.get(k - 4), o_yl_d.get(k - 1)] + o_yst_all[-1:] +
                 ([o_y_prev[0], o_y_prev[1]] if k < 2 else []) +
                 (o_vt_all if k in (2, 3) else []), ind=True)
        o_yl_d[k] = o_yl

    def em_hdl(k):
        b, tt = divmod(k, TT)
        p4 = k % 4
        gb = gbufs[p4]
        o_hdl = add("gpsimd", lambda g, b=b, tt=tt, gb=gb: g.dma_start(
            out=gb, in_=sp_hdt[b * T + tt * 128:b * T + (tt + 1) * 128, :]),
            deps=[o_hdl_d.get(k - 1), o_y2_d.get(k - 4)] +
                 ([o_tr_prev[p4], o_g_prev[p4]] if k < 2 else []) +
                 (o_vt_all if k in (2, 3) else []),
            dma=True, key="g_hd")
        o_hdl_d[k] = o_hdl

    def em_xl(k):
        b, tt = divmod(k, TT)
        p3 = k % 3
        xb = xtbs[p3]
        o_xl = add("sync", lambda s, b=b, tt=tt, xb=xb: s.dma_start(
            out=xb, in_=x[b, tt * 128:(tt + 1) * 128, :]),
            deps=[o_st_d.get(k - 3), o_xl_d.get(k - 1)] +
                 ([o_y_prev[0], o_y_prev[1]] if k < 3 else []),
            dma=True, key="sync_x")
        o_xl_d[k] = o_xl

    for k in range(3):
        em_hdl(k)
        em_sll_yl(k)
    for k in range(2):
        em_xl(k)
    for c in range(NT):
        if c + 3 < NT:
            em_hdl(c + 3)
            em_sll_yl(c + 3)
        if c + 2 < NT:
            em_xl(c + 2)
        b, tt = divmod(c, TT)
        p4 = c % 4
        p3 = c % 3
        yb = ybufs[p4]
        gb = gbufs[p4]
        xb = xtbs[p3]
        o_xl = o_xl_d[c]
        o_hdl = o_hdl_d[c]
        o_yl = o_yl_d[c]
        mcol = mrs[:, c * 2:c * 2 + 1]
        rcol = mrs[:, c * 2 + 1:c * 2 + 2]
        sycol = selYb[:, c:c + 1]
        szcol = selZb[:, c:c + 1]
        o_xn = add("vector", lambda v, xb=xb, mcol=mcol, rcol=rcol: v.tensor_scalar(
            cvo[0][:, :], xb[:, 0:512], mcol, rcol,
            ALU.subtract, ALU.mult), deps=[o_xl, o_prev_ln])
        o_xn2 = add("vector", lambda v, xb=xb, mcol=mcol, rcol=rcol: v.tensor_scalar(
            cvo[1][:, :], xb[:, 512:1024], mcol, rcol,
            ALU.subtract, ALU.mult), deps=[o_xl, o_xn])
        o_y1 = add("vector", lambda v, yb=yb, sycol=sycol: v.tensor_scalar(
            yb, yb, sycol, None, ALU.mult),
            deps=[o_yl, o_selp[(b, tt)]])
        o_y2 = add("vector", lambda v, yb=yb, gb=gb, szcol=szcol: v.scalar_tensor_tensor(
            yb, gb, szcol, yb, ALU.mult, ALU.add), deps=[o_y1, o_hdl])
        o_y2_d[c] = o_y2
        o_x2 = add("vector", lambda v, yb=yb: v.tensor_tensor(
            yb[:, 0:512], yb[:, 0:512], cvo[0][:, :], ALU.add),
            deps=[o_y2, o_xn])
        o_x2b = add("vector", lambda v, yb=yb: v.tensor_tensor(
            yb[:, 512:1024], yb[:, 512:1024], cvo[1][:, :], ALU.add),
            deps=[o_x2, o_xn2])
        o_b1 = add("vector", lambda v, yb=yb: v.bn_stats(bnb[:, 0:6], yb[:, 0:512]),
                   deps=[o_x2b])
        o_b22 = add("vector", lambda v, yb=yb: v.bn_stats(bnb[:, 6:12], yb[:, 512:1024]),
                    deps=[o_b1])
        o_b3 = add("vector", lambda v: v.bn_aggr(bnm[:, 0:2], bnb[:, 0:12]), deps=[o_b22])
        o_b4 = add("scalar", lambda sc: sc.activation(bnb[:, 2:3], bnm[:, 1:2],
                                                      AF.Ln, bias=cst[:, 0:1]), deps=[o_b3])
        o_b5 = add("scalar", lambda sc: sc.activation(bnb[:, 3:4], bnb[:, 2:3], AF.Exp,
                                                      scale=-0.5), deps=[o_b4])
        o_n1 = add("vector", lambda v, yb=yb: v.tensor_scalar(
            yb, yb, bnm[:, 0:1], bnb[:, 3:4],
            ALU.subtract, ALU.mult), deps=[o_b5])
        o_n2 = add("vector", lambda v, yb=yb: v.tensor_tensor(
            yb, yb, gamB[:, :], ALU.mult), deps=[o_n1])
        o_n3 = add("vector", lambda v, yb=yb: v.tensor_tensor(
            yb, yb, betB[:, :], ALU.add), deps=[o_n2])
        o_n4 = add("vector", lambda v, yb=yb, xb=xb: v.tensor_tensor(
            yb, yb, xb, ALU.add), deps=[o_n3])
        o_al = add("gpsimd", lambda g, b=b, tt=tt, yb=yb: g.dma_start(
            out=yb, in_=sp_attn[b, tt * 128:(tt + 1) * 128, :],
            accum_op=ALU.add), deps=[o_n4, o_al_prev], dma=True, key="g_al")
        o_al_prev = o_al
        o_prev_ln = o_n4
        o_st = add("sync", lambda s, b=b, tt=tt, yb=yb: s.dma_start(
            out=out[b, tt * 128:(tt + 1) * 128, :], in_=yb),
            deps=[o_al, o_prev_store], dma=True)
        o_prev_store = o_st
        o_st_d[c] = o_st
        o_out_final.append(o_st)

    add("gpsimd", lambda g: g.engine_nop(), deps=o_out_final)
    add("vector", lambda v: v.engine_nop(), deps=o_out_final)
    add("sync", lambda s: s.nop(), deps=o_out_final)
    add("tensor", lambda t: t.nop(), deps=o_out_final)
    add("scalar", lambda sc: sc.nop(), deps=o_out_final)

    S.emit()
    return nc


_NC_CACHE = [None]


def kernel(**inputs):
    def f(name):
        return np.ascontiguousarray(np.asarray(inputs[name], dtype=np.float32))

    xs = f("L0_attn_hidden_states")
    base = {
        "wq": f("L0_attn_Wq"), "wk": f("L0_attn_Wk"), "wv": f("L0_attn_Wv"),
        "wo": f("L0_attn_Wo"), "cwq": f("L0_attn_conv_wq"), "cwk": f("L0_attn_conv_wk"),
        "cwv": f("L0_attn_conv_wv"), "onw": f("L0_attn_onorm_w"),
        "fwq": f("L0_ffn_Wq").reshape(D, D), "fwk": f("L0_ffn_Wk").reshape(D, D),
        "fwv": f("L0_ffn_Wv").reshape(D, D),
        "fbq": f("L0_ffn_bq").reshape(D), "fbk": f("L0_ffn_bk").reshape(D),
        "fbv": f("L0_ffn_bv").reshape(D),
        "fwe": f("L0_ffn_We"), "fbe": f("L0_ffn_be"), "fwsw": f("L0_ffn_Wsw"),
        "fbsw": f("L0_ffn_bsw"), "fgam": f("L0_ffn_gamma"), "fbet": f("L0_ffn_beta"),
    }
    in_maps = []
    for c in range(NCORES):
        m = dict(base)
        m["x"] = xs[c * B_LOC:(c + 1) * B_LOC]
        pm = np.zeros((NE, 1), dtype=np.float32)
        pm[:c, 0] = 1.0
        m["pmask"] = pm
        in_maps.append(m)
    if _NC_CACHE[0] is None:
        _NC_CACHE[0] = build_program()
    res = run_bass_kernel_spmd(_NC_CACHE[0], in_maps, core_ids=list(range(NCORES)))
    outs = [np.asarray(r["out"]) for r in res.results]
    return np.concatenate(outs, axis=0)


# version 38
# speedup vs baseline: 1.1851x; 1.0054x over previous
import numpy as np
from contextlib import ExitStack

import concourse.bass as bass
import concourse.mybir as mybir
from concourse.bass_utils import run_bass_kernel_spmd

F32 = mybir.dt.float32
F32R = mybir.dt.float32r
BF16 = mybir.dt.bfloat16
I8 = mybir.dt.int8
AF = mybir.ActivationFunctionType
ALU = mybir.AluOpType
AX = mybir.AxisListType

NCORES = 8
B_LOC = 4
T = 2048
D = 1024
KT = D // 128
TT = T // 128
NQ = T // 512
H_DN = 4
DH = 256
NCH = 16
NE = 8
CAP = 9830
GR = 512
NG = 24
NSLOT = NG * GR
EPS_LN = 1e-5
EPS_L2 = 1e-6
EPS_RMS = 1e-5
LN16 = 2.772588722239781
DEBUG_SPILLS = False

ENGINES = ("sync", "scalar", "tensor", "vector", "gpsimd")


class Op:
    __slots__ = ("idx", "eng", "fn", "deps", "dma", "need_inc", "val", "key")

    def __init__(self, idx, eng, fn, deps, dma, ind, key):
        self.idx = idx
        self.eng = eng
        self.fn = fn
        self.deps = deps
        self.dma = dma
        self.need_inc = False
        self.val = None
        self.key = key if key else eng + ("_i" if ind else "_d" if dma else "")


class Sched:
    def __init__(self, nc, ctx):
        self.nc = nc
        self.ctx = ctx
        self.ops = []

    def add(self, eng, fn, deps=(), dma=False, ind=False, key=None):
        flat = []
        for d in deps:
            if d is None:
                continue
            if isinstance(d, (list, tuple)):
                flat.extend(q for q in d if q is not None)
            else:
                flat.append(d)
        o = Op(len(self.ops), eng, fn, flat, dma or ind, ind, key)
        self.ops.append(o)
        return o

    def emit(self):
        for o in self.ops:
            if o.dma:
                o.need_inc = True
            for d in o.deps:
                assert d.idx < o.idx, "dependency must precede op"
                d.need_inc = True
        counters = {}
        for o in self.ops:
            if o.need_inc:
                counters[o.key] = counters.get(o.key, 0) + (16 if o.dma else 1)
                o.val = counters[o.key]
        sems = {k: self.ctx.enter_context(self.nc.semaphore("sem_" + k)) for k in counters}
        per_eng = {e: [o for o in self.ops if o.eng == e] for e in ENGINES}

        def stream(eng_name, eng):
            for o in per_eng[eng_name]:
                waits = {}
                for d in o.deps:
                    waits[d.key] = max(waits.get(d.key, 0), d.val)
                for key, v in waits.items():
                    eng.wait_ge(sems[key], v)
                r = o.fn(eng)
                if o.need_inc:
                    instrs = r if isinstance(r, (list, tuple)) else [r]
                    if o.dma:
                        assert len(instrs) == 1
                        instrs[0].then_inc(sems[o.key], 16)
                    else:
                        instrs[-1].then_inc(sems[o.key], 1)

        with self.nc.Block() as blk:
            for name in ENGINES:
                getattr(blk, name)(lambda e, n=name: stream(n, e))


def build_program():
    nc = bass.Bass("TRN2", target_bir_lowering=False)
    ctx = ExitStack()
    ap = lambda t: t.ap()

    x = ap(nc.dram_tensor("x", [B_LOC, T, D], F32, kind="ExternalInput"))
    wq = ap(nc.dram_tensor("wq", [D, D], F32, kind="ExternalInput"))
    wk = ap(nc.dram_tensor("wk", [D, D], F32, kind="ExternalInput"))
    wv = ap(nc.dram_tensor("wv", [D, D], F32, kind="ExternalInput"))
    wo = ap(nc.dram_tensor("wo", [D, D], F32, kind="ExternalInput"))
    cwq = ap(nc.dram_tensor("cwq", [D, 4], F32, kind="ExternalInput"))
    cwk = ap(nc.dram_tensor("cwk", [D, 4], F32, kind="ExternalInput"))
    cwv = ap(nc.dram_tensor("cwv", [D, 4], F32, kind="ExternalInput"))
    onw = ap(nc.dram_tensor("onw", [DH], F32, kind="ExternalInput"))
    fwq = ap(nc.dram_tensor("fwq", [D, D], F32, kind="ExternalInput"))
    fwk = ap(nc.dram_tensor("fwk", [D, D], F32, kind="ExternalInput"))
    fwv = ap(nc.dram_tensor("fwv", [D, D], F32, kind="ExternalInput"))
    fbq = ap(nc.dram_tensor("fbq", [D], F32, kind="ExternalInput"))
    fbk = ap(nc.dram_tensor("fbk", [D], F32, kind="ExternalInput"))
    fbv = ap(nc.dram_tensor("fbv", [D], F32, kind="ExternalInput"))
    fwe = ap(nc.dram_tensor("fwe", [NE, D, D], F32, kind="ExternalInput"))
    fbe = ap(nc.dram_tensor("fbe", [NE, D], F32, kind="ExternalInput"))
    fwsw = ap(nc.dram_tensor("fwsw", [NE, D], F32, kind="ExternalInput"))
    fbsw = ap(nc.dram_tensor("fbsw", [NE], F32, kind="ExternalInput"))
    fgam = ap(nc.dram_tensor("fgam", [D], F32, kind="ExternalInput"))
    fbet = ap(nc.dram_tensor("fbet", [D], F32, kind="ExternalInput"))
    pmask = ap(nc.dram_tensor("pmask", [NE, 1], F32, kind="ExternalInput"))
    out = ap(nc.dram_tensor("out", [B_LOC, T, D], F32, kind="ExternalOutput"))

    dbg = dict(kind="ExternalOutput") if DEBUG_SPILLS else {}
    sc_w = [ap(nc.dram_tensor(n, [D, D], BF16)) for n in ("scfwq", "scfwk", "scfwv")]
    sc_we = ap(nc.dram_tensor("scfwe", [NE, 128, 9, 1024], BF16))
    sp_attn = ap(nc.dram_tensor("spattn", [B_LOC, T, D], F32, **dbg))
    sp_hd = ap(nc.dram_tensor("sphd", [B_LOC, D, T], BF16, **dbg))
    sp_hdt = ap(nc.dram_tensor("sphdt", [B_LOC * T + 128, D], BF16, **dbg))
    denM = ap(nc.dram_tensor("denM", [16, T], F32))
    if DEBUG_SPILLS:
        dbgq = ap(nc.dram_tensor("dbgq", [128, KT * T], BF16, kind="ExternalOutput"))
        dbgk = ap(nc.dram_tensor("dbgk", [128, KT * T], BF16, kind="ExternalOutput"))
        dbgv = ap(nc.dram_tensor("dbgv", [128, KT * T], BF16, kind="ExternalOutput"))
        dbgo = ap(nc.dram_tensor("dbgo", [128, TT * DH], BF16, kind="ExternalOutput"))
        dbgon = ap(nc.dram_tensor("dbgon", [128, TT * DH], BF16, kind="ExternalOutput"))
        dbgx = ap(nc.dram_tensor("dbgx", [128, KT * T], BF16, kind="ExternalOutput"))
        dbgs = ap(nc.dram_tensor("dbgs", [128, 512], BF16, kind="ExternalOutput"))
        dbgc = ap(nc.dram_tensor("dbgc", [128, 512], F32, kind="ExternalOutput"))
    idxT = ap(nc.dram_tensor("idxT", [NSLOT, 2], mybir.dt.uint32))
    yP = ap(nc.dram_tensor("yP", [NSLOT, D], F32))
    slotT = ap(nc.dram_tensor("slotT", [TT * B_LOC, 128], mybir.dt.uint32))
    aoD = ap(nc.dram_tensor("aoD", [NE, 2], mybir.dt.uint32))
    cc_in_t = nc.dram_tensor("ccin", [1, NE], F32)
    cc_out_t = nc.dram_tensor("ccout", [NCORES, NE], F32)
    cc_in = ap(cc_in_t)
    cc_out = ap(cc_out_t)

    sb = lambda name, shape, dt: ctx.enter_context(nc.sbuf_tensor(name, shape, dt))
    xnD = sb("xnD", [128, KT * T], BF16)
    qD = sb("qD", [128, KT * T], BF16)
    kD = sb("kD", [128, KT * T], BF16)
    vD = sb("vD", [128, KT * T], BF16)
    stg = [sb(f"stg{i}", [128, 515], BF16) for i in range(2)]
    cvo = [sb(f"cvo{i}", [128, 512], F32) for i in range(2)]
    denQ = cvo[0][0:16, 0:512]
    xtb = [sb(f"xtb{i}", [128, D], F32) for i in range(2)]

    bnb = sb("bnb", [128, 12], F32)
    bnm = sb("bnm", [128, 2], F32)
    mrs = sb("mrs", [128, 2 * TT * B_LOC], F32)
    Sb1 = sb("Sb1", [128, 2 * DH], BF16)
    oT = sb("oT", [128, TT * DH], BF16)
    rmsq = sb("rmsq", [128, TT * DH], BF16)
    rmsr = sb("rmsr", [128, TT], F32)
    ev0 = sb("ev0", [128, D], F32)
    ev1 = sb("ev1", [128, D], F32)
    QpT = [sb(f"QpT{j}", [128, 128], F32R) for j in range(6)]
    QtT = [sb(f"QtT{j}", [128, 128], F32R) for j in range(7)]
    CabT = [sb(f"CabT{i}", [128, DH], F32R) for i in range(2)]
    QpR = [t[:, :] for t in QpT]
    QtR = [t[:, :] for t in QtT]
    CabR = [t[:, :] for t in CabT]
    CabF = [t[:, :].bitcast(F32) for t in CabT]
    S1 = xtb[1][:, 0:512]
    vaug = [sb(f"vaug{i}", [128, TT * 65], BF16) for i in range(2)]
    wbufA = [vaug[0], vaug[1]]
    slabT = sb("slabT", [128, 2048], BF16)
    slab = [slabT[:, i * 512:(i + 1) * 512] for i in range(4)]
    xnT = [slabT[:, 0:1024], slabT[:, 1024:2048]]
    KTt = slabT[:, 1024:1280]
    VTt = slabT[:, 1280:1536]
    Usb = vaug[0][:, 0:256]
    Gm = vaug[0][:, 256:384]
    tsum = sb("tsum", [65, 2 * TT], F32)
    sufc = sb("sufc", [65, 2 * NQ], F32)
    denr = [sb(f"denr{i}", [65, 512], F32) for i in range(2)]
    rowst = denr[1][0:1, 0:512]
    oddst = [sb(f"odd{i}", [64, 512], BF16) for i in range(2)]
    beS = [oddst[0][0:1, 0:512], oddst[1][0:1, 0:512]]
    lgt = sb("lgt", [128, NE], F32)
    prb = sb("prb", [128, NE], F32)
    rt0 = sb("rt0", [128, NE], F32)
    rt1 = sb("rt1", [128, 1], F32)
    rt2 = sb("rt2", [128, 1], F32)
    rt3 = sb("rt3", [128, 1], F32)
    oneh8 = sb("oneh8", [128, NE], BF16)
    selP = sb("selP", [128, TT * B_LOC], F32)
    lpos = sb("lpos", [128, TT * B_LOC], F32)
    carry = sb("carry", [1, NE], F32)
    base8 = sb("base8", [128, NE], F32)
    selZb = sb("selZb", [128, TT * B_LOC], F32)
    selYb = sb("selYb", [128, TT * B_LOC], F32)
    slotu = sb("slotu", [128, 1], mybir.dt.uint32)
    idxC = sb("idxC", [128, 2 * TT * B_LOC], mybir.dt.uint32)
    idxS = [sb(f"idxS{i}", [128, 2], mybir.dt.uint32) for i in range(2)]
    oobU = sb("oobU", [128, 16], mybir.dt.uint32)
    routesF = sb("routesF", [128, TT * B_LOC], F32)
    widxP = sb("widxP", [128, 1], F32)
    egsb = sb("egsb", [128, NE], F32)
    widxF = egsb
    widxU = sb("widxU", [128, KT], mybir.dt.uint32)
    egcol = sb("egcol", [128, 1], F32)
    b2row = sb("b2row", [1, 48], F32)
    b2cb = sb("b2cb", [128, 1], BF16)
    egbU = sb("egbU", [128, 1], mybir.dt.uint32)
    widx24F = sb("widx24F", [128, 24], F32)
    widx24U = sb("widx24U", [128, 24], mybir.dt.uint32)
    slotS = idxS
    aosel = widxU[:, 0:2]
    ridxU = sb("ridxU", [128, 1], mybir.dt.uint32)
    cnts = sb("cnts", [NCORES, NE], F32)
    pmrow = sb("pmrow", [NE, 1], F32)
    ident = sb("ident", [128, 128], BF16)
    m_ltn = sb("m_ltn", [128, 128], BF16)
    m_utn = sb("m_utn", [128, 128], BF16)
    m_uti = sb("m_uti", [128, 128], BF16)
    m_fut = sb("m_fut", [128, 128], I8)
    onesb = sb("onesb", [128, 128], BF16)
    negb = sb("negb", [128, 128], F32)
    ones1 = sb("ones1", [1, 128], BF16)
    ones1f = sb("ones1f", [1, 128], F32)
    onescol = sb("onescol", [128, 1], BF16)
    iota8 = sb("iota8", [128, NE], F32)
    cst = sb("cst", [128, 10], F32)
    convw = sb("convw", [128, 3 * KT * 4], F32)
    onwB = sb("onwB", [128, DH], F32)
    gamB = sb("gamB", [128, D], F32)
    betB = sb("betB", [128, D], F32)
    bvB = sb("bvB", [128, D], BF16)
    bqc = sb("bqc", [128, KT], F32)
    bkc = sb("bkc", [128, KT], F32)
    wswT = sb("wswT", [128, KT * NE], BF16)
    bswB = sb("bswB", [128, NE], F32)

    psf = lambda name, shape: ctx.enter_context(nc.psum_tensor(name, shape, F32))
    ps = [psf(f"ps{i}", [128, 512]) for i in range(4)]
    bank4 = psf("bank4", [128, 512])
    bank5 = psf("bank5", [128, 512])
    bank6 = psf("bank6", [128, 512])
    bank7 = psf("bank7", [128, 512])
    pA = bank4[:, 0:128]
    pR = bank4[:, 128:136]
    pRc = bank4[0:1, 136:144]
    pB = bank5[:, 0:128]
    pL = bank5[0:1, 128:512]
    pL = bank5[0:1, 0:512]
    pC = bank6[:, 0:DH]
    pD = bank7[:, 0:DH]
    pT = bank7[:, 0:64].bitcast(BF16)
    cc_sem = ctx.enter_context(nc.semaphore("cc_sem"))

    S = Sched(nc, ctx)
    add = S.add

    def f_memsets(v):
        v.memset(onesb[:, :], 1.0)
        v.memset(ones1[:, :], 1.0)
        v.memset(ones1f[:, :], 1.0)
        v.memset(onescol[:, :], 1.0)
        v.memset(negb[:, :], -1.0)
        v.memset(cst[:, 0:1], EPS_LN)
        v.memset(cst[:, 1:2], EPS_L2)
        v.memset(cst[:, 2:3], EPS_RMS)
        v.memset(cst[:, 3:4], -1000.0)
        v.memset(cst[:, 4:5], 1.0)
        v.memset(cst[:, 5:6], float(CAP))
        v.memset(cst[:, 6:7], -LN16)
        v.memset(cst[:, 7:8], 0.0)
        v.memset(cst[:, 8:9], float(GR))
        v.memset(cst[:, 9:10], 65535.0)
        v.memset(carry[:, :], 0.0)
        return v.memset(cst[:, 7:8], 0.0)

    o_ms = add("vector", f_memsets)

    def f_consts(g):
        g.iota(iota8[:, :], [[1, NE]], base=0, channel_multiplier=0,
               allow_small_or_imprecise_dtypes=True)
        g.affine_select(ident[:, :], onesb[:, :], [[-1, 128]], ALU.is_equal, 0.0,
                        base=0, channel_multiplier=1)
        g.affine_select(m_uti[:, :], onesb[:, :], [[1, 128]], ALU.is_ge, 0.0,
                        base=0, channel_multiplier=-1)
        g.affine_select(m_ltn[:, :], negb[:, :], [[-1, 128]], ALU.is_gt, 0.0,
                        base=0, channel_multiplier=1)
        r = g.affine_select(m_utn[:, :], negb[:, :], [[1, 128]], ALU.is_gt, 0.0,
                            base=0, channel_multiplier=-1)
        return r

    o_cst = add("gpsimd", f_consts, deps=[o_ms])
    o_idc = add("gpsimd", lambda g: g.iota(idxC[:, :], [[128, TT * B_LOC], [0, 2]], base=0,
                                           channel_multiplier=1), deps=[o_ms])
    o_oob = add("gpsimd", lambda g: g.iota(oobU[:, :], [[0, 16]], base=B_LOC * T,
                                           channel_multiplier=1), deps=[o_ms])
    o_ikp = add("gpsimd", lambda g: g.iota(widxP[:, :], [[0, 1]], base=0,
                                           channel_multiplier=1,
                                           allow_small_or_imprecise_dtypes=True), deps=[o_ms])
    idxTv = idxT[:, :].rearrange("(a b) c -> b a c", b=128)
    o_pref = None
    for pi in range(12):
        o_pref = add("gpsimd", lambda g, pi=pi: g.dma_start(
            out=idxTv[:, pi * 8:(pi + 1) * 8, :],
            in_=oobU[:, :].rearrange("p (a c) -> p a c", c=2),
            allow_slow_non_contiguous=True), deps=[o_oob, o_pref], dma=True)
    o_z0 = add("vector", lambda v: v.memset(ev0[:, :], 0.0), deps=[])
    o_zhd = add("gpsimd", lambda g: g.dma_start(
        out=sp_hdt[B_LOC * T:B_LOC * T + 128, :], in_=ev0[:, :]), deps=[o_z0], dma=True)
    o_zlist = [o_zhd]
    o_f0 = add("gpsimd", lambda g: g.memset(m_fut[:, :], 1), deps=[o_cst])
    o_cst = add("gpsimd", lambda g: g.affine_select(
        m_fut[:, :], m_fut[:, :], [[-1, 128]], ALU.is_gt, 0, base=0, channel_multiplier=1),
        deps=[o_f0])

    for si, cw in enumerate([cwq, cwk, cwv]):
        for k in range(KT):
            add("gpsimd", lambda g, si=si, k=k, cw=cw: g.dma_start(
                out=convw[:, (si * KT + k) * 4:(si * KT + k) * 4 + 4],
                in_=cw[k * 128:(k + 1) * 128, :]), dma=True)
    for k in range(KT):
        add("gpsimd", lambda g, k=k: g.dma_start(
            out=bqc[:, k:k + 1], in_=fbq[k * 128:(k + 1) * 128].rearrange("(a b) -> a b", b=1)), dma=True)
        add("gpsimd", lambda g, k=k: g.dma_start(
            out=bkc[:, k:k + 1], in_=fbk[k * 128:(k + 1) * 128].rearrange("(a b) -> a b", b=1)), dma=True)
    for k in range(KT):
        add("gpsimd", lambda g, k=k: g.dma_start(
            out=wswT[:, k * NE:(k + 1) * NE],
            in_=fwsw.rearrange("e d -> d e")[k * 128:(k + 1) * 128, :],
            allow_slow_non_contiguous=True), dma=True)
    o_parm_last = add("gpsimd", lambda g: g.dma_start(out=pmrow[:, :], in_=pmask[:, :]), dma=True)

    o_prev = o_ms
    for srcv, dst, n in [(onw, onwB, DH), (fgam, gamB, D), (fbet, betB, D),
                         (fbv, bvB, D), (fbsw, bswB, NE)]:
        for cc in range(0, n, 512):
            w = min(512, n - cc)
            o_row = add("gpsimd", lambda g, srcv=srcv, cc=cc, w=w: g.dma_start(
                out=rowst[0:1, 0:w], in_=srcv[cc:cc + w].rearrange("(b a) -> b a", b=1)),
                deps=[o_prev], dma=True)
            o_m = add("tensor", lambda t, w=w: t.matmul(
                ps[0][:, 0:w], ones1f[:, :], rowst[0:1, 0:w], start=True, stop=True),
                deps=[o_row, o_prev])
            o_prev = add("vector", lambda v, dst=dst, cc=cc, w=w: v.tensor_copy(
                dst[:, cc:cc + w], ps[0][:, 0:w]), deps=[o_m])
    o_bc_done = o_prev

    o_bstage_prev = None
    p0rot = [pT, bank6[:, 0:64].bitcast(BF16),
             ps[2][:, 0:64].bitcast(BF16), ps[3][:, 0:64].bitcast(BF16),
             bank4[:, 0:64].bitcast(BF16), bank5[:, 0:64].bitcast(BF16),
             ps[0][:, 0:64].bitcast(BF16), ps[1][:, 0:64].bitcast(BF16)]
    o_p0rot = [None] * 8
    o_prev_ld = [None, None]
    o_prev_ev = None
    o_st_by_dk = {}
    o_prev_stores = []
    for src, dst, e in ([(fwq, sc_w[0], None), (fwk, sc_w[1], None), (fwv, sc_w[2], None)] +
                        [(fwe, sc_we, ee) for ee in range(NE)]):
        o_evs = []
        evs_by_dk = {dk: [] for dk in range(8)}
        for r in range(8):
            inap = src[r * 128:(r + 1) * 128, :] if e is None else src[e, r * 128:(r + 1) * 128, :]
            o_ld = add("gpsimd", lambda g, r=r, inap=inap: g.dma_start(
                out=wbufA[r % 2][:, 0:1024], in_=inap), deps=[o_prev_ld[r % 2]], dma=True)
            for dk in range(8):
                rk = (r * 8 + dk) % 8
                prt = p0rot[rk]
                o_tr = add("tensor", lambda t, r=r, dk=dk, prt=prt: t.transpose(
                    prt, wbufA[r % 2][:, dk * 128:(dk + 1) * 128], ident[:, :]),
                    deps=[o_ld, o_cst, o_p0rot[rk]] +
                         ([o_bc_done] if rk >= 4 and o_p0rot[rk] is None else []))
                o_prev_ev = add("vector", lambda v, r=r, dk=dk, prt=prt: v.tensor_copy(
                    kD[:, dk * 1024 + r * 128:dk * 1024 + (r + 1) * 128], prt),
                    deps=[o_tr, o_st_by_dk.get(dk)])
                o_p0rot[rk] = o_prev_ev
                o_evs.append(o_prev_ev)
                evs_by_dk[dk].append(o_prev_ev)
            o_prev_ld[r % 2] = o_evs[-1]
        o_prev_stores = []
        for dk in range(8):
            if e is None:
                outap = dst[dk * 128:(dk + 1) * 128, :]
            else:
                outap = dst[e, :, dk, :]
            o_stk = add("gpsimd", lambda g, outap=outap, dk=dk: g.dma_start(
                out=outap, in_=kD[:, dk * 1024:(dk + 1) * 1024]), deps=[evs_by_dk[dk]],
                dma=True)
            o_st_by_dk[dk] = o_stk
            o_prev_stores.append(o_stk)
        if e is not None:
            o_br1 = add("gpsimd", lambda g, e=e: g.dma_start(
                out=rowst[0:1, 0:512], in_=fbe[e, 0:512].rearrange("(b a) -> b a", b=1)),
                deps=[o_bc_done, o_bstage_prev], dma=True)
            o_bm1 = add("tensor", lambda t: t.matmul(ps[0][:, 0:512], ones1f[:, :],
                                                     rowst[0:1, 0:512], start=True, stop=True),
                        deps=[o_br1, o_p0rot[6]])
            o_be1 = add("vector", lambda v: v.tensor_copy(kD[:, 8192:8704], ps[0][:, 0:512]),
                        deps=[o_bm1, o_bstage_prev])
            o_br2 = add("gpsimd", lambda g, e=e: g.dma_start(
                out=rowst[0:1, 0:512], in_=fbe[e, 512:1024].rearrange("(b a) -> b a", b=1)),
                deps=[o_be1], dma=True)
            o_bm2 = add("tensor", lambda t: t.matmul(ps[0][:, 0:512], ones1f[:, :],
                                                     rowst[0:1, 0:512], start=True, stop=True),
                        deps=[o_br2])
            o_be2 = add("vector", lambda v: v.tensor_copy(kD[:, 8704:9216], ps[0][:, 0:512]),
                        deps=[o_bm2])
            o_p0rot[6] = o_be2
            o_bst = add("gpsimd", lambda g, e=e: g.dma_start(
                out=sc_we[e, :, 8, :], in_=kD[:, 8192:9216]), deps=[o_be2], dma=True)
            o_bstage_prev = o_bst
            o_prev_stores.append(o_bst)
    o_wt_stores = o_prev_stores

    last_xnD = list(o_wt_stores)
    last_qD = [o_prev_ld[0], o_prev_ld[1]] + list(o_wt_stores)
    last_kD = list(o_wt_stores)
    last_vD = []
    last_oT = []
    o_route_tail = o_ms
    o_idx_all = []
    o_rts = {}
    o_cm_prev = None
    o_tt2_prev = None
    o_conv_dump = None
    o_psE = None
    o_psO = None
    o_pv_war = [None, None]
    o_rc = None
    o_spA = {}
    o_nm_d = {}
    o_spH = {}
    o_spHT = {}
    last_wbufA = [None, None]
    o_scale = []
    all_last_A7 = []

    for b in range(B_LOC):
        o_tr_xn = None
        o_first_trs = []
        o_prev_ts = [None, None]
        o_ln_tail = None
        for tt in range(TT):
            o_xl = add("sync", lambda s, tt=tt, b=b: s.dma_start(
                out=xtb[tt % 2][:, :], in_=x[b, tt * 128:(tt + 1) * 128, :]),
                deps=[o_prev_ts[tt % 2]] +
                     ([o_route_tail] if tt < 2 and b > 0 else []), dma=True)
            o_s1 = add("vector", lambda v, tt=tt: v.bn_stats(bnb[:, 0:6], xtb[tt % 2][:, 0:512]),
                       deps=[o_xl, o_ln_tail])
            o_s2 = add("vector", lambda v, tt=tt: v.bn_stats(bnb[:, 6:12], xtb[tt % 2][:, 512:1024]),
                       deps=[o_s1])
            o_ag = add("vector", lambda v: v.bn_aggr(bnb[:, 0:2], bnb[:, 0:12]), deps=[o_s2])
            mcol = mrs[:, (b * TT + tt) * 2:(b * TT + tt) * 2 + 1]
            rcol = mrs[:, (b * TT + tt) * 2 + 1:(b * TT + tt) * 2 + 2]
            o_mc = add("vector", lambda v, mcol=mcol: v.tensor_copy(mcol, bnb[:, 0:1]),
                       deps=[o_ag])
            o_l1 = add("scalar", lambda sc: sc.activation(rt1[:, :], bnb[:, 1:2], AF.Ln,
                                                          bias=cst[:, 0:1]),
                       deps=[o_ag, o_bc_done if b == 0 and tt == 0 else None])
            o_l2 = add("scalar", lambda sc, rcol=rcol: sc.activation(rcol, rt1[:, :], AF.Exp,
                                                                     scale=-0.5), deps=[o_l1])
            o_ts = add("vector", lambda v, tt=tt, mcol=mcol, rcol=rcol: v.tensor_scalar(
                xnT[tt % 2][:, :], xtb[tt % 2][:, :], mcol, rcol, ALU.subtract, ALU.mult),
                deps=[o_l2, o_mc, o_tr_xn] +
                     ((o_pv_war + [o_route_tail]) if tt < 2 else []))
            o_prev_ts[tt % 2] = o_ts
            o_ln_tail = o_ts
            o_nm_d[b * TT + tt] = add("vector", lambda v, mcol=mcol, rcol=rcol:
                v.tensor_scalar(mcol, mcol, rcol, -1.0, ALU.mult, ALU.mult),
                deps=[o_ts])
            for k in range(KT):
                o_tr_xn = add("scalar", lambda sc, tt=tt, k=k: sc.dma_start_transpose(
                    xnD[:, k * T + tt * 128:k * T + (tt + 1) * 128],
                    xnT[tt % 2][:, k * 128:(k + 1) * 128]),
                    deps=[o_ts] + (last_xnD if tt == 0 else []), dma=True)
                if tt == 0:
                    o_first_trs.append(o_tr_xn)
        last_xnD = []

        if DEBUG_SPILLS and b == 0:
            o_dbgx = add("gpsimd", lambda g: g.dma_start(out=dbgx[:, :], in_=xnD[:, :]),
                         deps=[o_tr_xn], dma=True)
        dn_src = {}
        o_pl_last = None
        o_ps2_last = None
        o_lastmm_all = []
        for s_i, (wsrc, dstD) in enumerate([(wq, qD), (wk, kD), (wv, vD)]):
            dlast = [last_qD, last_kD, last_vD][s_i]
            cwoff = s_i * KT * 4
            o_wld = {}
            o_prev_ev = [None, None]
            o_silu = {}
            o_zs = {}
            o_conv_prev = {}
            for h in range(H_DN):
                for mi2, mi in enumerate((2 * h, 2 * h + 1)):
                    o_wld[mi] = add("gpsimd", lambda g, mi=mi, wsrc=wsrc: g.dma_start(
                        out=wbufA[mi % 2][:, 0:1024].rearrange("p (k m) -> p k m", k=KT),
                        in_=wsrc[:, mi * 128:(mi + 1) * 128].rearrange("(k p) m -> p k m", p=128)),
                        deps=[last_wbufA[mi % 2]], dma=True)
                    last_wbufA[mi % 2] = o_wld[mi]
                for mi2 in range(2):
                    o_zs[mi2] = add("vector", lambda v, mi2=mi2: v.memset(stg[mi2][:, 0:3], 0.0),
                                    deps=[o_conv_prev.get(mi2)])
                o_l2tail = None
                for qc in range(NQ):
                    for mi2, mi in enumerate((2 * h, 2 * h + 1)):
                        gi = mi * NQ + qc
                        pb = ps[gi % 2]
                        o_mm = add("tensor", lambda t, mi=mi, qc=qc, pb=pb: [
                            t.matmul(pb[:, :], wbufA[mi % 2][:, k * 128:(k + 1) * 128],
                                     xnD[:, k * T + qc * 512:k * T + (qc + 1) * 512],
                                     start=(k == 0), stop=(k == KT - 1))
                            for k in range(KT)][-1],
                            deps=[o_wld[mi], o_tr_xn, o_prev_ev[gi % 2]])
                        o_lastmm_all.append(o_mm)
                        last_wbufA[mi % 2] = o_mm
                        o_ev = add("vector", lambda v, mi2=mi2, pb=pb: v.tensor_copy(
                            stg[mi2][:, 3:515], pb[:, :]),
                            deps=[o_mm, o_zs[mi2]])
                        if DEBUG_SPILLS and b == 0 and s_i == 2 and mi == 0 and qc == 0:
                            o_dmp = add("gpsimd", lambda g: g.dma_start(
                                out=dbgs[:, :], in_=stg[0][:, 3:515]), deps=[o_ev], dma=True)
                            o_ev = o_dmp
                        o_prev_ev[gi % 2] = o_ev
                        wtap = [convw[:, cwoff + mi * 4 + tap:cwoff + mi * 4 + tap + 1]
                                for tap in range(4)]
                        o_c = add("vector", lambda v, mi2=mi2, w3=wtap[3]: v.tensor_scalar(
                            cvo[mi2][:, :], stg[mi2][:, 3:515], w3, None, ALU.mult),
                            deps=[o_ev, o_silu.get((mi2, qc - 1))] +
                                 ([o_parm_last] if b == 0 and s_i == 0 and h == 0 and qc == 0 else []) +
                                 ([o_rc] if b > 0 and s_i == 0 and h == 0 and qc == 0 else []))
                        for tap in (2, 1, 0):
                            o_c = add("vector", lambda v, mi2=mi2, tap=tap, wt=wtap[tap]:
                                      v.scalar_tensor_tensor(
                                          cvo[mi2][:, :], stg[mi2][:, tap:tap + 512], wt,
                                          cvo[mi2][:, :], ALU.mult, ALU.add), deps=[o_c])
                        o_cc = add("vector", lambda v, mi2=mi2: v.tensor_copy(
                            stg[mi2][:, 0:3], stg[mi2][:, 512:515]), deps=[o_c])
                        if DEBUG_SPILLS and b == 0 and s_i == 2 and mi == 0 and qc == 0:
                            o_dm2 = add("gpsimd", lambda g: g.dma_start(
                                out=dbgc[:, :], in_=cvo[0][:, :]), deps=[o_c], dma=True)
                            o_conv_dump = o_dm2
                        o_zs[mi2] = o_cc
                        o_conv_prev[mi2] = o_cc
                        dbg_cd = [o_conv_dump] if (DEBUG_SPILLS and b == 0 and s_i == 2
                                                    and mi == 0 and qc == 0) else []
                        if s_i < 2:
                            o_si = add("scalar", lambda sc, mi2=mi2: sc.activation(
                                cvo[mi2][:, :], cvo[mi2][:, :], AF.Silu), deps=[o_c] + dbg_cd)
                        else:
                            o_si = add("scalar", lambda sc, mi=mi, qc=qc, mi2=mi2: sc.activation(
                                dstD[:, mi * T + qc * 512:mi * T + (qc + 1) * 512],
                                cvo[mi2][:, :], AF.Silu),
                                deps=[o_c] + dbg_cd + (dlast if mi == 0 and qc == 0 else []))
                            dn_src[(s_i, mi, qc)] = o_si
                        o_silu[(mi2, qc)] = o_si
                    if s_i < 2:
                        o_q0 = add("scalar", lambda sc: sc.activation(
                            slab[0][:, :], cvo[0][:, :], AF.Square),
                            deps=[o_silu[(0, qc)], o_l2tail] +
                                 ([o_tr_xn] if s_i == 0 and h == 0 and qc == 0 else []))
                        o_q1 = add("scalar", lambda sc: sc.activation(
                            slab[1][:, :], cvo[1][:, :], AF.Square),
                            deps=[o_silu[(1, qc)], o_l2tail])
                        o_sm = add("tensor", lambda t: [
                            t.matmul(pL[:, :], onescol[:, :], slab[0][:, :], start=True, stop=False),
                            t.matmul(pL[:, :], onescol[:, :], slab[1][:, :], start=False, stop=True)][-1],
                            deps=[o_q0, o_q1, o_pl_last])
                        o_r1 = add("scalar", lambda sc: sc.activation(
                            rowst[0:1, 0:512], pL[:, :], AF.Ln, bias=cst[0:1, 1:2]), deps=[o_sm])
                        o_pl_last = o_r1
                        b_ap = cst[0:1, 6:7] if s_i == 0 else cst[0:1, 7:8]
                        o_r2 = add("scalar", lambda sc, b_ap=b_ap: sc.activation(
                            rowst[0:1, 0:512], rowst[0:1, 0:512], AF.Exp, bias=b_ap,
                            scale=-0.5), deps=[o_r1])
                        o_bc = add("tensor", lambda t: t.matmul(
                            ps[2][:, :], ones1f[:, :], rowst[0:1, 0:512], start=True, stop=True),
                            deps=[o_r2, o_ps2_last])
                        o_t0 = add("vector", lambda v, h=h, qc=qc, dstD=dstD: v.tensor_tensor(
                            dstD[:, (2 * h) * T + qc * 512:(2 * h) * T + (qc + 1) * 512],
                            cvo[0][:, :], ps[2][:, :], ALU.mult),
                            deps=[o_bc] + (dlast if h == 0 and qc == 0 else []))
                        o_t1 = add("vector", lambda v, h=h, qc=qc, dstD=dstD: v.tensor_tensor(
                            dstD[:, (2 * h + 1) * T + qc * 512:(2 * h + 1) * T + (qc + 1) * 512],
                            cvo[1][:, :], ps[2][:, :], ALU.mult), deps=[o_bc])
                        o_ps2_last = o_t1
                        o_l2tail = o_t1
                        dn_src[(s_i, 2 * h, qc)] = o_t0
                        dn_src[(s_i, 2 * h + 1, qc)] = o_t1
        last_qD = []
        last_kD = []
        last_vD = []

        dbg_ops = []
        if DEBUG_SPILLS and b == 0:
            alldn = [dn_src[(s2, m2, q2)] for s2 in range(3) for m2 in range(KT)
                     for q2 in range(NQ)]
            dbg_ops.append(add("gpsimd", lambda g: g.dma_start(out=dbgq[:, :], in_=qD[:, :]),
                               deps=alldn, dma=True))
            dbg_ops.append(add("gpsimd", lambda g: g.dma_start(out=dbgk[:, :], in_=kD[:, :]),
                               deps=alldn, dma=True))
            dbg_ops.append(add("gpsimd", lambda g: g.dma_start(out=dbgv[:, :], in_=vD[:, :]),
                               deps=alldn, dma=True))
        o_pAe = None
        o_pBe = None
        o_pCe = o_ps2_last and None
        o_pDe = None
        o_ps23_init = o_ps2_last
        o_ps2_war = None
        o_ps3_war = None
        o_ps3f_war = None
        o_pb16_war = None
        o_tt_war = []
        o_b6_war = []
        o_om_prev = None
        o_usb_war = None
        o_setK_war = [[], []]
        o_setV_war = [None, None]
        o_setTV_war = [None, None]
        o_setTK_war = [None, None]
        o_dn_last = []
        o_trV2_list = []
        o_prev_head_end = None
        o_odt_war = [None, None]
        o_ps01_init = o_l2tail
        Pb16 = QtT[2][:, 0:64].bitcast(BF16)
        Tt_sb = QtT[3][:, 0:64].bitcast(BF16)
        for h in range(H_DN):
            m0, m1 = 2 * h, 2 * h + 1
            o_Su = add("vector", lambda v: [
                v.memset(S1[:, :], 0.0), v.memset(Sb1[:, :], 0.0)][-1],
                deps=[dn_src[(0, 2 * h, 0)], o_prev_head_end])
            o_oTev = []
            dep_k = [dn_src[(1, m0, qc)] for qc in range(NQ)] + \
                    [dn_src[(1, m1, qc)] for qc in range(NQ)]
            dep_q = [dn_src[(0, m0, qc)] for qc in range(NQ)] + \
                    [dn_src[(0, m1, qc)] for qc in range(NQ)]
            dep_v = [dn_src[(2, m0, qc)] for qc in range(NQ)] + \
                    [dn_src[(2, m1, qc)] for qc in range(NQ)]
            for ci in range(NCH):
                t0 = ci * 128
                par = (h * NCH + ci) % 2
                sbase = 1024 * par
                KTs = slabT[:, sbase + 0:sbase + 256]
                VTs = slabT[:, sbase + 256:sbase + 512]
                TKs = slabT[:, sbase + 512:sbase + 768]
                TVs2 = slabT[:, sbase + 768:sbase + 1024]
                kd0 = kD[:, m0 * T + t0:m0 * T + t0 + 128]
                kd1 = kD[:, m1 * T + t0:m1 * T + t0 + 128]
                qd0 = qD[:, m0 * T + t0:m0 * T + t0 + 128]
                qd1 = qD[:, m1 * T + t0:m1 * T + t0 + 128]
                vd0 = vD[:, m0 * T + t0:m0 * T + t0 + 128]
                vd1 = vD[:, m1 * T + t0:m1 * T + t0 + 128]
                last_ch = (ci == NCH - 1)
                ps2b0 = ps[2][:, 0:64].bitcast(BF16)
                ps2b1 = ps[2][:, 64:128].bitcast(BF16)
                ps2b01 = ps[2][:, 0:128].bitcast(BF16)
                ps3b0 = ps[3][:, 0:64].bitcast(BF16)
                ps3b1 = ps[3][:, 64:128].bitcast(BF16)
                ps3b01 = ps[3][:, 0:128].bitcast(BF16)
                ps3f = ps[3][:, 256:384]
                o_ptK1 = add("tensor", lambda t, kd0=kd0, ps2b0=ps2b0: t.transpose(
                    ps2b0, kd0, ident[:, :]),
                    deps=dep_k + [o_ps2_war] + o_setK_war[par] +
                         ([o_ps23_init] if o_ps23_init else []))
                o_ptK2 = add("tensor", lambda t, kd1=kd1, ps2b1=ps2b1: t.transpose(
                    ps2b1, kd1, ident[:, :]), deps=[o_ptK1])
                o_evK = add("vector", lambda v, KTs=KTs, ps2b01=ps2b01: v.tensor_copy(
                    KTs, ps2b01), deps=[o_ptK2])
                o_ptV1 = add("tensor", lambda t, vd0=vd0, ps3b0=ps3b0: t.transpose(
                    ps3b0, vd0, ident[:, :]),
                    deps=dep_v + [o_ps3_war, o_setV_war[par]] +
                         ([o_ps23_init] if o_ps23_init else []))
                o_ps23_init = None
                o_ptV2 = add("tensor", lambda t, vd1=vd1, ps3b1=ps3b1: t.transpose(
                    ps3b1, vd1, ident[:, :]), deps=[o_ptV1])
                o_evV = add("vector", lambda v, VTs=VTs, ps3b01=ps3b01: v.tensor_copy(
                    VTs, ps3b01), deps=[o_ptV2])
                o_trV2_list.append(o_evV)
                o_A = add("tensor", lambda t, kd0=kd0, kd1=kd1: [
                    t.matmul(pA[:, :], kd0, kd0, start=True, stop=False),
                    t.matmul(pA[:, :], kd1, kd1, start=False, stop=True)][-1],
                    deps=dep_k + [o_pAe])
                o_Q0 = add("vector", lambda v: v.tensor_tensor(QpR[0], pA[:, :],
                                                               m_ltn[:, :], ALU.mult),
                           deps=[o_A] + ([o_spA.get((b - 1, TT - 2)), o_spA.get((b - 1, TT - 1))]
                                         if h == 0 and ci == 0 else []))
                o_Qt0 = add("vector", lambda v: v.tensor_tensor(QtR[0], pA[:, :],
                                                                m_utn[:, :], ALU.mult),
                            deps=[o_Q0, o_A])
                o_pAe = o_Qt0
                o_P0 = add("vector", lambda v: v.tensor_tensor(
                    QpR[2], QpT[0][:, :].bitcast(F32), ident[:, :], ALU.add),
                    deps=[o_Q0, o_A])
                o_qp_prev, o_qt_prev, o_p_prev = o_Q0, o_Qt0, o_P0
                for j in range(1, 7):
                    c0, c1 = (j - 1) % 2, j % 2
                    o_mq = add("tensor", lambda t, c0=c0: t.matmul(
                        pA[:, :], QtR[c0], QpR[c0], start=True, stop=True),
                        deps=[o_qp_prev, o_qt_prev, o_pAe])
                    o_qp = add("vector", lambda v, c1=c1: v.tensor_copy(QpR[c1], pA[:, :]),
                               deps=[o_mq])
                    o_pAe = o_qp
                    o_mqt = add("tensor", lambda t, c0=c0: t.matmul(
                        pB[:, :], QpR[c0], QtR[c0], start=True, stop=True),
                        deps=[o_qp_prev, o_qt_prev, o_pBe])
                    o_qt = add("vector", lambda v, c1=c1: v.tensor_copy(QtR[c1], pB[:, :]),
                               deps=[o_mqt])
                    o_pBe = o_qt
                    o_mp = add("tensor", lambda t, c1=c1, c0=c0, ps3f=ps3f: t.matmul(
                        ps3f, QtR[c1], QpR[2 + c0], start=True, stop=True),
                        deps=[o_qt, o_p_prev, o_ps3f_war, o_evV if j == 1 else None])
                    o_pev = add("vector", lambda v, c1=c1, c0=c0, ps3f=ps3f: v.tensor_tensor(
                        QpR[2 + c1], QpT[2 + c0][:, :].bitcast(F32), ps3f, ALU.add),
                        deps=[o_mp])
                    o_ps3f_war = o_pev
                    o_ps3_war = o_pev
                    o_qp_prev, o_qt_prev, o_p_prev = o_qp, o_qt, o_pev
                o_pb16 = add("vector", lambda v: v.tensor_copy(
                    Pb16, QpT[2][:, :].bitcast(F32)), deps=[o_p_prev, o_pb16_war])
                o_ttr = add("tensor", lambda t, ps2b0=ps2b0: t.transpose(
                    ps2b0, Pb16, ident[:, :]), deps=[o_pb16, o_evK])
                o_evTt = add("vector", lambda v, ps2b0=ps2b0: v.tensor_copy(Tt_sb, ps2b0),
                             deps=[o_ttr] + o_tt_war)
                o_ps2_war = o_evTt
                o_TV = add("tensor", lambda t, VTs=VTs: t.matmul(
                    bank4[:, 0:256], Tt_sb, VTs, start=True, stop=True),
                    deps=[o_evTt, o_evV, o_pAe])
                o_TVev = add("vector", lambda v, TVs2=TVs2: v.tensor_copy(
                    TVs2, bank4[:, 0:256]), deps=[o_TV, o_setTV_war[par]])
                o_pAe = o_TVev
                o_TK1 = add("tensor", lambda t, KTs=KTs: t.matmul(
                    bank5[:, 0:128], KTs[:, 0:128], Tt_sb, start=True, stop=True),
                    deps=[o_evTt, o_evK, o_pBe])
                o_TK2 = add("tensor", lambda t, KTs=KTs: t.matmul(
                    bank5[:, 128:256], KTs[:, 128:256], Tt_sb, start=True, stop=True),
                    deps=[o_TK1])
                o_TKev = add("vector", lambda v, TKs=TKs: v.tensor_copy(
                    TKs, bank5[:, 0:256]), deps=[o_TK2, o_setTK_war[par]])
                o_pBe = o_TKev
                o_tt_war = [o_TV, o_TK2]
                o_pb16_war = o_ttr
                o_G = add("tensor", lambda t, kd0=kd0, kd1=kd1, qd0=qd0, qd1=qd1: [
                    t.matmul(bank6[:, 256:384], kd0, qd0, start=True, stop=False),
                    t.matmul(bank6[:, 256:384], kd1, qd1, start=False, stop=True)][-1],
                    deps=dep_q + o_b6_war)
                o_Gm = add("vector", lambda v: v.tensor_tensor(Gm[:, :], bank6[:, 256:384],
                                                               m_uti[:, :], ALU.mult),
                           deps=[o_G, o_om_prev])
                o_U12 = add("tensor", lambda t, TKs=TKs: [
                    t.matmul(pC[:, :], TKs[:, 0:128], Sb1[:, 0:DH], start=True, stop=False),
                    t.matmul(pC[:, :], TKs[:, 128:256], Sb1[:, DH:2 * DH],
                             start=False, stop=True)][-1],
                    deps=[o_TKev, o_Su, o_pCe, o_Gm])
                o_Uev = add("vector", lambda v, TVs2=TVs2: v.tensor_tensor(
                    Usb[:, :], TVs2, pC[:, :], ALU.subtract),
                    deps=[o_U12, o_TVev, o_usb_war])
                o_pCe = o_Uev
                o_setTV_war[par] = o_Uev
                o_setTK_war[par] = o_U12
                o_setV_war[par] = o_TV
                o_om = add("tensor", lambda t, qd0=qd0, qd1=qd1: [
                    t.matmul(pC[:, :], qd0, Sb1[:, 0:DH], start=True, stop=False),
                    t.matmul(pC[:, :], qd1, Sb1[:, DH:2 * DH], start=False, stop=False),
                    t.matmul(pC[:, :], Gm[:, :], Usb[:, :], start=False, stop=True)][-1],
                    deps=dep_q + [o_Uev, o_Gm, o_Su, o_pCe])
                o_om_prev = o_om
                o_oe = add("vector", lambda v, ci=ci: v.tensor_copy(
                    oT[:, ci * DH:(ci + 1) * DH], pC[:, :]),
                    deps=[o_om] + (last_oT if ci == 0 and h == 0 else []))
                o_pCe = o_oe
                o_b6_war = [o_om, o_oe]
                o_oTev.append(o_oe)
                if not last_ch:
                    o_su1 = add("tensor", lambda t, KTs=KTs: t.matmul(
                        pD[:, :], KTs[:, 0:128], Usb[:, :], start=True, stop=True),
                        deps=[o_Uev, o_evK, o_pDe, o_om])
                    o_sa1 = add("vector", lambda v: v.tensor_tensor(
                        S1[:, 0:DH], S1[:, 0:DH], pD[:, :], ALU.add),
                        deps=[o_su1, o_Su])
                    o_sb1 = add("vector", lambda v: v.tensor_copy(
                        Sb1[:, 0:DH], S1[:, 0:DH]), deps=[o_sa1])
                    o_su2 = add("tensor", lambda t, KTs=KTs: t.matmul(
                        pD[:, :], KTs[:, 128:256], Usb[:, :], start=True, stop=True),
                        deps=[o_sa1, o_evK, o_Uev])
                    o_sa2 = add("vector", lambda v: v.tensor_tensor(
                        S1[:, DH:2 * DH], S1[:, DH:2 * DH], pD[:, :], ALU.add),
                        deps=[o_su2, o_Su])
                    o_sb2 = add("vector", lambda v: v.tensor_copy(
                        Sb1[:, DH:2 * DH], S1[:, DH:2 * DH]), deps=[o_sa2])
                    o_pDe = o_sa2
                    o_usb_war = o_su2
                    o_setK_war[par] = [o_su2, o_TK2]
                    o_Su = add("vector", lambda v: v.engine_nop(), deps=[o_sb1, o_sb2])
                else:
                    o_usb_war = o_om
                    o_setK_war[par] = [o_TK2]
            o_dn_last.append(o_oe)
            o_prev_head_end = o_oe
            if DEBUG_SPILLS and b == 0 and h == 0:
                dbg_ops.append(add("gpsimd", lambda g: g.dma_start(out=dbgo[:, :], in_=oT[:, :]),
                                   deps=o_oTev, dma=True))
            o_sq = add("scalar", lambda sc: sc.activation(rmsq[:, :], oT[:, :], AF.Square),
                       deps=o_oTev + (dbg_ops if DEBUG_SPILLS else []))
            o_rd = add("vector", lambda v: v.tensor_reduce(
                rmsr[:, :], rmsq.ap().rearrange("p (i d) -> p i d", d=DH), AX.X, ALU.add),
                deps=[o_sq])
            o_r1 = add("scalar", lambda sc: sc.activation(
                rmsr[:, :], rmsr[:, :], AF.Ln, bias=cst[:, 2:3], scale=1.0 / DH), deps=[o_rd])
            o_r2 = add("scalar", lambda sc: sc.activation(
                rmsr[:, :], rmsr[:, :], AF.Exp, scale=-0.5), deps=[o_r1])
            o_on = []
            for ci in range(NCH):
                o_on.append(add("vector", lambda v, ci=ci: v.scalar_tensor_tensor(
                    rmsq[:, ci * DH:(ci + 1) * DH], oT[:, ci * DH:(ci + 1) * DH],
                    rmsr[:, ci:ci + 1], onwB[:, :], ALU.mult, ALU.mult), deps=[o_r2, o_sq]))
            for ci in range(NCH):
                for dvt in range(2):
                    pso = ps[dvt][:, 0:64].bitcast(BF16)
                    o_pto = add("tensor", lambda t, ci=ci, dvt=dvt, pso=pso: t.transpose(
                        pso,
                        rmsq[:, ci * DH + dvt * 128:ci * DH + (dvt + 1) * 128], ident[:, :]),
                        deps=[o_on[ci], o_om, o_odt_war[dvt]] +
                             ([o_ps01_init] if o_ps01_init else []))
                    o_ps01_init = None
                    o_odt = add("vector", lambda v, ci=ci, dvt=dvt, h=h, pso=pso: v.tensor_copy(
                        qD[:, (2 * h + dvt) * T + ci * 128:(2 * h + dvt) * T + (ci + 1) * 128],
                        pso),
                        deps=[o_pto] + [dn_src[(0, 2 * h + dvt, qc)] for qc in range(NQ)])
                    o_odt_war[dvt] = o_odt
                    last_qD.append(o_odt)
        o_oD_done = list(last_qD)
        last_oT = []

        o_wol = {}
        o_wol[0] = add("gpsimd", lambda g: g.dma_start(
            out=oT[:, 0:4096].rearrange("p (k m) -> p k m", k=KT),
            in_=wo[:, 0:512].rearrange("(k p) m -> p k m", p=128)),
            deps=[o_sq, o_on[NCH - 1]] + o_oD_done[-2:], dma=True)
        o_wol[1] = add("gpsimd", lambda g: g.dma_start(
            out=rmsq[:, 0:4096].rearrange("p (k m) -> p k m", k=KT),
            in_=wo[:, 512:1024].rearrange("(k p) m -> p k m", p=128)),
            deps=o_oD_done, dma=True)
        o_evw = [None, None]
        o_wo_mm_last = None
        for tt in range(TT):
            for chh in range(2):
                gi = tt * 2 + chh
                pb = ps[gi % 2]
                rbuf = oT if chh == 0 else rmsq
                o_mm = add("tensor", lambda t, tt=tt, pb=pb, rbuf=rbuf: [
                    t.matmul(pb[:, :], qD[:, k * T + tt * 128:k * T + (tt + 1) * 128],
                             rbuf[:, k * 512:(k + 1) * 512],
                             start=(k == 0), stop=(k == KT - 1))
                    for k in range(KT)][-1],
                    deps=o_oD_done + [o_wol[chh], o_evw[gi % 2]])
                o_wo_mm_last = o_mm
                evb = ev0 if tt % 2 == 0 else ev1
                o_evw[gi % 2] = add("vector", lambda v, chh=chh, pb=pb, evb=evb: v.tensor_copy(
                    evb[:, chh * 512:(chh + 1) * 512], pb[:, :]),
                    deps=[o_mm, o_spA.get((b, tt - 2))] +
                         (o_zlist if b == 0 and tt < 2 else []))
            o_spA[(b, tt)] = add("gpsimd", lambda g, tt=tt, b=b,
                                 evb=(ev0 if tt % 2 == 0 else ev1): g.dma_start(
                out=sp_attn[b, tt * 128:(tt + 1) * 128, :], in_=evb[:, :]),
                deps=[o_evw[0], o_evw[1]], dma=True)
        o_attn_done = [o_spA[(b, tt)] for tt in range(TT)]

        o_fw = {}
        o_fq_done = {}
        o_evf = [None, None]
        o_first_evf = {}
        for s_i2, (dstD2, bcc) in enumerate([(qD, bqc), (kD, bkc)]):
            scw = sc_w[s_i2]
            for mi in range(KT):
                o_fw[(s_i2, mi)] = add("gpsimd", lambda g, scw=scw, mi=mi: g.dma_start(
                    out=wbufA[mi % 2][:, 0:1024].rearrange("p (k m) -> p k m", k=KT),
                    in_=scw[:, mi * 128:(mi + 1) * 128].rearrange("(k p) m -> p k m", p=128)),
                    deps=[last_wbufA[mi % 2]] +
                         ([o_wol[0], o_wol[1]] if s_i2 == 0 and mi < 2 else []), dma=True)
                last_wbufA[mi % 2] = o_fw[(s_i2, mi)]
                for qc in range(NQ):
                    gi = mi * NQ + qc
                    pb = ps[gi % 2]
                    o_mm = add("tensor", lambda t, mi=mi, qc=qc, pb=pb: [
                        t.matmul(pb[:, :], wbufA[mi % 2][:, k * 128:(k + 1) * 128],
                                 xnD[:, k * T + qc * 512:k * T + (qc + 1) * 512],
                                 start=(k == 0), stop=(k == KT - 1))
                        for k in range(KT)][-1],
                        deps=[o_fw[(s_i2, mi)], o_evf[gi % 2],
                              o_evw[0] if s_i2 == 0 and mi == 0 and qc == 0 else None,
                              o_evw[1] if s_i2 == 0 and mi == 0 and qc == 0 else None])
                    last_wbufA[mi % 2] = o_mm
                    extra = []
                    if s_i2 == 0 and mi == 0 and qc == 0:
                        extra = [o_wo_mm_last]
                    if s_i2 == 1 and mi == 0 and qc == 0:
                        extra = o_dn_last + [o_wo_mm_last]
                    o_evf[gi % 2] = add("vector", lambda v, mi=mi, qc=qc, pb=pb,
                                        dstD2=dstD2, bcc=bcc: v.tensor_scalar(
                        dstD2[:, mi * T + qc * 512:mi * T + (qc + 1) * 512], pb[:, :],
                        bcc[:, mi:mi + 1], None, ALU.add),
                        deps=[o_mm] + extra +
                             ([o_parm_last] if b == 0 and s_i2 == 0 and mi == 0 and qc == 0 else []))
                    o_fq_done[(s_i2, mi, qc)] = o_evf[gi % 2]
        all_fqk = [o_fq_done[(si2, mi, qc)] for si2 in range(2) for mi in range(KT)
                   for qc in range(NQ)]
        vwb = [oT, rmsq]
        o_vw = {}
        for hc in range(2):
            o_vw[hc] = add("gpsimd", lambda g, hc=hc: g.dma_start(
                out=vwb[hc][:, 0:4096].rearrange("p (k f) -> p k f", k=KT),
                in_=sc_w[2][:, hc * 512:(hc + 1) * 512].rearrange("(k p) f -> p k f", p=128)),
                deps=[o_wo_mm_last], dma=True)
        o_evv = [None, None]
        o_vt_last = None
        first_vt_deps = [o_trV2_list[-1]] + (list(o_scale) if b > 0 else list(last_vD))
        for tt in range(TT):
            for hc in range(2):
                gi = tt * 2 + hc
                pb = ps[2 + gi % 2]
                o_mm = add("tensor", lambda t, tt=tt, hc=hc, pb=pb: [
                    t.matmul(pb[:, :], xnD[:, k * T + tt * 128:k * T + (tt + 1) * 128],
                             vwb[hc][:, k * 512:(k + 1) * 512],
                             start=(k == 0), stop=(k == KT - 1))
                    for k in range(KT)][-1],
                    deps=[o_vw[hc], o_tr_xn, o_evv[gi % 2], o_ps2_last if tt == 0 else None])
                o_evv[gi % 2] = add("vector", lambda v, tt=tt, hc=hc, pb=pb: v.tensor_tensor(
                    vD[:, tt * 1024 + hc * 512:tt * 1024 + (hc + 1) * 512], pb[:, :],
                    bvB[:, hc * 512:(hc + 1) * 512], ALU.add),
                    deps=[o_mm] + (first_vt_deps if tt == 0 and hc == 0 else []))
                o_vt_last = o_evv[gi % 2]
        o_vt_all = [o_evv[0], o_evv[1]]

        o_hd_parts = []
        o_den_dma = []
        o_poE = None
        o_poO = None
        o_ts_ev_prev = None
        o_slab_war = {}
        o_sh_prev = [None, None]
        o_denr_war = [None, None]
        for hp in range(8):
            hE, hO = 2 * hp, 2 * hp + 1
            o_vg = []
            for par, hh in enumerate((hE, hO)):
                o_v1 = add("vector", lambda v, par=par, hh=hh: v.tensor_copy(
                    vaug[par].ap().rearrange("p (i c) -> p i c", c=65)[:, :, 0:64],
                    vD.ap().rearrange("p (i c) -> p i c", c=1024)[:, :, hh * 64:(hh + 1) * 64]),
                    deps=(o_vt_all if hp == 0 else []) + [o_pv_war[par]] +
                         ([last_wbufA[par]] if hp == 0 else []))
                o_v2 = add("vector", lambda v, par=par: v.memset(
                    vaug[par].ap().rearrange("p (i c) -> p i c", c=65)[:, :, 64:65], 1.0),
                    deps=[o_v1])
                o_vg.append(o_v2)
            o_tsm = add("tensor", lambda t: [
                t.matmul(ps[2][0:65, par * 16 + kt:par * 16 + kt + 1],
                         vaug[par][:, kt * 65:(kt + 1) * 65], onescol[:, :],
                         start=True, stop=True)
                for par in range(2) for kt in range(TT)][-1],
                deps=o_vg + [o_poE, o_ts_ev_prev])
            o_tse = add("vector", lambda v: v.tensor_copy(tsum[:, :], ps[2][0:65, 0:32]),
                        deps=[o_tsm])
            o_ts_ev_prev = o_tse
            o_suf = None
            for par in range(2):
                for qc in range(NQ):
                    lo = 4 * qc + 4
                    if lo < TT:
                        o_suf = add("vector", lambda v, par=par, qc=qc, lo=lo: v.tensor_reduce(
                            sufc[:, par * NQ + qc:par * NQ + qc + 1],
                            tsum[:, par * 16 + lo:par * 16 + 16], AX.X, ALU.add), deps=[o_tse, o_suf])
                    else:
                        o_suf = add("vector", lambda v, par=par, qc=qc: v.memset(
                            sufc[:, par * NQ + qc:par * NQ + qc + 1], 0.0), deps=[o_tse, o_suf])
            for qc in range(NQ):
                nkt = 4 * qc + 4
                o_pv_lastE = None
                o_pv_lastO = None
                for kt in range(nkt):
                    sl_e = slab[kt % 2]
                    sl_o = slab[2 + kt % 2]
                    kslE = kD[0:64, hp * T + kt * 128:hp * T + (kt + 1) * 128]
                    kslO = kD[64:128, hp * T + kt * 128:hp * T + (kt + 1) * 128]
                    qslE = qD[0:64, hp * T + qc * 512:hp * T + (qc + 1) * 512]
                    qslO = qD[64:128, hp * T + qc * 512:hp * T + (qc + 1) * 512]
                    o_sE = add("tensor", lambda t, kslE=kslE, qslE=qslE: t.matmul(
                        ps[0][:, :], kslE, qslE, start=True, stop=True),
                        deps=[o_psE] + (all_fqk if hp == 0 and qc == 0 and kt == 0 else []))
                    o_sO = add("tensor", lambda t, kslO=kslO, qslO=qslO: t.matmul(
                        ps[1][:, :], kslO, qslO, start=True, stop=True),
                        deps=[o_psO, o_sE])
                    o_xE = add("scalar", lambda sc, sl_e=sl_e: sc.activation(
                        sl_e[:, :], ps[0][:, :], AF.Exp, scale=1.0 / 32.0),
                        deps=[o_sE, o_slab_war.get((0, kt % 2))])
                    o_psE = o_xE
                    o_xO = add("scalar", lambda sc, sl_o=sl_o: sc.activation(
                        sl_o[:, :], ps[1][:, :], AF.Exp, scale=1.0 / 32.0),
                        deps=[o_sO, o_slab_war.get((1, kt % 2))])
                    o_psO = o_xO
                    r = kt - 4 * qc
                    if r >= 0:
                        def f_fix(v, sl=sl_e, r=r):
                            if r > 0:
                                v.memset(sl[:, 0:r * 128], 1.0)
                            return v.copy_predicated(sl[:, r * 128:(r + 1) * 128],
                                                     m_fut[:, :], onesb[:, :])
                        o_fE = add("vector", f_fix, deps=[o_xE])

                        def f_fixO(v, sl=sl_o, r=r):
                            if r > 0:
                                v.memset(sl[:, 0:r * 128], 1.0)
                            return v.copy_predicated(sl[:, r * 128:(r + 1) * 128],
                                                     m_fut[:, :], onesb[:, :])
                        o_fO = add("vector", f_fixO, deps=[o_xO])
                    else:
                        o_fE, o_fO = o_xE, o_xO
                    o_pv_lastE = add("tensor", lambda t, kt=kt, sl_e=sl_e, nkt=nkt: t.matmul(
                        ps[2][0:65, :], vaug[0][:, kt * 65:(kt + 1) * 65], sl_e[:, :],
                        start=(kt == 0), stop=(kt == nkt - 1)),
                        deps=[o_fE, o_poE if kt == 0 else None, o_tse if kt == 0 else None])
                    o_slab_war[(0, kt % 3)] = o_pv_lastE
                    o_pv_lastO = add("tensor", lambda t, kt=kt, sl_o=sl_o, nkt=nkt: t.matmul(
                        ps[3][0:65, :], vaug[1][:, kt * 65:(kt + 1) * 65], sl_o[:, :],
                        start=(kt == 0), stop=(kt == nkt - 1)),
                        deps=[o_fO, o_poO if kt == 0 else None])
                    o_slab_war[(1, kt % 3)] = o_pv_lastO
                o_pv_war[0] = o_pv_lastE
                o_pv_war[1] = o_pv_lastO
                hd_dep = [o_vt_last] + all_fqk if hp == 0 and qc == 0 else []
                o_heE = add("vector", lambda v, hp=hp, qc=qc: v.tensor_scalar(
                    xnD[0:64, hp * T + qc * 512:hp * T + (qc + 1) * 512],
                    ps[2][0:64, :], sufc[0:64, qc:qc + 1], None, ALU.add),
                    deps=[o_pv_lastE, o_suf] + hd_dep)
                o_dnE = add("vector", lambda v, qc=qc: v.tensor_scalar(
                    denr[0][64:65, :], ps[2][64:65, :], sufc[64:65, qc:qc + 1], None, ALU.add),
                    deps=[o_pv_lastE, o_suf, o_denr_war[0]])
                o_poE = o_dnE
                o_hoO = add("vector", lambda v, qc=qc: v.tensor_scalar(
                    oddst[qc % 2][:, :], ps[3][0:64, :], sufc[0:64, NQ + qc:NQ + qc + 1],
                    None, ALU.add),
                    deps=[o_pv_lastO, o_suf, o_sh_prev[qc % 2]])
                o_dnO = add("vector", lambda v, qc=qc: v.tensor_scalar(
                    denr[1][64:65, :], ps[3][64:65, :], sufc[64:65, NQ + qc:NQ + qc + 1],
                    None, ALU.add),
                    deps=[o_pv_lastO, o_suf, o_denr_war[1]])
                o_poO = o_dnO
                o_sh = add("gpsimd", lambda g, hp=hp, qc=qc: g.dma_start(
                    out=xnD[64:128, hp * T + qc * 512:hp * T + (qc + 1) * 512],
                    in_=oddst[qc % 2][:, :]),
                    deps=[o_hoO] + hd_dep, dma=True)
                o_sh_prev[qc % 2] = o_sh
                o_hd_parts += [o_heE, o_sh]
                o_d1 = add("gpsimd", lambda g, hE=hE, qc=qc: g.dma_start(
                    out=denM[hE:hE + 1, qc * 512:(qc + 1) * 512], in_=denr[0][64:65, :]),
                    deps=[o_dnE], dma=True)
                o_d2 = add("gpsimd", lambda g, hO=hO, qc=qc: g.dma_start(
                    out=denM[hO:hO + 1, qc * 512:(qc + 1) * 512], in_=denr[1][64:65, :]),
                    deps=[o_dnO], dma=True)
                o_denr_war[0] = o_d1
                o_denr_war[1] = o_d2
                o_den_dma += [o_d1, o_d2]
        last_wbufA = [o_pv_war[0], o_pv_war[1]]
        o_scale = []
        o_pbc = None
        o_bm_prev = None
        o_rc = None
        for qc in range(NQ):
            o_dl = add("gpsimd", lambda g, qc=qc: g.dma_start(
                out=denQ, in_=denM[0:16, qc * 512:(qc + 1) * 512]),
                deps=o_den_dma + [o_rc, o_pv_war[0], o_pv_war[1]], dma=True)
            o_rc = add("vector", lambda v: v.reciprocal(denr[0][0:16, :], denQ),
                       deps=[o_dl, o_denr_war[0] if qc == 0 else None, o_bm_prev])
            o_rowE = None
            o_rowO = None
            for hp in range(8):
                o_rowE = add("gpsimd", lambda g, hp=hp: g.dma_start(
                    out=cvo[1][0:1, 0:512], in_=denr[0][2 * hp:2 * hp + 1, :]),
                    deps=[o_rc, o_bm_prev, o_rowE], dma=True)
                o_rowO = add("gpsimd", lambda g, hp=hp: g.dma_start(
                    out=rowst[0:1, 0:512], in_=denr[0][2 * hp + 1:2 * hp + 2, :]),
                    deps=[o_rc, o_bm_prev, o_rowO, o_ps2_last if hp == 0 and qc == 0 else None],
                    dma=True)
                o_bm = add("tensor", lambda t: [
                    t.matmul(ps[0][0:64, :], ones1f[0:1, 0:64], cvo[1][0:1, 0:512],
                             start=True, stop=True),
                    t.matmul(ps[0][64:128, :], ones1f[0:1, 0:64], rowst[0:1, 0:512],
                             start=True, stop=True)][-1],
                    deps=[o_rowE, o_rowO, o_pbc,
                          o_psE if hp == 0 and qc == 0 else None])
                o_bm_prev = o_bm
                o_pbc = add("vector", lambda v, hp=hp, qc=qc: v.tensor_tensor(
                    xnD[:, hp * T + qc * 512:hp * T + (qc + 1) * 512],
                    xnD[:, hp * T + qc * 512:hp * T + (qc + 1) * 512],
                    ps[0][:, :], ALU.mult),
                    deps=[o_bm] + (o_hd_parts if hp == 0 and qc == 0 else []))
                o_scale.append(o_pbc)
        o_tr_hd = []
        for tt in range(TT):
            for k in range(KT):
                o_tr_hd.append(add("sync", lambda sc, tt=tt, k=k: sc.dma_start_transpose(
                    qD[:, tt * D + k * 128:tt * D + (k + 1) * 128],
                    xnD[:, k * T + tt * 128:k * T + (tt + 1) * 128]),
                    deps=o_scale + ([o_psE, o_psO, o_wo_mm_last] if tt == 0 and k == 0 else []),
                    dma=True))
        for tt in range(TT):
            o_spHT[(b, tt)] = add("gpsimd", lambda g, tt=tt, b=b: g.dma_start(
                out=sp_hdt[b * T + tt * 128:b * T + (tt + 1) * 128, :],
                in_=qD[:, tt * D:(tt + 1) * D]),
                deps=[o_tr_hd[tt * KT + KT - 1]], dma=True)
        all_last_A7 = o_scale + [o_spHT[(b, tt)] for tt in range(TT)]

        o_pR_last = None
        for tt in range(TT):
            o_lg = add("tensor", lambda t, tt=tt: [
                t.matmul(pR[:, :], xnD[:, k * T + tt * 128:k * T + (tt + 1) * 128],
                         wswT[:, k * NE:(k + 1) * NE], start=(k == 0), stop=(k == KT - 1))
                for k in range(KT)][-1],
                deps=o_scale + [o_pR_last, o_route_tail] +
                     ([o_parm_last] if b == 0 and tt == 0 else []))
            o_1 = add("vector", lambda v: v.tensor_tensor(lgt[:, :], pR[:, :], bswB[:, :],
                                                          ALU.add), deps=[o_lg, o_route_tail])
            o_2 = add("vector", lambda v: v.tensor_reduce(rt1[:, :], lgt[:, :], AX.X, ALU.max,
                                                          negate=True), deps=[o_1])
            o_3 = add("scalar", lambda sc: sc.activation(prb[:, :], lgt[:, :], AF.Exp,
                                                         bias=rt1[:, :]), deps=[o_2])
            o_4 = add("vector", lambda v: v.tensor_reduce(rt2[:, :], prb[:, :], AX.X, ALU.add),
                      deps=[o_3])
            o_5 = add("vector", lambda v, b=b, tt=tt: v.reciprocal(
                selP[:, b * TT + tt:b * TT + tt + 1], rt2[:, :]), deps=[o_4])
            o_6 = add("vector", lambda v: v.tensor_reduce(rt3[:, :], prb[:, :], AX.X, ALU.max),
                      deps=[o_3])
            o_7 = add("vector", lambda v: v.tensor_scalar(rt0[:, :], prb[:, :], rt3[:, :],
                                                          None, ALU.is_ge), deps=[o_6])
            o_8 = add("vector", lambda v: v.scalar_tensor_tensor(
                rt0[:, :], rt0[:, :], cst[:, 3:4], iota8[:, :], ALU.mult, ALU.add), deps=[o_7])
            o_9 = add("vector", lambda v: v.tensor_reduce(rt3[:, :], rt0[:, :], AX.X, ALU.min),
                      deps=[o_8])
            rcol = routesF[:, b * TT + tt:b * TT + tt + 1]
            o_10 = add("vector", lambda v, rcol=rcol: v.tensor_scalar(
                rcol, rt3[:, :], 1000.0, None, ALU.add), deps=[o_9])
            o_rts[b * TT + tt] = o_10
            ohsl = oneh8[:, :]
            o_11 = add("vector", lambda v, ohsl=ohsl, rcol=rcol: v.tensor_scalar(
                ohsl, iota8[:, :], rcol[:, 0:1], None, ALU.is_equal),
                deps=[o_10, o_cm_prev, o_tt2_prev])
            o_cm = add("tensor", lambda t, ohsl=ohsl: t.matmul(
                pR[:, :], m_uti[:, :], ohsl, start=True, stop=False), deps=[o_11, o_1])
            o_cm_prev = o_cm
            o_cb = add("tensor", lambda t: t.matmul(
                pR[:, :], ones1f[:, :], carry[:, :], start=False, stop=True),
                deps=[o_cm, o_route_tail])
            o_sel = add("vector", lambda v, ohsl=ohsl: v.tensor_tensor(
                rt0[:, :], pR[:, :], ohsl, ALU.mult), deps=[o_cb])
            o_pR_last = o_sel
            o_lp = add("vector", lambda v, b=b, tt=tt: v.tensor_reduce(
                lpos[:, b * TT + tt:b * TT + tt + 1], rt0[:, :], AX.X, ALU.add), deps=[o_sel])
            o_tt2 = add("tensor", lambda t, ohsl=ohsl: t.matmul(
                pRc[:, :], onescol[:, :], ohsl, start=True, stop=True),
                deps=[o_11, o_route_tail, o_sel])
            o_tt2_prev = o_tt2
            o_route_tail = add("vector", lambda v: v.tensor_tensor(
                carry[:, :], carry[:, :], pRc[:, :], ALU.add), deps=[o_tt2, o_lp])
        last_xnD = list(all_last_A7) + [o_pR_last, o_route_tail]
        last_qD = [o_spHT[(b, tt)] for tt in range(TT)]
        last_kD = [o_psE, o_psO]
        last_vD = [o_pv_war[0], o_pv_war[1]]
        last_oT = [o_vt_all[0], o_vt_all[1], o_vt_last, o_wo_mm_last]

    o_ci = add("gpsimd", lambda g: g.dma_start(out=cc_in[:, :], in_=carry[:, :]),
               deps=[o_route_tail], dma=True)

    o_b2a = add("gpsimd", lambda g: g.dma_start(out=rt1[0:8, 0:1], in_=carry[0:1, 0:NE]),
                deps=[o_route_tail], dma=True)
    o_b2b = add("vector", lambda v: v.tensor_scalar(rt2[0:8, 0:1], rt1[0:8, 0:1],
                                                    float(GR - 1), None, ALU.add), deps=[o_b2a])
    o_b2c1 = add("vector", lambda v: v.tensor_copy(slotu[0:8, 0:1], rt2[0:8, 0:1]),
                 deps=[o_b2b])
    o_b2c2 = add("vector", lambda v: v.tensor_scalar(slotu[0:8, 0:1], slotu[0:8, 0:1],
                                                     9, None, ALU.logical_shift_right),
                 deps=[o_b2c1])
    o_b2c3 = add("vector", lambda v: v.tensor_scalar(slotu[0:8, 0:1], slotu[0:8, 0:1],
                                                     9, None, ALU.logical_shift_left),
                 deps=[o_b2c2])
    o_b2d = add("vector", lambda v: v.tensor_copy(rt1[0:8, 0:1], slotu[0:8, 0:1]),
                deps=[o_b2c3])
    o_b2e = add("vector", lambda v: v.tensor_copy(b2cb[0:8, 0:1], rt1[0:8, 0:1]), deps=[o_b2d])
    o_b2f = add("tensor", lambda t: t.matmul(pA[0:8, 0:1], m_utn[0:8, 0:8], b2cb[0:8, 0:1],
                                             start=True, stop=True),
                deps=[o_b2e, o_route_tail, o_pR_last])
    o_b2g = add("vector", lambda v: v.tensor_scalar(rt2[0:8, 0:1], pA[0:8, 0:1], -1.0,
                                                    None, ALU.mult), deps=[o_b2f])
    o_b2h = add("vector", lambda v: v.tensor_copy(slotu[0:8, 0:1], rt2[0:8, 0:1]), deps=[o_b2g])
    o_aod1 = add("gpsimd", lambda g: g.dma_start(out=aoD[:, 0:1], in_=slotu[0:8, 0:1],
                                                 allow_slow_non_contiguous=True),
                 deps=[o_b2h], dma=True)
    o_aod2 = add("gpsimd", lambda g: g.dma_start(out=aoD[:, 1:2], in_=slotu[0:8, 0:1],
                                                 allow_slow_non_contiguous=True),
                 deps=[o_b2h], dma=True)
    o_b2i = add("vector", lambda v: v.tensor_tensor(rt3[0:8, 0:1], rt2[0:8, 0:1],
                                                    rt1[0:8, 0:1], ALU.add), deps=[o_b2g, o_b2d])
    o_b2i2 = add("vector", lambda v: v.tensor_scalar(rt3[0:8, 0:1], rt3[0:8, 0:1],
                                                     1.0 / GR, None, ALU.mult), deps=[o_b2i])
    o_b2j = add("gpsimd", lambda g: g.dma_start(out=b2row[0:1, 0:NE], in_=rt3[0:8, 0:1]),
                deps=[o_b2i2], dma=True)
    o_b2k = add("tensor", lambda t: t.matmul(pA[0:NG, 0:NE], ones1f[0:1, 0:NG],
                                             b2row[0:1, 0:NE], start=True, stop=True),
                deps=[o_b2j, o_b2g])
    o_b2l = add("vector", lambda v: v.tensor_scalar(egsb[0:NG, 0:NE], pA[0:NG, 0:NE],
                                                    widxP[0:NG, 0:1], None, ALU.is_le),
                deps=[o_b2k, o_ikp])
    o_b2m = add("vector", lambda v: v.tensor_reduce(egcol[0:NG, 0:1], egsb[0:NG, 0:NE],
                                                    AX.X, ALU.add), deps=[o_b2l])
    o_b2n = add("vector", lambda v: v.tensor_scalar(egcol[0:NG, 0:1], egcol[0:NG, 0:1],
                                                    7.0, None, ALU.min), deps=[o_b2m])
    o_b2o = add("gpsimd", lambda g: g.dma_start(out=b2row[0:1, 8:8 + NG],
                                                in_=egcol[0:NG, 0:1]),
                deps=[o_b2n], dma=True)

    o_aor = add("gpsimd", lambda g: g.dma_start(out=b2row[0:1, 8 + NG:8 + NG + NE],
                                                in_=rt2[0:8, 0:1]),
                deps=[o_b2g, o_b2o], dma=True)
    o_aom = add("tensor", lambda t: t.matmul(pA[:, 0:NE], ones1f[:, :],
                                             b2row[0:1, 8 + NG:8 + NG + NE],
                                             start=True, stop=True),
                deps=[o_aor, o_b2k, o_b2l])
    o_aob = add("vector", lambda v: v.tensor_copy(prb[:, :], pA[:, 0:NE]),
                deps=[o_aom])

    o_p2_prev = None
    o_p3_prev = None
    o_scT_prev = []
    o_slotT = {}
    for c in range(TT * B_LOC):
        o_p1 = add("vector", lambda v, c=c: v.tensor_scalar(
            rt0[:, :], iota8[:, :], routesF[:, c:c + 1], None, ALU.is_equal),
            deps=[o_rts[c], o_aob, o_p3_prev])
        o_p2 = add("vector", lambda v: v.tensor_tensor(rt0[:, :], rt0[:, :], prb[:, :],
                                                       ALU.mult), deps=[o_p1])
        o_p3 = add("vector", lambda v: v.tensor_reduce(rt1[:, 0:1], rt0[:, :], AX.X,
                                                       ALU.add), deps=[o_p2])
        o_p3_prev = o_p3
        o_p2_prev = o_p2
        o_p4 = add("vector", lambda v, c=c: v.tensor_scalar(
            rt2[:, :], lpos[:, c:c + 1], rt1[:, 0:1], -1.0, ALU.add, ALU.add),
            deps=[o_p3])
        o_p5 = add("vector", lambda v: v.tensor_copy(slotu[:, :], rt2[:, :]),
                   deps=[o_p4] + o_scT_prev + ([o_aod1, o_aod2] if c == 0 else []))
        o_sc2 = add("gpsimd", lambda g, c=c: g.indirect_dma_start(
            out=idxT[:, :], out_offset=bass.IndirectOffsetOnAxis(ap=slotu[:, :], axis=0),
            in_=idxC[:, 2 * c:2 * c + 2], in_offset=None),
            deps=[o_p5, o_idc, o_pref if c == 0 else None], ind=True)
        o_st2 = add("sync", lambda s, c=c: s.dma_start(
            out=slotT[c:c + 1, :], in_=slotu[:, :]),
            deps=[o_p5], dma=True)
        o_scT_prev = [o_sc2, o_st2]
        o_idx_all.append(o_sc2)
        o_slotT[c] = o_st2
        o_p3_last = o_p3
        o_p2_last = o_p2

    def f_cc(g):
        g.collective_compute("AllGather", ALU.bypass, replica_groups=[list(range(NCORES))],
                             ins=[cc_in_t.ap().opt()], outs=[cc_out_t.ap().opt()]).then_inc(cc_sem)
        g.wait_ge(cc_sem, 1)
        return g.engine_nop()

    o_cc = add("gpsimd", f_cc, deps=[o_ci])
    o_cl = add("gpsimd", lambda g: g.dma_start(out=cnts[:, :], in_=cc_out[:, :]),
               deps=[o_cc], dma=True)
    o_pm = add("tensor", lambda t: t.matmul(pRc[:, :], pmrow[:, :], cnts[:, :],
                                            start=True, stop=True),
               deps=[o_cl, o_route_tail, o_parm_last, o_b2g])
    o_pe = add("vector", lambda v: v.tensor_copy(rowst[0:1, 0:NE], pRc[:, :]),
               deps=[o_pm, o_ps2_last])
    o_pb = add("tensor", lambda t: t.matmul(pR[:, :], ones1f[:, :], rowst[0:1, 0:NE],
                                            start=True, stop=True),
               deps=[o_pe, o_pR_last, o_b2l])
    o_base = add("vector", lambda v: v.tensor_copy(base8[:, :], pR[:, :]), deps=[o_pb])

    o_sprev = o_base
    o_selp = {}
    for b in range(B_LOC):
        for tt in range(TT):
            o_a0 = add("vector", lambda v, b=b, tt=tt: v.tensor_scalar(
                rt0[:, :], iota8[:, :], routesF[:, b * TT + tt:b * TT + tt + 1],
                None, ALU.is_equal), deps=[o_base, o_sprev])
            o_a = add("vector", lambda v: v.tensor_tensor(
                rt0[:, :], rt0[:, :], base8[:, :], ALU.mult), deps=[o_a0])
            o_b2 = add("vector", lambda v: v.tensor_reduce(rt1[:, :], rt0[:, :], AX.X, ALU.add),
                       deps=[o_a])
            o_c2 = add("vector", lambda v, b=b, tt=tt: v.tensor_tensor(
                rt2[:, :], lpos[:, b * TT + tt:b * TT + tt + 1], rt1[:, :], ALU.add),
                deps=[o_b2])
            o_d2 = add("vector", lambda v: v.tensor_scalar(
                rt2[:, :], rt2[:, :], cst[:, 5:6], None, ALU.is_le), deps=[o_c2])
            o_e2 = add("vector", lambda v, b=b, tt=tt: v.tensor_scalar(
                selYb[:, b * TT + tt:b * TT + tt + 1], rt2[:, :],
                selP[:, b * TT + tt:b * TT + tt + 1], None, ALU.mult), deps=[o_d2])
            o_g2 = add("vector", lambda v, b=b, tt=tt: v.scalar_tensor_tensor(
                selZb[:, b * TT + tt:b * TT + tt + 1],
                selYb[:, b * TT + tt:b * TT + tt + 1], negb[:, 0:1],
                selP[:, b * TT + tt:b * TT + tt + 1], ALU.mult, ALU.add), deps=[o_e2])
            o_selp[(b, tt)] = o_g2
            o_sprev = o_g2

    sp_hdt_flat = sp_hdt
    sc_we_flat = sc_we.rearrange("e p k f -> (e p) (k f)")
    gbuf = [kD[:, 0:1024], kD[:, 1024:2048], kD[:, 4096:5120], kD[:, 5120:6144]]
    gXd = [kD[:, 2048:3072], kD[:, 3072:4096], kD[:, 6144:7168], kD[:, 7168:8192]]
    idxS4 = [idxS[0][:, :], idxS[1][:, :], widxU[:, 2:4], widxU[:, 4:6]]
    ybuf = [ev0, ev1]
    wEb = [vD[:, 0:9 * 1024], qD[:, 0:9 * 1024]]
    all_spHT = [o_spHT[(bb, tt)] for bb in range(B_LOC) for tt in range(TT)]
    o_wpv = None
    for g2 in range(NG):
        o_eg1p = add("tensor", lambda t, g2=g2: t.matmul(
            pA[:, 0:1], ones1f[0:1, 0:128], b2row[0:1, 8 + g2:9 + g2],
            start=True, stop=True),
            deps=[o_b2o, o_wpv] + ([o_pe, o_b2l, o_aob] if g2 == 0 else []))
        o_wpv = add("vector", lambda v, g2=g2: v.tensor_scalar(
            widx24F[:, g2:g2 + 1], pA[:, 0:1], 128.0, None, ALU.mult),
            deps=[o_eg1p] + ([o_b2m, o_p3_last, o_p2_last] if g2 == 0 else []))
    o_wadd = add("vector", lambda v: v.tensor_scalar(
        widx24F[:, :], widx24F[:, :], widxP[:, 0:1], None, ALU.add),
        deps=[o_wpv, o_ikp])
    o_wcast = add("vector", lambda v: v.tensor_copy(widx24U[:, :], widx24F[:, :]),
                  deps=[o_wadd])
    o_g_prev = [None, None, None, None]
    o_tr_prev = [None, None, None, None]
    o_mm_prev4 = [None, None, None, None]
    o_y_prev = [None, None]
    o_yst_prev = [None, None]
    o_pTe = None
    pTrot = [pT, bank6[:, 0:64].bitcast(BF16),
             ps[2][:, 0:64].bitcast(BF16), ps[3][:, 0:64].bitcast(BF16)]
    o_pTe_rot = [None, None, None, None]
    o_mm_prev = [None, None]
    o_wread = [None, None]
    o_wg_last = [None, None]
    o_eg_prev = []
    o_il_prev = [None, None]
    o_yst_all = []
    for g in range(NG):
        wp = g % 2
        o_wg = add("gpsimd", lambda gg, wp=wp, g=g: gg.indirect_dma_start(
            out=wEb[wp][:, 0:9 * 1024], out_offset=None,
            in_=sc_we_flat,
            in_offset=bass.IndirectOffsetOnAxis(ap=widx24U[:, g:g + 1], axis=0)),
            deps=[o_wcast, o_wread[wp]] +
                 ((last_vD if wp == 0 else last_qD) if g < 2 else []), ind=True)
        o_wg_last[wp] = o_wg
        o_bg = o_wg
        for t4 in range(4):
            s4 = g * 4 + t4
            par = s4 % 4
            par2 = t4 % 2
            r0 = g * GR + t4 * 128
            o_il = add("gpsimd", lambda gg, r0=r0, par=par: gg.dma_start(
                out=idxS4[par],
                in_=idxT[r0:r0 + 128, :]),
                deps=[o_il_prev[0], o_g_prev[par]] +
                     (o_idx_all if g == 0 and t4 == 0 else []), dma=True)
            o_il_prev[0] = o_il
            o_g = add("gpsimd", lambda gg, par=par: gg.indirect_dma_start(
                out=gbuf[par], out_offset=None,
                in_=sp_hdt_flat,
                in_offset=bass.IndirectOffsetOnAxis(ap=idxS4[par][:, 0:1], axis=0)),
                deps=[o_il, o_tr_prev[par]] +
                     (all_spHT + o_zlist if g == 0 else []), ind=True)
            o_g_prev[par] = o_g
            o_trs = None
            for k in range(KT):
                pTk = pTrot[k % 4]
                o_trk = add("tensor", lambda t, par=par, k=k, pTk=pTk: t.transpose(
                    pTk, gbuf[par][:, k * 128:(k + 1) * 128], ident[:, :]),
                    deps=[o_g, o_pTe_rot[k % 4]] +
                         ([o_pCe, o_pDe, o_b6_war] if g == 0 and t4 == 0 else []))
                o_ev = add("vector", lambda v, par=par, k=k, pTk=pTk: v.tensor_copy(
                    gXd[par][:, k * 128:(k + 1) * 128], pTk),
                    deps=[o_trk, o_mm_prev4[par] if k == 0 else None])
                o_pTe_rot[k % 4] = o_ev
                o_pTe = o_ev
                o_trs = o_ev
            o_tr_prev[par] = o_trs
            for fc in range(2):
                pb = ps[fc]
                o_mm = add("tensor", lambda t, par=par, fc=fc, pb=pb, wp=wp: [
                    t.matmul(pb[:, :], gXd[par][:, k * 128:(k + 1) * 128],
                             wEb[wp][:, k * D + fc * 512:k * D + fc * 512 + 512],
                             start=(k == 0), stop=(k == KT - 1))
                    for k in range(KT)][-1],
                    deps=[o_trs, o_wg_last[wp], o_y_prev[fc]] +
                         ([o_bm_prev, o_pR_last] if g == 0 and t4 == 0 else []))
                o_mm_prev4[par] = o_mm
                o_wread[wp] = o_mm
                o_ye = add("vector", lambda v, par2=par2, fc=fc, pb=pb, wp=wp: v.tensor_tensor(
                    ybuf[par2][:, fc * 512:(fc + 1) * 512], pb[:, :],
                    wEb[wp][:, 8 * 1024 + fc * 512:8 * 1024 + (fc + 1) * 512], ALU.add),
                    deps=[o_mm, o_bg, o_yst_prev[par2]] +
                         ([o_spA.get((B_LOC - 1, TT - 2)), o_spA.get((B_LOC - 1, TT - 1))]
                          if g == 0 and t4 < 2 else []))
                o_y_prev[fc] = o_ye
            o_yst = add("sync", lambda s, r0=r0, par2=par2: s.dma_start(
                out=yP[r0:r0 + 128, :], in_=ybuf[par2][:, :]),
                deps=[o_y_prev[0], o_y_prev[1]], dma=True)
            o_yst_prev[par2] = o_yst
            o_yst_all.append(o_yst)

    ybufs = [ev0[:, :], ev1[:, :],
             rmsq[:, 0:2048].bitcast(F32), rmsq[:, 2048:4096].bitcast(F32)]
    gbufs = [kD[:, 0:1024], kD[:, 1024:2048], oT[:, 0:1024], oT[:, 1024:2048]]
    xtbs = [xtb[0][:, :], xtb[1][:, :], slabT[:, :].bitcast(F32)]
    slotSs = [idxS[0][:, 0:1], idxS[1][:, 0:1], widxU[:, 0:1], widxU[:, 1:2]]
    NT = B_LOC * TT
    o_st_d = {}
    o_y2_d = {}
    o_yl_d = {}
    o_xl_d = {}
    o_hdl_d = {}
    o_sll_d = {}
    o_out_final = []
    o_prev_store = None
    o_prev_ln = None
    o_hdl_prev = None
    o_yl_prev = None
    o_xl_prev = None
    o_sll_prev = None
    o_al_prev = None

    def em_sll_yl(k):
        global_unused = None
        b, tt = divmod(k, TT)
        p4 = k % 4
        sS = slotSs[p4]
        yb = ybufs[p4]
        o_sll = add("sync", lambda s, k=k, sS=sS: s.dma_start(
            out=sS, in_=slotT[k:k + 1, :]),
            deps=[o_slotT[k], o_yl_d.get(k - 4), o_sll_d.get(k - 1)] +
                 ([o_g_prev[p4]] if k < 2 else []) +
                 ([o_wg_last[0], o_wg_last[1]] if k in (2, 3) else []),
            dma=True, key="sync_s")
        o_sll_d[k] = o_sll
        o_yl = add("gpsimd", lambda g, yb=yb, sS=sS: g.indirect_dma_start(
            out=yb, out_offset=None, in_=yP,
            in_offset=bass.IndirectOffsetOnAxis(ap=sS, axis=0)),
            deps=[o_sll, o_st_d.get(k - 4), o_yl_d.get(k - 1)] + o_yst_all[-1:] +
                 ([o_y_prev[0], o_y_prev[1]] if k < 2 else []) +
                 (o_vt_all if k in (2, 3) else []), ind=True)
        o_yl_d[k] = o_yl

    def em_hdl(k):
        b, tt = divmod(k, TT)
        p4 = k % 4
        gb = gbufs[p4]
        o_hdl = add("gpsimd", lambda g, b=b, tt=tt, gb=gb: g.dma_start(
            out=gb, in_=sp_hdt[b * T + tt * 128:b * T + (tt + 1) * 128, :]),
            deps=[o_hdl_d.get(k - 1), o_y2_d.get(k - 4)] +
                 ([o_tr_prev[p4], o_g_prev[p4]] if k < 2 else []) +
                 (o_vt_all if k in (2, 3) else []),
            dma=True, key="g_hd")
        o_hdl_d[k] = o_hdl

    def em_xl(k):
        b, tt = divmod(k, TT)
        p3 = k % 3
        xb = xtbs[p3]
        o_xl = add("sync", lambda s, b=b, tt=tt, xb=xb: s.dma_start(
            out=xb, in_=x[b, tt * 128:(tt + 1) * 128, :]),
            deps=[o_st_d.get(k - 3), o_xl_d.get(k - 1)] +
                 ([o_y_prev[0], o_y_prev[1]] if k < 3 else []),
            dma=True, key="sync_x")
        o_xl_d[k] = o_xl

    for k in range(3):
        em_hdl(k)
        em_sll_yl(k)
    for k in range(2):
        em_xl(k)
    for c in range(NT):
        if c + 3 < NT:
            em_hdl(c + 3)
            em_sll_yl(c + 3)
        if c + 2 < NT:
            em_xl(c + 2)
        b, tt = divmod(c, TT)
        p4 = c % 4
        p3 = c % 3
        yb = ybufs[p4]
        gb = gbufs[p4]
        xb = xtbs[p3]
        o_xl = o_xl_d[c]
        o_hdl = o_hdl_d[c]
        o_yl = o_yl_d[c]
        mcol = mrs[:, c * 2:c * 2 + 1]
        rcol = mrs[:, c * 2 + 1:c * 2 + 2]
        sycol = selYb[:, c:c + 1]
        szcol = selZb[:, c:c + 1]
        nmcol = mcol
        o_xn = add("scalar", lambda sc, xb=xb, rcol=rcol, nmcol=nmcol: sc.activation(
            cvo[0][:, :], xb[:, 0:512], AF.Identity, bias=nmcol, scale=rcol),
            deps=[o_xl, o_prev_ln, o_nm_d[c]])
        o_xn2 = add("scalar", lambda sc, xb=xb, rcol=rcol, nmcol=nmcol: sc.activation(
            cvo[1][:, :], xb[:, 512:1024], AF.Identity, bias=nmcol, scale=rcol),
            deps=[o_xl, o_xn])
        o_y1 = add("vector", lambda v, yb=yb, sycol=sycol: v.tensor_scalar(
            yb, yb, sycol, None, ALU.mult),
            deps=[o_yl, o_selp[(b, tt)]])
        o_y2 = add("vector", lambda v, yb=yb, gb=gb, szcol=szcol: v.scalar_tensor_tensor(
            yb, gb, szcol, yb, ALU.mult, ALU.add), deps=[o_y1, o_hdl])
        o_y2_d[c] = o_y2
        o_x2 = add("vector", lambda v, yb=yb: v.tensor_tensor(
            yb[:, 0:512], yb[:, 0:512], cvo[0][:, :], ALU.add),
            deps=[o_y2, o_xn])
        o_x2b = add("vector", lambda v, yb=yb: v.tensor_tensor(
            yb[:, 512:1024], yb[:, 512:1024], cvo[1][:, :], ALU.add),
            deps=[o_x2, o_xn2])
        o_b1 = add("vector", lambda v, yb=yb: v.bn_stats(bnb[:, 0:6], yb[:, 0:512]),
                   deps=[o_x2b])
        o_b22 = add("vector", lambda v, yb=yb: v.bn_stats(bnb[:, 6:12], yb[:, 512:1024]),
                    deps=[o_b1])
        o_b3 = add("vector", lambda v: v.bn_aggr(bnm[:, 0:2], bnb[:, 0:12]), deps=[o_b22])
        o_b4 = add("scalar", lambda sc: sc.activation(bnb[:, 2:3], bnm[:, 1:2],
                                                      AF.Ln, bias=cst[:, 0:1]), deps=[o_b3])
        o_b5 = add("scalar", lambda sc: sc.activation(bnb[:, 3:4], bnb[:, 2:3], AF.Exp,
                                                      scale=-0.5), deps=[o_b4])
        o_n1 = add("vector", lambda v, yb=yb: v.tensor_scalar(
            yb, yb, bnm[:, 0:1], bnb[:, 3:4],
            ALU.subtract, ALU.mult), deps=[o_b5])
        o_n2 = add("vector", lambda v, yb=yb: v.tensor_tensor(
            yb, yb, gamB[:, :], ALU.mult), deps=[o_n1])
        o_n3 = add("vector", lambda v, yb=yb: v.tensor_tensor(
            yb, yb, betB[:, :], ALU.add), deps=[o_n2])
        o_n4 = add("vector", lambda v, yb=yb, xb=xb: v.tensor_tensor(
            yb, yb, xb, ALU.add), deps=[o_n3])
        o_al = add("gpsimd", lambda g, b=b, tt=tt, yb=yb: g.dma_start(
            out=yb, in_=sp_attn[b, tt * 128:(tt + 1) * 128, :],
            accum_op=ALU.add), deps=[o_n4, o_al_prev], dma=True, key="g_al")
        o_al_prev = o_al
        o_prev_ln = o_n4
        o_st = add("sync", lambda s, b=b, tt=tt, yb=yb: s.dma_start(
            out=out[b, tt * 128:(tt + 1) * 128, :], in_=yb),
            deps=[o_al, o_prev_store], dma=True)
        o_prev_store = o_st
        o_st_d[c] = o_st
        o_out_final.append(o_st)

    add("gpsimd", lambda g: g.engine_nop(), deps=o_out_final)
    add("vector", lambda v: v.engine_nop(), deps=o_out_final)
    add("sync", lambda s: s.nop(), deps=o_out_final)
    add("tensor", lambda t: t.nop(), deps=o_out_final)
    add("scalar", lambda sc: sc.nop(), deps=o_out_final)

    S.emit()
    return nc


_NC_CACHE = [None]


def kernel(**inputs):
    def f(name):
        return np.ascontiguousarray(np.asarray(inputs[name], dtype=np.float32))

    xs = f("L0_attn_hidden_states")
    base = {
        "wq": f("L0_attn_Wq"), "wk": f("L0_attn_Wk"), "wv": f("L0_attn_Wv"),
        "wo": f("L0_attn_Wo"), "cwq": f("L0_attn_conv_wq"), "cwk": f("L0_attn_conv_wk"),
        "cwv": f("L0_attn_conv_wv"), "onw": f("L0_attn_onorm_w"),
        "fwq": f("L0_ffn_Wq").reshape(D, D), "fwk": f("L0_ffn_Wk").reshape(D, D),
        "fwv": f("L0_ffn_Wv").reshape(D, D),
        "fbq": f("L0_ffn_bq").reshape(D), "fbk": f("L0_ffn_bk").reshape(D),
        "fbv": f("L0_ffn_bv").reshape(D),
        "fwe": f("L0_ffn_We"), "fbe": f("L0_ffn_be"), "fwsw": f("L0_ffn_Wsw"),
        "fbsw": f("L0_ffn_bsw"), "fgam": f("L0_ffn_gamma"), "fbet": f("L0_ffn_beta"),
    }
    in_maps = []
    for c in range(NCORES):
        m = dict(base)
        m["x"] = xs[c * B_LOC:(c + 1) * B_LOC]
        pm = np.zeros((NE, 1), dtype=np.float32)
        pm[:c, 0] = 1.0
        m["pmask"] = pm
        in_maps.append(m)
    if _NC_CACHE[0] is None:
        _NC_CACHE[0] = build_program()
    res = run_bass_kernel_spmd(_NC_CACHE[0], in_maps, core_ids=list(range(NCORES)))
    outs = [np.asarray(r["out"]) for r in res.results]
    return np.concatenate(outs, axis=0)


# version 39
# speedup vs baseline: 1.1852x; 1.0001x over previous
import numpy as np
from contextlib import ExitStack

import concourse.bass as bass
import concourse.mybir as mybir
from concourse.bass_utils import run_bass_kernel_spmd

F32 = mybir.dt.float32
F32R = mybir.dt.float32r
BF16 = mybir.dt.bfloat16
I8 = mybir.dt.int8
AF = mybir.ActivationFunctionType
ALU = mybir.AluOpType
AX = mybir.AxisListType

NCORES = 8
B_LOC = 4
T = 2048
D = 1024
KT = D // 128
TT = T // 128
NQ = T // 512
H_DN = 4
DH = 256
NCH = 16
NE = 8
CAP = 9830
GR = 512
NG = 24
NSLOT = NG * GR
EPS_LN = 1e-5
EPS_L2 = 1e-6
EPS_RMS = 1e-5
LN16 = 2.772588722239781
DEBUG_SPILLS = False

ENGINES = ("sync", "scalar", "tensor", "vector", "gpsimd")


class Op:
    __slots__ = ("idx", "eng", "fn", "deps", "dma", "need_inc", "val", "key")

    def __init__(self, idx, eng, fn, deps, dma, ind, key):
        self.idx = idx
        self.eng = eng
        self.fn = fn
        self.deps = deps
        self.dma = dma
        self.need_inc = False
        self.val = None
        self.key = key if key else eng + ("_i" if ind else "_d" if dma else "")


class Sched:
    def __init__(self, nc, ctx):
        self.nc = nc
        self.ctx = ctx
        self.ops = []

    def add(self, eng, fn, deps=(), dma=False, ind=False, key=None):
        flat = []
        for d in deps:
            if d is None:
                continue
            if isinstance(d, (list, tuple)):
                flat.extend(q for q in d if q is not None)
            else:
                flat.append(d)
        o = Op(len(self.ops), eng, fn, flat, dma or ind, ind, key)
        self.ops.append(o)
        return o

    def emit(self):
        for o in self.ops:
            if o.dma:
                o.need_inc = True
            for d in o.deps:
                assert d.idx < o.idx, "dependency must precede op"
                d.need_inc = True
        counters = {}
        for o in self.ops:
            if o.need_inc:
                counters[o.key] = counters.get(o.key, 0) + (16 if o.dma else 1)
                o.val = counters[o.key]
        sems = {k: self.ctx.enter_context(self.nc.semaphore("sem_" + k)) for k in counters}
        per_eng = {e: [o for o in self.ops if o.eng == e] for e in ENGINES}

        def stream(eng_name, eng):
            for o in per_eng[eng_name]:
                waits = {}
                for d in o.deps:
                    waits[d.key] = max(waits.get(d.key, 0), d.val)
                for key, v in waits.items():
                    eng.wait_ge(sems[key], v)
                r = o.fn(eng)
                if o.need_inc:
                    instrs = r if isinstance(r, (list, tuple)) else [r]
                    if o.dma:
                        assert len(instrs) == 1
                        instrs[0].then_inc(sems[o.key], 16)
                    else:
                        instrs[-1].then_inc(sems[o.key], 1)

        with self.nc.Block() as blk:
            for name in ENGINES:
                getattr(blk, name)(lambda e, n=name: stream(n, e))


def build_program():
    nc = bass.Bass("TRN2", target_bir_lowering=False)
    ctx = ExitStack()
    ap = lambda t: t.ap()

    x = ap(nc.dram_tensor("x", [B_LOC, T, D], F32, kind="ExternalInput"))
    wq = ap(nc.dram_tensor("wq", [D, D], F32, kind="ExternalInput"))
    wk = ap(nc.dram_tensor("wk", [D, D], F32, kind="ExternalInput"))
    wv = ap(nc.dram_tensor("wv", [D, D], F32, kind="ExternalInput"))
    wo = ap(nc.dram_tensor("wo", [D, D], F32, kind="ExternalInput"))
    cwq = ap(nc.dram_tensor("cwq", [D, 4], F32, kind="ExternalInput"))
    cwk = ap(nc.dram_tensor("cwk", [D, 4], F32, kind="ExternalInput"))
    cwv = ap(nc.dram_tensor("cwv", [D, 4], F32, kind="ExternalInput"))
    onw = ap(nc.dram_tensor("onw", [DH], F32, kind="ExternalInput"))
    fwq = ap(nc.dram_tensor("fwq", [D, D], F32, kind="ExternalInput"))
    fwk = ap(nc.dram_tensor("fwk", [D, D], F32, kind="ExternalInput"))
    fwv = ap(nc.dram_tensor("fwv", [D, D], F32, kind="ExternalInput"))
    fbq = ap(nc.dram_tensor("fbq", [D], F32, kind="ExternalInput"))
    fbk = ap(nc.dram_tensor("fbk", [D], F32, kind="ExternalInput"))
    fbv = ap(nc.dram_tensor("fbv", [D], F32, kind="ExternalInput"))
    fwe = ap(nc.dram_tensor("fwe", [NE, D, D], F32, kind="ExternalInput"))
    fbe = ap(nc.dram_tensor("fbe", [NE, D], F32, kind="ExternalInput"))
    fwsw = ap(nc.dram_tensor("fwsw", [NE, D], F32, kind="ExternalInput"))
    fbsw = ap(nc.dram_tensor("fbsw", [NE], F32, kind="ExternalInput"))
    fgam = ap(nc.dram_tensor("fgam", [D], F32, kind="ExternalInput"))
    fbet = ap(nc.dram_tensor("fbet", [D], F32, kind="ExternalInput"))
    pmask = ap(nc.dram_tensor("pmask", [NE, 1], F32, kind="ExternalInput"))
    out = ap(nc.dram_tensor("out", [B_LOC, T, D], F32, kind="ExternalOutput"))

    dbg = dict(kind="ExternalOutput") if DEBUG_SPILLS else {}
    sc_w = [ap(nc.dram_tensor(n, [D, D], BF16)) for n in ("scfwq", "scfwk", "scfwv")]
    sc_we = ap(nc.dram_tensor("scfwe", [NE, 128, 9, 1024], BF16))
    sp_attn = ap(nc.dram_tensor("spattn", [B_LOC, T, D], F32, **dbg))
    sp_hd = ap(nc.dram_tensor("sphd", [B_LOC, D, T], BF16, **dbg))
    sp_hdt = ap(nc.dram_tensor("sphdt", [B_LOC * T + 128, D], BF16, **dbg))
    denM = ap(nc.dram_tensor("denM", [16, T], F32))
    if DEBUG_SPILLS:
        dbgq = ap(nc.dram_tensor("dbgq", [128, KT * T], BF16, kind="ExternalOutput"))
        dbgk = ap(nc.dram_tensor("dbgk", [128, KT * T], BF16, kind="ExternalOutput"))
        dbgv = ap(nc.dram_tensor("dbgv", [128, KT * T], BF16, kind="ExternalOutput"))
        dbgo = ap(nc.dram_tensor("dbgo", [128, TT * DH], BF16, kind="ExternalOutput"))
        dbgon = ap(nc.dram_tensor("dbgon", [128, TT * DH], BF16, kind="ExternalOutput"))
        dbgx = ap(nc.dram_tensor("dbgx", [128, KT * T], BF16, kind="ExternalOutput"))
        dbgs = ap(nc.dram_tensor("dbgs", [128, 512], BF16, kind="ExternalOutput"))
        dbgc = ap(nc.dram_tensor("dbgc", [128, 512], F32, kind="ExternalOutput"))
    idxT = ap(nc.dram_tensor("idxT", [NSLOT, 2], mybir.dt.uint32))
    yP = ap(nc.dram_tensor("yP", [NSLOT, D], F32))
    slotT = ap(nc.dram_tensor("slotT", [TT * B_LOC, 128], mybir.dt.uint32))
    aoD = ap(nc.dram_tensor("aoD", [NE, 2], mybir.dt.uint32))
    cc_in_t = nc.dram_tensor("ccin", [1, NE], F32)
    cc_out_t = nc.dram_tensor("ccout", [NCORES, NE], F32)
    cc_in = ap(cc_in_t)
    cc_out = ap(cc_out_t)

    sb = lambda name, shape, dt: ctx.enter_context(nc.sbuf_tensor(name, shape, dt))
    xnD = sb("xnD", [128, KT * T], BF16)
    qD = sb("qD", [128, KT * T], BF16)
    kD = sb("kD", [128, KT * T], BF16)
    vD = sb("vD", [128, KT * T], BF16)
    stg = [sb(f"stg{i}", [128, 515], BF16) for i in range(2)]
    cvo = [sb(f"cvo{i}", [128, 512], F32) for i in range(2)]
    denQ = cvo[0][0:16, 0:512]
    xtb = [sb(f"xtb{i}", [128, D], F32) for i in range(2)]

    bnb = sb("bnb", [128, 12], F32)
    bnm = sb("bnm", [128, 2], F32)
    mrs = sb("mrs", [128, 2 * TT * B_LOC], F32)
    Sb1 = sb("Sb1", [128, 2 * DH], BF16)
    oT = sb("oT", [128, TT * DH], BF16)
    rmsq = sb("rmsq", [128, TT * DH], BF16)
    rmsr = sb("rmsr", [128, TT], F32)
    ev0 = sb("ev0", [128, D], F32)
    ev1 = sb("ev1", [128, D], F32)
    QpT = [sb(f"QpT{j}", [128, 128], F32R) for j in range(6)]
    QtT = [sb(f"QtT{j}", [128, 128], F32R) for j in range(7)]
    CabT = [sb(f"CabT{i}", [128, DH], F32R) for i in range(2)]
    QpR = [t[:, :] for t in QpT]
    QtR = [t[:, :] for t in QtT]
    CabR = [t[:, :] for t in CabT]
    CabF = [t[:, :].bitcast(F32) for t in CabT]
    S1 = xtb[1][:, 0:512]
    vaug = [sb(f"vaug{i}", [128, TT * 65], BF16) for i in range(2)]
    wbufA = [vaug[0], vaug[1]]
    slabT = sb("slabT", [128, 2048], BF16)
    slab = [slabT[:, i * 512:(i + 1) * 512] for i in range(4)]
    xnT = [slabT[:, 0:1024], slabT[:, 1024:2048]]
    KTt = slabT[:, 1024:1280]
    VTt = slabT[:, 1280:1536]
    Usb = vaug[0][:, 0:256]
    Gm = vaug[0][:, 256:384]
    tsum = sb("tsum", [65, 2 * TT], F32)
    sufc = sb("sufc", [65, 2 * NQ], F32)
    denr = [sb(f"denr{i}", [65, 512], F32) for i in range(2)]
    rowst = denr[1][0:1, 0:512]
    oddst = [sb(f"odd{i}", [64, 512], BF16) for i in range(2)]
    beS = [oddst[0][0:1, 0:512], oddst[1][0:1, 0:512]]
    lgt = sb("lgt", [128, NE], F32)
    prb = sb("prb", [128, NE], F32)
    rt0 = sb("rt0", [128, NE], F32)
    rt1 = sb("rt1", [128, 1], F32)
    rt2 = sb("rt2", [128, 1], F32)
    rt3 = sb("rt3", [128, 1], F32)
    oneh8 = sb("oneh8", [128, NE], BF16)
    selP = sb("selP", [128, TT * B_LOC], F32)
    lpos = sb("lpos", [128, TT * B_LOC], F32)
    carry = sb("carry", [1, NE], F32)
    base8 = sb("base8", [128, NE], F32)
    selZb = sb("selZb", [128, TT * B_LOC], F32)
    selYb = sb("selYb", [128, TT * B_LOC], F32)
    slotu = sb("slotu", [128, 1], mybir.dt.uint32)
    idxC = sb("idxC", [128, 2 * TT * B_LOC], mybir.dt.uint32)
    idxS = [sb(f"idxS{i}", [128, 2], mybir.dt.uint32) for i in range(2)]
    oobU = sb("oobU", [128, 16], mybir.dt.uint32)
    routesF = sb("routesF", [128, TT * B_LOC], F32)
    widxP = sb("widxP", [128, 1], F32)
    egsb = sb("egsb", [128, NE], F32)
    widxF = egsb
    widxU = sb("widxU", [128, KT], mybir.dt.uint32)
    egcol = sb("egcol", [128, 1], F32)
    b2row = sb("b2row", [1, 48], F32)
    b2cb = sb("b2cb", [128, 1], BF16)
    egbU = sb("egbU", [128, 1], mybir.dt.uint32)
    widx24F = sb("widx24F", [128, 24], F32)
    widx24U = sb("widx24U", [128, 24], mybir.dt.uint32)
    slotS = idxS
    aosel = widxU[:, 0:2]
    ridxU = sb("ridxU", [128, 1], mybir.dt.uint32)
    cnts = sb("cnts", [NCORES, NE], F32)
    pmrow = sb("pmrow", [NE, 1], F32)
    ident = sb("ident", [128, 128], BF16)
    m_ltn = sb("m_ltn", [128, 128], BF16)
    m_utn = sb("m_utn", [128, 128], BF16)
    m_uti = sb("m_uti", [128, 128], BF16)
    m_fut = sb("m_fut", [128, 128], I8)
    onesb = sb("onesb", [128, 128], BF16)
    negb = sb("negb", [128, 128], F32)
    ones1 = sb("ones1", [1, 128], BF16)
    ones1f = sb("ones1f", [1, 128], F32)
    onescol = sb("onescol", [128, 1], BF16)
    iota8 = sb("iota8", [128, NE], F32)
    cst = sb("cst", [128, 10], F32)
    convw = sb("convw", [128, 3 * KT * 4], F32)
    onwB = sb("onwB", [128, DH], F32)
    gamB = sb("gamB", [128, D], F32)
    betB = sb("betB", [128, D], F32)
    bvB = sb("bvB", [128, D], BF16)
    bqc = sb("bqc", [128, KT], F32)
    bkc = sb("bkc", [128, KT], F32)
    wswT = sb("wswT", [128, KT * NE], BF16)
    bswB = sb("bswB", [128, NE], F32)

    psf = lambda name, shape: ctx.enter_context(nc.psum_tensor(name, shape, F32))
    ps = [psf(f"ps{i}", [128, 512]) for i in range(4)]
    bank4 = psf("bank4", [128, 512])
    bank5 = psf("bank5", [128, 512])
    bank6 = psf("bank6", [128, 512])
    bank7 = psf("bank7", [128, 512])
    pA = bank4[:, 0:128]
    pR = bank4[:, 128:136]
    pRc = bank4[0:1, 136:144]
    pB = bank5[:, 0:128]
    pL = bank5[0:1, 128:512]
    pL = bank5[0:1, 0:512]
    pC = bank6[:, 0:DH]
    pD = bank7[:, 0:DH]
    pT = bank7[:, 0:64].bitcast(BF16)
    cc_sem = ctx.enter_context(nc.semaphore("cc_sem"))

    S = Sched(nc, ctx)
    add = S.add

    def f_memsets(v):
        v.memset(onesb[:, :], 1.0)
        v.memset(ones1[:, :], 1.0)
        v.memset(ones1f[:, :], 1.0)
        v.memset(onescol[:, :], 1.0)
        v.memset(negb[:, :], -1.0)
        v.memset(cst[:, 0:1], EPS_LN)
        v.memset(cst[:, 1:2], EPS_L2)
        v.memset(cst[:, 2:3], EPS_RMS)
        v.memset(cst[:, 3:4], -1000.0)
        v.memset(cst[:, 4:5], 1.0)
        v.memset(cst[:, 5:6], float(CAP))
        v.memset(cst[:, 6:7], -LN16)
        v.memset(cst[:, 7:8], 0.0)
        v.memset(cst[:, 8:9], float(GR))
        v.memset(cst[:, 9:10], 65535.0)
        v.memset(carry[:, :], 0.0)
        return v.memset(cst[:, 7:8], 0.0)

    o_ms = add("vector", f_memsets)

    def f_consts(g):
        g.iota(iota8[:, :], [[1, NE]], base=0, channel_multiplier=0,
               allow_small_or_imprecise_dtypes=True)
        g.affine_select(ident[:, :], onesb[:, :], [[-1, 128]], ALU.is_equal, 0.0,
                        base=0, channel_multiplier=1)
        g.affine_select(m_uti[:, :], onesb[:, :], [[1, 128]], ALU.is_ge, 0.0,
                        base=0, channel_multiplier=-1)
        g.affine_select(m_ltn[:, :], negb[:, :], [[-1, 128]], ALU.is_gt, 0.0,
                        base=0, channel_multiplier=1)
        r = g.affine_select(m_utn[:, :], negb[:, :], [[1, 128]], ALU.is_gt, 0.0,
                            base=0, channel_multiplier=-1)
        return r

    o_cst = add("gpsimd", f_consts, deps=[o_ms])
    o_idc = add("gpsimd", lambda g: g.iota(idxC[:, :], [[128, TT * B_LOC], [0, 2]], base=0,
                                           channel_multiplier=1), deps=[o_ms])
    o_oob = add("gpsimd", lambda g: g.iota(oobU[:, :], [[0, 16]], base=B_LOC * T,
                                           channel_multiplier=1), deps=[o_ms])
    o_ikp = add("gpsimd", lambda g: g.iota(widxP[:, :], [[0, 1]], base=0,
                                           channel_multiplier=1,
                                           allow_small_or_imprecise_dtypes=True), deps=[o_ms])
    idxTv = idxT[:, :].rearrange("(a b) c -> b a c", b=128)
    o_pref = None
    for pi in range(12):
        o_pref = add("gpsimd", lambda g, pi=pi: g.dma_start(
            out=idxTv[:, pi * 8:(pi + 1) * 8, :],
            in_=oobU[:, :].rearrange("p (a c) -> p a c", c=2),
            allow_slow_non_contiguous=True), deps=[o_oob, o_pref], dma=True)
    o_z0 = add("vector", lambda v: v.memset(ev0[:, :], 0.0), deps=[])
    o_zhd = add("gpsimd", lambda g: g.dma_start(
        out=sp_hdt[B_LOC * T:B_LOC * T + 128, :], in_=ev0[:, :]), deps=[o_z0], dma=True)
    o_zlist = [o_zhd]
    o_f0 = add("gpsimd", lambda g: g.memset(m_fut[:, :], 1), deps=[o_cst])
    o_cst = add("gpsimd", lambda g: g.affine_select(
        m_fut[:, :], m_fut[:, :], [[-1, 128]], ALU.is_gt, 0, base=0, channel_multiplier=1),
        deps=[o_f0])

    for si, cw in enumerate([cwq, cwk, cwv]):
        for k in range(KT):
            add("gpsimd", lambda g, si=si, k=k, cw=cw: g.dma_start(
                out=convw[:, (si * KT + k) * 4:(si * KT + k) * 4 + 4],
                in_=cw[k * 128:(k + 1) * 128, :]), dma=True)
    for k in range(KT):
        add("gpsimd", lambda g, k=k: g.dma_start(
            out=bqc[:, k:k + 1], in_=fbq[k * 128:(k + 1) * 128].rearrange("(a b) -> a b", b=1)), dma=True)
        add("gpsimd", lambda g, k=k: g.dma_start(
            out=bkc[:, k:k + 1], in_=fbk[k * 128:(k + 1) * 128].rearrange("(a b) -> a b", b=1)), dma=True)
    for k in range(KT):
        add("gpsimd", lambda g, k=k: g.dma_start(
            out=wswT[:, k * NE:(k + 1) * NE],
            in_=fwsw.rearrange("e d -> d e")[k * 128:(k + 1) * 128, :],
            allow_slow_non_contiguous=True), dma=True)
    o_parm_last = add("gpsimd", lambda g: g.dma_start(out=pmrow[:, :], in_=pmask[:, :]), dma=True)

    o_prev = o_ms
    for srcv, dst, n in [(onw, onwB, DH), (fgam, gamB, D), (fbet, betB, D),
                         (fbv, bvB, D), (fbsw, bswB, NE)]:
        for cc in range(0, n, 512):
            w = min(512, n - cc)
            o_row = add("gpsimd", lambda g, srcv=srcv, cc=cc, w=w: g.dma_start(
                out=rowst[0:1, 0:w], in_=srcv[cc:cc + w].rearrange("(b a) -> b a", b=1)),
                deps=[o_prev], dma=True)
            o_m = add("tensor", lambda t, w=w: t.matmul(
                ps[0][:, 0:w], ones1f[:, :], rowst[0:1, 0:w], start=True, stop=True),
                deps=[o_row, o_prev])
            o_prev = add("vector", lambda v, dst=dst, cc=cc, w=w: v.tensor_copy(
                dst[:, cc:cc + w], ps[0][:, 0:w]), deps=[o_m])
    o_bc_done = o_prev

    o_bstage_prev = None
    p0rot = [pT, bank6[:, 0:64].bitcast(BF16),
             ps[2][:, 0:64].bitcast(BF16), ps[3][:, 0:64].bitcast(BF16),
             bank4[:, 0:64].bitcast(BF16), bank5[:, 0:64].bitcast(BF16),
             ps[0][:, 0:64].bitcast(BF16), ps[1][:, 0:64].bitcast(BF16)]
    o_p0rot = [None] * 8
    o_prev_ld = [None, None]
    o_prev_ev = None
    o_st_by_dk = {}
    o_prev_stores = []
    for src, dst, e in ([(fwq, sc_w[0], None), (fwk, sc_w[1], None), (fwv, sc_w[2], None)] +
                        [(fwe, sc_we, ee) for ee in range(NE)]):
        o_evs = []
        evs_by_dk = {dk: [] for dk in range(8)}
        for r in range(8):
            inap = src[r * 128:(r + 1) * 128, :] if e is None else src[e, r * 128:(r + 1) * 128, :]
            o_ld = add("gpsimd", lambda g, r=r, inap=inap: g.dma_start(
                out=wbufA[r % 2][:, 0:1024], in_=inap), deps=[o_prev_ld[r % 2]], dma=True)
            for dk in range(8):
                rk = (r * 8 + dk) % 8
                prt = p0rot[rk]
                o_tr = add("tensor", lambda t, r=r, dk=dk, prt=prt: t.transpose(
                    prt, wbufA[r % 2][:, dk * 128:(dk + 1) * 128], ident[:, :]),
                    deps=[o_ld, o_cst, o_p0rot[rk]] +
                         ([o_bc_done] if rk >= 4 and o_p0rot[rk] is None else []))
                o_prev_ev = add("vector", lambda v, r=r, dk=dk, prt=prt: v.tensor_copy(
                    kD[:, dk * 1024 + r * 128:dk * 1024 + (r + 1) * 128], prt),
                    deps=[o_tr, o_st_by_dk.get(dk)])
                o_p0rot[rk] = o_prev_ev
                o_evs.append(o_prev_ev)
                evs_by_dk[dk].append(o_prev_ev)
            o_prev_ld[r % 2] = o_evs[-1]
        o_prev_stores = []
        for dk in range(8):
            if e is None:
                outap = dst[dk * 128:(dk + 1) * 128, :]
            else:
                outap = dst[e, :, dk, :]
            o_stk = add("gpsimd", lambda g, outap=outap, dk=dk: g.dma_start(
                out=outap, in_=kD[:, dk * 1024:(dk + 1) * 1024]), deps=[evs_by_dk[dk]],
                dma=True)
            o_st_by_dk[dk] = o_stk
            o_prev_stores.append(o_stk)
        if e is not None:
            o_br1 = add("gpsimd", lambda g, e=e: g.dma_start(
                out=rowst[0:1, 0:512], in_=fbe[e, 0:512].rearrange("(b a) -> b a", b=1)),
                deps=[o_bc_done, o_bstage_prev], dma=True)
            o_bm1 = add("tensor", lambda t: t.matmul(ps[0][:, 0:512], ones1f[:, :],
                                                     rowst[0:1, 0:512], start=True, stop=True),
                        deps=[o_br1, o_p0rot[6]])
            o_be1 = add("vector", lambda v: v.tensor_copy(kD[:, 8192:8704], ps[0][:, 0:512]),
                        deps=[o_bm1, o_bstage_prev])
            o_br2 = add("gpsimd", lambda g, e=e: g.dma_start(
                out=rowst[0:1, 0:512], in_=fbe[e, 512:1024].rearrange("(b a) -> b a", b=1)),
                deps=[o_be1], dma=True)
            o_bm2 = add("tensor", lambda t: t.matmul(ps[0][:, 0:512], ones1f[:, :],
                                                     rowst[0:1, 0:512], start=True, stop=True),
                        deps=[o_br2])
            o_be2 = add("vector", lambda v: v.tensor_copy(kD[:, 8704:9216], ps[0][:, 0:512]),
                        deps=[o_bm2])
            o_p0rot[6] = o_be2
            o_bst = add("gpsimd", lambda g, e=e: g.dma_start(
                out=sc_we[e, :, 8, :], in_=kD[:, 8192:9216]), deps=[o_be2], dma=True)
            o_bstage_prev = o_bst
            o_prev_stores.append(o_bst)
    o_wt_stores = o_prev_stores

    last_xnD = list(o_wt_stores)
    last_qD = [o_prev_ld[0], o_prev_ld[1]] + list(o_wt_stores)
    last_kD = list(o_wt_stores)
    last_vD = []
    last_oT = []
    o_route_tail = o_ms
    o_idx_all = []
    o_rts = {}
    o_cm_prev = None
    o_tt2_prev = None
    o_conv_dump = None
    o_psE = None
    o_psO = None
    o_pv_war = [None, None]
    o_rc = None
    o_spA = {}
    o_spH = {}
    o_spHT = {}
    last_wbufA = [None, None]
    o_scale = []
    all_last_A7 = []

    for b in range(B_LOC):
        o_tr_xn = None
        o_first_trs = []
        o_prev_ts = [None, None]
        o_ln_tail = None
        for tt in range(TT):
            o_xl = add("sync", lambda s, tt=tt, b=b: s.dma_start(
                out=xtb[tt % 2][:, :], in_=x[b, tt * 128:(tt + 1) * 128, :]),
                deps=[o_prev_ts[tt % 2]] +
                     ([o_route_tail] if tt < 2 and b > 0 else []), dma=True)
            o_s1 = add("vector", lambda v, tt=tt: v.bn_stats(bnb[:, 0:6], xtb[tt % 2][:, 0:512]),
                       deps=[o_xl, o_ln_tail])
            o_s2 = add("vector", lambda v, tt=tt: v.bn_stats(bnb[:, 6:12], xtb[tt % 2][:, 512:1024]),
                       deps=[o_s1])
            o_ag = add("vector", lambda v: v.bn_aggr(bnb[:, 0:2], bnb[:, 0:12]), deps=[o_s2])
            mcol = mrs[:, (b * TT + tt) * 2:(b * TT + tt) * 2 + 1]
            rcol = mrs[:, (b * TT + tt) * 2 + 1:(b * TT + tt) * 2 + 2]
            o_mc = add("vector", lambda v, mcol=mcol: v.tensor_copy(mcol, bnb[:, 0:1]),
                       deps=[o_ag])
            o_l1 = add("scalar", lambda sc: sc.activation(rt1[:, :], bnb[:, 1:2], AF.Ln,
                                                          bias=cst[:, 0:1]),
                       deps=[o_ag, o_bc_done if b == 0 and tt == 0 else None])
            o_l2 = add("scalar", lambda sc, rcol=rcol: sc.activation(rcol, rt1[:, :], AF.Exp,
                                                                     scale=-0.5), deps=[o_l1])
            o_ts = add("vector", lambda v, tt=tt, mcol=mcol, rcol=rcol: v.tensor_scalar(
                xnT[tt % 2][:, :], xtb[tt % 2][:, :], mcol, rcol, ALU.subtract, ALU.mult),
                deps=[o_l2, o_mc, o_tr_xn] +
                     ((o_pv_war + [o_route_tail]) if tt < 2 else []))
            o_prev_ts[tt % 2] = o_ts
            o_ln_tail = o_ts
            for k in range(KT):
                o_tr_xn = add("scalar", lambda sc, tt=tt, k=k: sc.dma_start_transpose(
                    xnD[:, k * T + tt * 128:k * T + (tt + 1) * 128],
                    xnT[tt % 2][:, k * 128:(k + 1) * 128]),
                    deps=[o_ts] + (last_xnD if tt == 0 else []), dma=True)
                if tt == 0:
                    o_first_trs.append(o_tr_xn)
        last_xnD = []

        if DEBUG_SPILLS and b == 0:
            o_dbgx = add("gpsimd", lambda g: g.dma_start(out=dbgx[:, :], in_=xnD[:, :]),
                         deps=[o_tr_xn], dma=True)
        dn_src = {}
        o_pl_last = None
        o_ps2_last = None
        o_lastmm_all = []
        for s_i, (wsrc, dstD) in enumerate([(wq, qD), (wk, kD), (wv, vD)]):
            dlast = [last_qD, last_kD, last_vD][s_i]
            cwoff = s_i * KT * 4
            o_wld = {}
            o_prev_ev = [None, None]
            o_silu = {}
            o_zs = {}
            o_conv_prev = {}
            for h in range(H_DN):
                for mi2, mi in enumerate((2 * h, 2 * h + 1)):
                    o_wld[mi] = add("gpsimd", lambda g, mi=mi, wsrc=wsrc: g.dma_start(
                        out=wbufA[mi % 2][:, 0:1024].rearrange("p (k m) -> p k m", k=KT),
                        in_=wsrc[:, mi * 128:(mi + 1) * 128].rearrange("(k p) m -> p k m", p=128)),
                        deps=[last_wbufA[mi % 2]], dma=True)
                    last_wbufA[mi % 2] = o_wld[mi]
                for mi2 in range(2):
                    o_zs[mi2] = add("vector", lambda v, mi2=mi2: v.memset(stg[mi2][:, 0:3], 0.0),
                                    deps=[o_conv_prev.get(mi2)])
                o_l2tail = None
                for qc in range(NQ):
                    for mi2, mi in enumerate((2 * h, 2 * h + 1)):
                        gi = mi * NQ + qc
                        pb = ps[gi % 2]
                        o_mm = add("tensor", lambda t, mi=mi, qc=qc, pb=pb: [
                            t.matmul(pb[:, :], wbufA[mi % 2][:, k * 128:(k + 1) * 128],
                                     xnD[:, k * T + qc * 512:k * T + (qc + 1) * 512],
                                     start=(k == 0), stop=(k == KT - 1))
                            for k in range(KT)][-1],
                            deps=[o_wld[mi], o_tr_xn, o_prev_ev[gi % 2]])
                        o_lastmm_all.append(o_mm)
                        last_wbufA[mi % 2] = o_mm
                        o_ev = add("vector", lambda v, mi2=mi2, pb=pb: v.tensor_copy(
                            stg[mi2][:, 3:515], pb[:, :]),
                            deps=[o_mm, o_zs[mi2]])
                        if DEBUG_SPILLS and b == 0 and s_i == 2 and mi == 0 and qc == 0:
                            o_dmp = add("gpsimd", lambda g: g.dma_start(
                                out=dbgs[:, :], in_=stg[0][:, 3:515]), deps=[o_ev], dma=True)
                            o_ev = o_dmp
                        o_prev_ev[gi % 2] = o_ev
                        wtap = [convw[:, cwoff + mi * 4 + tap:cwoff + mi * 4 + tap + 1]
                                for tap in range(4)]
                        o_c = add("vector", lambda v, mi2=mi2, w3=wtap[3]: v.tensor_scalar(
                            cvo[mi2][:, :], stg[mi2][:, 3:515], w3, None, ALU.mult),
                            deps=[o_ev, o_silu.get((mi2, qc - 1))] +
                                 ([o_parm_last] if b == 0 and s_i == 0 and h == 0 and qc == 0 else []) +
                                 ([o_rc] if b > 0 and s_i == 0 and h == 0 and qc == 0 else []))
                        for tap in (2, 1, 0):
                            o_c = add("vector", lambda v, mi2=mi2, tap=tap, wt=wtap[tap]:
                                      v.scalar_tensor_tensor(
                                          cvo[mi2][:, :], stg[mi2][:, tap:tap + 512], wt,
                                          cvo[mi2][:, :], ALU.mult, ALU.add), deps=[o_c])
                        o_cc = add("vector", lambda v, mi2=mi2: v.tensor_copy(
                            stg[mi2][:, 0:3], stg[mi2][:, 512:515]), deps=[o_c])
                        if DEBUG_SPILLS and b == 0 and s_i == 2 and mi == 0 and qc == 0:
                            o_dm2 = add("gpsimd", lambda g: g.dma_start(
                                out=dbgc[:, :], in_=cvo[0][:, :]), deps=[o_c], dma=True)
                            o_conv_dump = o_dm2
                        o_zs[mi2] = o_cc
                        o_conv_prev[mi2] = o_cc
                        dbg_cd = [o_conv_dump] if (DEBUG_SPILLS and b == 0 and s_i == 2
                                                    and mi == 0 and qc == 0) else []
                        if s_i < 2:
                            o_si = add("scalar", lambda sc, mi2=mi2: sc.activation(
                                cvo[mi2][:, :], cvo[mi2][:, :], AF.Silu), deps=[o_c] + dbg_cd)
                        else:
                            o_si = add("scalar", lambda sc, mi=mi, qc=qc, mi2=mi2: sc.activation(
                                dstD[:, mi * T + qc * 512:mi * T + (qc + 1) * 512],
                                cvo[mi2][:, :], AF.Silu),
                                deps=[o_c] + dbg_cd + (dlast if mi == 0 and qc == 0 else []))
                            dn_src[(s_i, mi, qc)] = o_si
                        o_silu[(mi2, qc)] = o_si
                    if s_i < 2:
                        o_q0 = add("scalar", lambda sc: sc.activation(
                            slab[0][:, :], cvo[0][:, :], AF.Square),
                            deps=[o_silu[(0, qc)], o_l2tail] +
                                 ([o_tr_xn] if s_i == 0 and h == 0 and qc == 0 else []))
                        o_q1 = add("scalar", lambda sc: sc.activation(
                            slab[1][:, :], cvo[1][:, :], AF.Square),
                            deps=[o_silu[(1, qc)], o_l2tail])
                        o_sm = add("tensor", lambda t: [
                            t.matmul(pL[:, :], onescol[:, :], slab[0][:, :], start=True, stop=False),
                            t.matmul(pL[:, :], onescol[:, :], slab[1][:, :], start=False, stop=True)][-1],
                            deps=[o_q0, o_q1, o_pl_last])
                        o_r1 = add("scalar", lambda sc: sc.activation(
                            rowst[0:1, 0:512], pL[:, :], AF.Ln, bias=cst[0:1, 1:2]), deps=[o_sm])
                        o_pl_last = o_r1
                        b_ap = cst[0:1, 6:7] if s_i == 0 else cst[0:1, 7:8]
                        o_r2 = add("scalar", lambda sc, b_ap=b_ap: sc.activation(
                            rowst[0:1, 0:512], rowst[0:1, 0:512], AF.Exp, bias=b_ap,
                            scale=-0.5), deps=[o_r1])
                        o_bc = add("tensor", lambda t: t.matmul(
                            ps[2][:, :], ones1f[:, :], rowst[0:1, 0:512], start=True, stop=True),
                            deps=[o_r2, o_ps2_last])
                        o_t0 = add("vector", lambda v, h=h, qc=qc, dstD=dstD: v.tensor_tensor(
                            dstD[:, (2 * h) * T + qc * 512:(2 * h) * T + (qc + 1) * 512],
                            cvo[0][:, :], ps[2][:, :], ALU.mult),
                            deps=[o_bc] + (dlast if h == 0 and qc == 0 else []))
                        o_t1 = add("vector", lambda v, h=h, qc=qc, dstD=dstD: v.tensor_tensor(
                            dstD[:, (2 * h + 1) * T + qc * 512:(2 * h + 1) * T + (qc + 1) * 512],
                            cvo[1][:, :], ps[2][:, :], ALU.mult), deps=[o_bc])
                        o_ps2_last = o_t1
                        o_l2tail = o_t1
                        dn_src[(s_i, 2 * h, qc)] = o_t0
                        dn_src[(s_i, 2 * h + 1, qc)] = o_t1
        last_qD = []
        last_kD = []
        last_vD = []

        dbg_ops = []
        if DEBUG_SPILLS and b == 0:
            alldn = [dn_src[(s2, m2, q2)] for s2 in range(3) for m2 in range(KT)
                     for q2 in range(NQ)]
            dbg_ops.append(add("gpsimd", lambda g: g.dma_start(out=dbgq[:, :], in_=qD[:, :]),
                               deps=alldn, dma=True))
            dbg_ops.append(add("gpsimd", lambda g: g.dma_start(out=dbgk[:, :], in_=kD[:, :]),
                               deps=alldn, dma=True))
            dbg_ops.append(add("gpsimd", lambda g: g.dma_start(out=dbgv[:, :], in_=vD[:, :]),
                               deps=alldn, dma=True))
        o_pAe = None
        o_pBe = None
        o_pCe = o_ps2_last and None
        o_pDe = None
        o_ps23_init = o_ps2_last
        o_ps2_war = None
        o_ps3_war = None
        o_ps3f_war = None
        o_pb16_war = None
        o_tt_war = []
        o_b6_war = []
        o_om_prev = None
        o_usb_war = None
        o_setK_war = [[], []]
        o_setV_war = [None, None]
        o_setTV_war = [None, None]
        o_setTK_war = [None, None]
        o_dn_last = []
        o_trV2_list = []
        o_prev_head_end = None
        o_odt_war = [None, None]
        o_ps01_init = o_l2tail
        Pb16 = QtT[2][:, 0:64].bitcast(BF16)
        Tt_sb = QtT[3][:, 0:64].bitcast(BF16)
        for h in range(H_DN):
            m0, m1 = 2 * h, 2 * h + 1
            o_Su = add("vector", lambda v: [
                v.memset(S1[:, :], 0.0), v.memset(Sb1[:, :], 0.0)][-1],
                deps=[dn_src[(0, 2 * h, 0)], o_prev_head_end])
            o_oTev = []
            dep_k = [dn_src[(1, m0, qc)] for qc in range(NQ)] + \
                    [dn_src[(1, m1, qc)] for qc in range(NQ)]
            dep_q = [dn_src[(0, m0, qc)] for qc in range(NQ)] + \
                    [dn_src[(0, m1, qc)] for qc in range(NQ)]
            dep_v = [dn_src[(2, m0, qc)] for qc in range(NQ)] + \
                    [dn_src[(2, m1, qc)] for qc in range(NQ)]
            for ci in range(NCH):
                t0 = ci * 128
                par = (h * NCH + ci) % 2
                sbase = 1024 * par
                KTs = slabT[:, sbase + 0:sbase + 256]
                VTs = slabT[:, sbase + 256:sbase + 512]
                TKs = slabT[:, sbase + 512:sbase + 768]
                TVs2 = slabT[:, sbase + 768:sbase + 1024]
                kd0 = kD[:, m0 * T + t0:m0 * T + t0 + 128]
                kd1 = kD[:, m1 * T + t0:m1 * T + t0 + 128]
                qd0 = qD[:, m0 * T + t0:m0 * T + t0 + 128]
                qd1 = qD[:, m1 * T + t0:m1 * T + t0 + 128]
                vd0 = vD[:, m0 * T + t0:m0 * T + t0 + 128]
                vd1 = vD[:, m1 * T + t0:m1 * T + t0 + 128]
                last_ch = (ci == NCH - 1)
                ps2b0 = ps[2][:, 0:64].bitcast(BF16)
                ps2b1 = ps[2][:, 64:128].bitcast(BF16)
                ps2b01 = ps[2][:, 0:128].bitcast(BF16)
                ps3b0 = ps[3][:, 0:64].bitcast(BF16)
                ps3b1 = ps[3][:, 64:128].bitcast(BF16)
                ps3b01 = ps[3][:, 0:128].bitcast(BF16)
                ps3f = ps[3][:, 256:384]
                o_ptK1 = add("tensor", lambda t, kd0=kd0, ps2b0=ps2b0: t.transpose(
                    ps2b0, kd0, ident[:, :]),
                    deps=dep_k + [o_ps2_war] + o_setK_war[par] +
                         ([o_ps23_init] if o_ps23_init else []))
                o_ptK2 = add("tensor", lambda t, kd1=kd1, ps2b1=ps2b1: t.transpose(
                    ps2b1, kd1, ident[:, :]), deps=[o_ptK1])
                o_evK = add("vector", lambda v, KTs=KTs, ps2b01=ps2b01: v.tensor_copy(
                    KTs, ps2b01), deps=[o_ptK2])
                o_ptV1 = add("tensor", lambda t, vd0=vd0, ps3b0=ps3b0: t.transpose(
                    ps3b0, vd0, ident[:, :]),
                    deps=dep_v + [o_ps3_war, o_setV_war[par]] +
                         ([o_ps23_init] if o_ps23_init else []))
                o_ps23_init = None
                o_ptV2 = add("tensor", lambda t, vd1=vd1, ps3b1=ps3b1: t.transpose(
                    ps3b1, vd1, ident[:, :]), deps=[o_ptV1])
                o_evV = add("vector", lambda v, VTs=VTs, ps3b01=ps3b01: v.tensor_copy(
                    VTs, ps3b01), deps=[o_ptV2])
                o_trV2_list.append(o_evV)
                o_A = add("tensor", lambda t, kd0=kd0, kd1=kd1: [
                    t.matmul(pA[:, :], kd0, kd0, start=True, stop=False),
                    t.matmul(pA[:, :], kd1, kd1, start=False, stop=True)][-1],
                    deps=dep_k + [o_pAe])
                o_Q0 = add("vector", lambda v: v.tensor_tensor(QpR[0], pA[:, :],
                                                               m_ltn[:, :], ALU.mult),
                           deps=[o_A] + ([o_spA.get((b - 1, TT - 2)), o_spA.get((b - 1, TT - 1))]
                                         if h == 0 and ci == 0 else []))
                o_Qt0 = add("vector", lambda v: v.tensor_tensor(QtR[0], pA[:, :],
                                                                m_utn[:, :], ALU.mult),
                            deps=[o_Q0, o_A])
                o_pAe = o_Qt0
                o_P0 = add("vector", lambda v: v.tensor_tensor(
                    QpR[2], QpT[0][:, :].bitcast(F32), ident[:, :], ALU.add),
                    deps=[o_Q0, o_A])
                o_qp_prev, o_qt_prev, o_p_prev = o_Q0, o_Qt0, o_P0
                for j in range(1, 7):
                    c0, c1 = (j - 1) % 2, j % 2
                    o_mq = add("tensor", lambda t, c0=c0: t.matmul(
                        pA[:, :], QtR[c0], QpR[c0], start=True, stop=True),
                        deps=[o_qp_prev, o_qt_prev, o_pAe])
                    o_qp = add("vector", lambda v, c1=c1: v.tensor_copy(QpR[c1], pA[:, :]),
                               deps=[o_mq])
                    o_pAe = o_qp
                    o_mqt = add("tensor", lambda t, c0=c0: t.matmul(
                        pB[:, :], QpR[c0], QtR[c0], start=True, stop=True),
                        deps=[o_qp_prev, o_qt_prev, o_pBe])
                    o_qt = add("vector", lambda v, c1=c1: v.tensor_copy(QtR[c1], pB[:, :]),
                               deps=[o_mqt])
                    o_pBe = o_qt
                    o_mp = add("tensor", lambda t, c1=c1, c0=c0, ps3f=ps3f: t.matmul(
                        ps3f, QtR[c1], QpR[2 + c0], start=True, stop=True),
                        deps=[o_qt, o_p_prev, o_ps3f_war, o_evV if j == 1 else None])
                    o_pev = add("vector", lambda v, c1=c1, c0=c0, ps3f=ps3f: v.tensor_tensor(
                        QpR[2 + c1], QpT[2 + c0][:, :].bitcast(F32), ps3f, ALU.add),
                        deps=[o_mp])
                    o_ps3f_war = o_pev
                    o_ps3_war = o_pev
                    o_qp_prev, o_qt_prev, o_p_prev = o_qp, o_qt, o_pev
                o_pb16 = add("vector", lambda v: v.tensor_copy(
                    Pb16, QpT[2][:, :].bitcast(F32)), deps=[o_p_prev, o_pb16_war])
                o_ttr = add("tensor", lambda t, ps2b0=ps2b0: t.transpose(
                    ps2b0, Pb16, ident[:, :]), deps=[o_pb16, o_evK])
                o_evTt = add("vector", lambda v, ps2b0=ps2b0: v.tensor_copy(Tt_sb, ps2b0),
                             deps=[o_ttr] + o_tt_war)
                o_ps2_war = o_evTt
                o_TV = add("tensor", lambda t, VTs=VTs: t.matmul(
                    bank4[:, 0:256], Tt_sb, VTs, start=True, stop=True),
                    deps=[o_evTt, o_evV, o_pAe])
                o_TVev = add("vector", lambda v, TVs2=TVs2: v.tensor_copy(
                    TVs2, bank4[:, 0:256]), deps=[o_TV, o_setTV_war[par]])
                o_pAe = o_TVev
                o_TK1 = add("tensor", lambda t, KTs=KTs: t.matmul(
                    bank5[:, 0:128], KTs[:, 0:128], Tt_sb, start=True, stop=True),
                    deps=[o_evTt, o_evK, o_pBe])
                o_TK2 = add("tensor", lambda t, KTs=KTs: t.matmul(
                    bank5[:, 128:256], KTs[:, 128:256], Tt_sb, start=True, stop=True),
                    deps=[o_TK1])
                o_TKev = add("vector", lambda v, TKs=TKs: v.tensor_copy(
                    TKs, bank5[:, 0:256]), deps=[o_TK2, o_setTK_war[par]])
                o_pBe = o_TKev
                o_tt_war = [o_TV, o_TK2]
                o_pb16_war = o_ttr
                o_G = add("tensor", lambda t, kd0=kd0, kd1=kd1, qd0=qd0, qd1=qd1: [
                    t.matmul(bank6[:, 256:384], kd0, qd0, start=True, stop=False),
                    t.matmul(bank6[:, 256:384], kd1, qd1, start=False, stop=True)][-1],
                    deps=dep_q + o_b6_war)
                o_Gm = add("vector", lambda v: v.tensor_tensor(Gm[:, :], bank6[:, 256:384],
                                                               m_uti[:, :], ALU.mult),
                           deps=[o_G, o_om_prev])
                o_U12 = add("tensor", lambda t, TKs=TKs: [
                    t.matmul(pC[:, :], TKs[:, 0:128], Sb1[:, 0:DH], start=True, stop=False),
                    t.matmul(pC[:, :], TKs[:, 128:256], Sb1[:, DH:2 * DH],
                             start=False, stop=True)][-1],
                    deps=[o_TKev, o_Su, o_pCe, o_Gm])
                o_Uev = add("vector", lambda v, TVs2=TVs2: v.tensor_tensor(
                    Usb[:, :], TVs2, pC[:, :], ALU.subtract),
                    deps=[o_U12, o_TVev, o_usb_war])
                o_pCe = o_Uev
                o_setTV_war[par] = o_Uev
                o_setTK_war[par] = o_U12
                o_setV_war[par] = o_TV
                o_om = add("tensor", lambda t, qd0=qd0, qd1=qd1: [
                    t.matmul(pC[:, :], qd0, Sb1[:, 0:DH], start=True, stop=False),
                    t.matmul(pC[:, :], qd1, Sb1[:, DH:2 * DH], start=False, stop=False),
                    t.matmul(pC[:, :], Gm[:, :], Usb[:, :], start=False, stop=True)][-1],
                    deps=dep_q + [o_Uev, o_Gm, o_Su, o_pCe])
                o_om_prev = o_om
                o_oe = add("vector", lambda v, ci=ci: v.tensor_copy(
                    oT[:, ci * DH:(ci + 1) * DH], pC[:, :]),
                    deps=[o_om] + (last_oT if ci == 0 and h == 0 else []))
                o_pCe = o_oe
                o_b6_war = [o_om, o_oe]
                o_oTev.append(o_oe)
                if not last_ch:
                    o_su1 = add("tensor", lambda t, KTs=KTs: t.matmul(
                        pD[:, :], KTs[:, 0:128], Usb[:, :], start=True, stop=True),
                        deps=[o_Uev, o_evK, o_pDe, o_om])
                    o_sa1 = add("vector", lambda v: v.tensor_tensor(
                        S1[:, 0:DH], S1[:, 0:DH], pD[:, :], ALU.add),
                        deps=[o_su1, o_Su])
                    o_sb1 = add("vector", lambda v: v.tensor_copy(
                        Sb1[:, 0:DH], S1[:, 0:DH]), deps=[o_sa1])
                    o_su2 = add("tensor", lambda t, KTs=KTs: t.matmul(
                        pD[:, :], KTs[:, 128:256], Usb[:, :], start=True, stop=True),
                        deps=[o_sa1, o_evK, o_Uev])
                    o_sa2 = add("vector", lambda v: v.tensor_tensor(
                        S1[:, DH:2 * DH], S1[:, DH:2 * DH], pD[:, :], ALU.add),
                        deps=[o_su2, o_Su])
                    o_sb2 = add("vector", lambda v: v.tensor_copy(
                        Sb1[:, DH:2 * DH], S1[:, DH:2 * DH]), deps=[o_sa2])
                    o_pDe = o_sa2
                    o_usb_war = o_su2
                    o_setK_war[par] = [o_su2, o_TK2]
                    o_Su = add("vector", lambda v: v.engine_nop(), deps=[o_sb1, o_sb2])
                else:
                    o_usb_war = o_om
                    o_setK_war[par] = [o_TK2]
            o_dn_last.append(o_oe)
            o_prev_head_end = o_oe
            if DEBUG_SPILLS and b == 0 and h == 0:
                dbg_ops.append(add("gpsimd", lambda g: g.dma_start(out=dbgo[:, :], in_=oT[:, :]),
                                   deps=o_oTev, dma=True))
            o_sq = add("scalar", lambda sc: sc.activation(rmsq[:, :], oT[:, :], AF.Square),
                       deps=o_oTev + (dbg_ops if DEBUG_SPILLS else []))
            o_rd = add("vector", lambda v: v.tensor_reduce(
                rmsr[:, :], rmsq.ap().rearrange("p (i d) -> p i d", d=DH), AX.X, ALU.add),
                deps=[o_sq])
            o_r1 = add("scalar", lambda sc: sc.activation(
                rmsr[:, :], rmsr[:, :], AF.Ln, bias=cst[:, 2:3], scale=1.0 / DH), deps=[o_rd])
            o_r2 = add("scalar", lambda sc: sc.activation(
                rmsr[:, :], rmsr[:, :], AF.Exp, scale=-0.5), deps=[o_r1])
            o_on = []
            for ci in range(NCH):
                o_on.append(add("vector", lambda v, ci=ci: v.scalar_tensor_tensor(
                    rmsq[:, ci * DH:(ci + 1) * DH], oT[:, ci * DH:(ci + 1) * DH],
                    rmsr[:, ci:ci + 1], onwB[:, :], ALU.mult, ALU.mult), deps=[o_r2, o_sq]))
            for ci in range(NCH):
                for dvt in range(2):
                    pso = ps[dvt][:, 0:64].bitcast(BF16)
                    o_pto = add("tensor", lambda t, ci=ci, dvt=dvt, pso=pso: t.transpose(
                        pso,
                        rmsq[:, ci * DH + dvt * 128:ci * DH + (dvt + 1) * 128], ident[:, :]),
                        deps=[o_on[ci], o_om, o_odt_war[dvt]] +
                             ([o_ps01_init] if o_ps01_init else []))
                    o_ps01_init = None
                    o_odt = add("vector", lambda v, ci=ci, dvt=dvt, h=h, pso=pso: v.tensor_copy(
                        qD[:, (2 * h + dvt) * T + ci * 128:(2 * h + dvt) * T + (ci + 1) * 128],
                        pso),
                        deps=[o_pto] + [dn_src[(0, 2 * h + dvt, qc)] for qc in range(NQ)])
                    o_odt_war[dvt] = o_odt
                    last_qD.append(o_odt)
        o_oD_done = list(last_qD)
        last_oT = []

        o_wol = {}
        o_wol[0] = add("gpsimd", lambda g: g.dma_start(
            out=oT[:, 0:4096].rearrange("p (k m) -> p k m", k=KT),
            in_=wo[:, 0:512].rearrange("(k p) m -> p k m", p=128)),
            deps=[o_sq, o_on[NCH - 1]] + o_oD_done[-2:], dma=True)
        o_wol[1] = add("gpsimd", lambda g: g.dma_start(
            out=rmsq[:, 0:4096].rearrange("p (k m) -> p k m", k=KT),
            in_=wo[:, 512:1024].rearrange("(k p) m -> p k m", p=128)),
            deps=o_oD_done, dma=True)
        o_evw = [None, None]
        o_wo_mm_last = None
        for tt in range(TT):
            for chh in range(2):
                gi = tt * 2 + chh
                pb = ps[gi % 2]
                rbuf = oT if chh == 0 else rmsq
                o_mm = add("tensor", lambda t, tt=tt, pb=pb, rbuf=rbuf: [
                    t.matmul(pb[:, :], qD[:, k * T + tt * 128:k * T + (tt + 1) * 128],
                             rbuf[:, k * 512:(k + 1) * 512],
                             start=(k == 0), stop=(k == KT - 1))
                    for k in range(KT)][-1],
                    deps=o_oD_done + [o_wol[chh], o_evw[gi % 2]])
                o_wo_mm_last = o_mm
                evb = ev0 if tt % 2 == 0 else ev1
                o_evw[gi % 2] = add("vector", lambda v, chh=chh, pb=pb, evb=evb: v.tensor_copy(
                    evb[:, chh * 512:(chh + 1) * 512], pb[:, :]),
                    deps=[o_mm, o_spA.get((b, tt - 2))] +
                         (o_zlist if b == 0 and tt < 2 else []))
            o_spA[(b, tt)] = add("gpsimd", lambda g, tt=tt, b=b,
                                 evb=(ev0 if tt % 2 == 0 else ev1): g.dma_start(
                out=sp_attn[b, tt * 128:(tt + 1) * 128, :], in_=evb[:, :]),
                deps=[o_evw[0], o_evw[1]], dma=True)
        o_attn_done = [o_spA[(b, tt)] for tt in range(TT)]

        o_fw = {}
        o_fq_done = {}
        o_evf = [None, None]
        o_first_evf = {}
        for s_i2, (dstD2, bcc) in enumerate([(qD, bqc), (kD, bkc)]):
            scw = sc_w[s_i2]
            for mi in range(KT):
                o_fw[(s_i2, mi)] = add("gpsimd", lambda g, scw=scw, mi=mi: g.dma_start(
                    out=wbufA[mi % 2][:, 0:1024].rearrange("p (k m) -> p k m", k=KT),
                    in_=scw[:, mi * 128:(mi + 1) * 128].rearrange("(k p) m -> p k m", p=128)),
                    deps=[last_wbufA[mi % 2]] +
                         ([o_wol[0], o_wol[1]] if s_i2 == 0 and mi < 2 else []), dma=True)
                last_wbufA[mi % 2] = o_fw[(s_i2, mi)]
                for qc in range(NQ):
                    gi = mi * NQ + qc
                    pb = ps[gi % 2]
                    o_mm = add("tensor", lambda t, mi=mi, qc=qc, pb=pb: [
                        t.matmul(pb[:, :], wbufA[mi % 2][:, k * 128:(k + 1) * 128],
                                 xnD[:, k * T + qc * 512:k * T + (qc + 1) * 512],
                                 start=(k == 0), stop=(k == KT - 1))
                        for k in range(KT)][-1],
                        deps=[o_fw[(s_i2, mi)], o_evf[gi % 2],
                              o_evw[0] if s_i2 == 0 and mi == 0 and qc == 0 else None,
                              o_evw[1] if s_i2 == 0 and mi == 0 and qc == 0 else None])
                    last_wbufA[mi % 2] = o_mm
                    extra = []
                    if s_i2 == 0 and mi == 0 and qc == 0:
                        extra = [o_wo_mm_last]
                    if s_i2 == 1 and mi == 0 and qc == 0:
                        extra = o_dn_last + [o_wo_mm_last]
                    o_evf[gi % 2] = add("vector", lambda v, mi=mi, qc=qc, pb=pb,
                                        dstD2=dstD2, bcc=bcc: v.tensor_scalar(
                        dstD2[:, mi * T + qc * 512:mi * T + (qc + 1) * 512], pb[:, :],
                        bcc[:, mi:mi + 1], None, ALU.add),
                        deps=[o_mm] + extra +
                             ([o_parm_last] if b == 0 and s_i2 == 0 and mi == 0 and qc == 0 else []))
                    o_fq_done[(s_i2, mi, qc)] = o_evf[gi % 2]
        all_fqk = [o_fq_done[(si2, mi, qc)] for si2 in range(2) for mi in range(KT)
                   for qc in range(NQ)]
        vwb = [oT, rmsq]
        o_vw = {}
        for hc in range(2):
            o_vw[hc] = add("gpsimd", lambda g, hc=hc: g.dma_start(
                out=vwb[hc][:, 0:4096].rearrange("p (k f) -> p k f", k=KT),
                in_=sc_w[2][:, hc * 512:(hc + 1) * 512].rearrange("(k p) f -> p k f", p=128)),
                deps=[o_wo_mm_last], dma=True)
        o_evv = [None, None]
        o_vt_last = None
        first_vt_deps = [o_trV2_list[-1]] + (list(o_scale) if b > 0 else list(last_vD))
        for tt in range(TT):
            for hc in range(2):
                gi = tt * 2 + hc
                pb = ps[2 + gi % 2]
                o_mm = add("tensor", lambda t, tt=tt, hc=hc, pb=pb: [
                    t.matmul(pb[:, :], xnD[:, k * T + tt * 128:k * T + (tt + 1) * 128],
                             vwb[hc][:, k * 512:(k + 1) * 512],
                             start=(k == 0), stop=(k == KT - 1))
                    for k in range(KT)][-1],
                    deps=[o_vw[hc], o_tr_xn, o_evv[gi % 2], o_ps2_last if tt == 0 else None])
                o_evv[gi % 2] = add("vector", lambda v, tt=tt, hc=hc, pb=pb: v.tensor_tensor(
                    vD[:, tt * 1024 + hc * 512:tt * 1024 + (hc + 1) * 512], pb[:, :],
                    bvB[:, hc * 512:(hc + 1) * 512], ALU.add),
                    deps=[o_mm] + (first_vt_deps if tt == 0 and hc == 0 else []))
                o_vt_last = o_evv[gi % 2]
        o_vt_all = [o_evv[0], o_evv[1]]

        o_hd_parts = []
        o_den_dma = []
        o_poE = None
        o_poO = None
        o_ts_ev_prev = None
        o_slab_war = {}
        o_sh_prev = [None, None]
        o_denr_war = [None, None]
        for hp in range(8):
            hE, hO = 2 * hp, 2 * hp + 1
            o_vg = []
            for par, hh in enumerate((hE, hO)):
                o_v1 = add("vector", lambda v, par=par, hh=hh: v.tensor_copy(
                    vaug[par].ap().rearrange("p (i c) -> p i c", c=65)[:, :, 0:64],
                    vD.ap().rearrange("p (i c) -> p i c", c=1024)[:, :, hh * 64:(hh + 1) * 64]),
                    deps=(o_vt_all if hp == 0 else []) + [o_pv_war[par]] +
                         ([last_wbufA[par]] if hp == 0 else []))
                o_v2 = add("vector", lambda v, par=par: v.memset(
                    vaug[par].ap().rearrange("p (i c) -> p i c", c=65)[:, :, 64:65], 1.0),
                    deps=[o_v1])
                o_vg.append(o_v2)
            o_tsm = add("tensor", lambda t: [
                t.matmul(ps[2][0:65, par * 16 + kt:par * 16 + kt + 1],
                         vaug[par][:, kt * 65:(kt + 1) * 65], onescol[:, :],
                         start=True, stop=True)
                for par in range(2) for kt in range(TT)][-1],
                deps=o_vg + [o_poE, o_ts_ev_prev])
            o_tse = add("vector", lambda v: v.tensor_copy(tsum[:, :], ps[2][0:65, 0:32]),
                        deps=[o_tsm])
            o_ts_ev_prev = o_tse
            o_suf = None
            for par in range(2):
                for qc in range(NQ):
                    lo = 4 * qc + 4
                    if lo < TT:
                        o_suf = add("vector", lambda v, par=par, qc=qc, lo=lo: v.tensor_reduce(
                            sufc[:, par * NQ + qc:par * NQ + qc + 1],
                            tsum[:, par * 16 + lo:par * 16 + 16], AX.X, ALU.add), deps=[o_tse, o_suf])
                    else:
                        o_suf = add("vector", lambda v, par=par, qc=qc: v.memset(
                            sufc[:, par * NQ + qc:par * NQ + qc + 1], 0.0), deps=[o_tse, o_suf])
            for qc in range(NQ):
                nkt = 4 * qc + 4
                o_pv_lastE = None
                o_pv_lastO = None
                for kt in range(nkt):
                    sl_e = slab[kt % 2]
                    sl_o = slab[2 + kt % 2]
                    kslE = kD[0:64, hp * T + kt * 128:hp * T + (kt + 1) * 128]
                    kslO = kD[64:128, hp * T + kt * 128:hp * T + (kt + 1) * 128]
                    qslE = qD[0:64, hp * T + qc * 512:hp * T + (qc + 1) * 512]
                    qslO = qD[64:128, hp * T + qc * 512:hp * T + (qc + 1) * 512]
                    o_sE = add("tensor", lambda t, kslE=kslE, qslE=qslE: t.matmul(
                        ps[0][:, :], kslE, qslE, start=True, stop=True),
                        deps=[o_psE] + (all_fqk if hp == 0 and qc == 0 and kt == 0 else []))
                    o_sO = add("tensor", lambda t, kslO=kslO, qslO=qslO: t.matmul(
                        ps[1][:, :], kslO, qslO, start=True, stop=True),
                        deps=[o_psO, o_sE])
                    o_xE = add("scalar", lambda sc, sl_e=sl_e: sc.activation(
                        sl_e[:, :], ps[0][:, :], AF.Exp, scale=1.0 / 32.0),
                        deps=[o_sE, o_slab_war.get((0, kt % 2))])
                    o_psE = o_xE
                    o_xO = add("scalar", lambda sc, sl_o=sl_o: sc.activation(
                        sl_o[:, :], ps[1][:, :], AF.Exp, scale=1.0 / 32.0),
                        deps=[o_sO, o_slab_war.get((1, kt % 2))])
                    o_psO = o_xO
                    r = kt - 4 * qc
                    if r >= 0:
                        def f_fix(v, sl=sl_e, r=r):
                            if r > 0:
                                v.memset(sl[:, 0:r * 128], 1.0)
                            return v.copy_predicated(sl[:, r * 128:(r + 1) * 128],
                                                     m_fut[:, :], onesb[:, :])
                        o_fE = add("vector", f_fix, deps=[o_xE])

                        def f_fixO(v, sl=sl_o, r=r):
                            if r > 0:
                                v.memset(sl[:, 0:r * 128], 1.0)
                            return v.copy_predicated(sl[:, r * 128:(r + 1) * 128],
                                                     m_fut[:, :], onesb[:, :])
                        o_fO = add("vector", f_fixO, deps=[o_xO])
                    else:
                        o_fE, o_fO = o_xE, o_xO
                    o_pv_lastE = add("tensor", lambda t, kt=kt, sl_e=sl_e, nkt=nkt: t.matmul(
                        ps[2][0:65, :], vaug[0][:, kt * 65:(kt + 1) * 65], sl_e[:, :],
                        start=(kt == 0), stop=(kt == nkt - 1)),
                        deps=[o_fE, o_poE if kt == 0 else None, o_tse if kt == 0 else None])
                    o_slab_war[(0, kt % 3)] = o_pv_lastE
                    o_pv_lastO = add("tensor", lambda t, kt=kt, sl_o=sl_o, nkt=nkt: t.matmul(
                        ps[3][0:65, :], vaug[1][:, kt * 65:(kt + 1) * 65], sl_o[:, :],
                        start=(kt == 0), stop=(kt == nkt - 1)),
                        deps=[o_fO, o_poO if kt == 0 else None])
                    o_slab_war[(1, kt % 3)] = o_pv_lastO
                o_pv_war[0] = o_pv_lastE
                o_pv_war[1] = o_pv_lastO
                hd_dep = [o_vt_last] + all_fqk if hp == 0 and qc == 0 else []
                o_heE = add("vector", lambda v, hp=hp, qc=qc: v.tensor_scalar(
                    xnD[0:64, hp * T + qc * 512:hp * T + (qc + 1) * 512],
                    ps[2][0:64, :], sufc[0:64, qc:qc + 1], None, ALU.add),
                    deps=[o_pv_lastE, o_suf] + hd_dep)
                o_dnE = add("vector", lambda v, qc=qc: v.tensor_scalar(
                    denr[0][64:65, :], ps[2][64:65, :], sufc[64:65, qc:qc + 1], None, ALU.add),
                    deps=[o_pv_lastE, o_suf, o_denr_war[0]])
                o_poE = o_dnE
                o_hoO = add("vector", lambda v, qc=qc: v.tensor_scalar(
                    oddst[qc % 2][:, :], ps[3][0:64, :], sufc[0:64, NQ + qc:NQ + qc + 1],
                    None, ALU.add),
                    deps=[o_pv_lastO, o_suf, o_sh_prev[qc % 2]])
                o_dnO = add("vector", lambda v, qc=qc: v.tensor_scalar(
                    denr[1][64:65, :], ps[3][64:65, :], sufc[64:65, NQ + qc:NQ + qc + 1],
                    None, ALU.add),
                    deps=[o_pv_lastO, o_suf, o_denr_war[1]])
                o_poO = o_dnO
                o_sh = add("gpsimd", lambda g, hp=hp, qc=qc: g.dma_start(
                    out=xnD[64:128, hp * T + qc * 512:hp * T + (qc + 1) * 512],
                    in_=oddst[qc % 2][:, :]),
                    deps=[o_hoO] + hd_dep, dma=True)
                o_sh_prev[qc % 2] = o_sh
                o_hd_parts += [o_heE, o_sh]
                o_d1 = add("gpsimd", lambda g, hE=hE, qc=qc: g.dma_start(
                    out=denM[hE:hE + 1, qc * 512:(qc + 1) * 512], in_=denr[0][64:65, :]),
                    deps=[o_dnE], dma=True)
                o_d2 = add("gpsimd", lambda g, hO=hO, qc=qc: g.dma_start(
                    out=denM[hO:hO + 1, qc * 512:(qc + 1) * 512], in_=denr[1][64:65, :]),
                    deps=[o_dnO], dma=True)
                o_denr_war[0] = o_d1
                o_denr_war[1] = o_d2
                o_den_dma += [o_d1, o_d2]
        last_wbufA = [o_pv_war[0], o_pv_war[1]]
        o_scale = []
        o_pbc = None
        o_bm_prev = None
        o_rc = None
        for qc in range(NQ):
            o_dl = add("gpsimd", lambda g, qc=qc: g.dma_start(
                out=denQ, in_=denM[0:16, qc * 512:(qc + 1) * 512]),
                deps=o_den_dma + [o_rc, o_pv_war[0], o_pv_war[1]], dma=True)
            o_rc = add("vector", lambda v: v.reciprocal(denr[0][0:16, :], denQ),
                       deps=[o_dl, o_denr_war[0] if qc == 0 else None, o_bm_prev])
            o_rowE = None
            o_rowO = None
            for hp in range(8):
                o_rowE = add("gpsimd", lambda g, hp=hp: g.dma_start(
                    out=cvo[1][0:1, 0:512], in_=denr[0][2 * hp:2 * hp + 1, :]),
                    deps=[o_rc, o_bm_prev, o_rowE], dma=True)
                o_rowO = add("gpsimd", lambda g, hp=hp: g.dma_start(
                    out=rowst[0:1, 0:512], in_=denr[0][2 * hp + 1:2 * hp + 2, :]),
                    deps=[o_rc, o_bm_prev, o_rowO, o_ps2_last if hp == 0 and qc == 0 else None],
                    dma=True)
                o_bm = add("tensor", lambda t: [
                    t.matmul(ps[0][0:64, :], ones1f[0:1, 0:64], cvo[1][0:1, 0:512],
                             start=True, stop=True),
                    t.matmul(ps[0][64:128, :], ones1f[0:1, 0:64], rowst[0:1, 0:512],
                             start=True, stop=True)][-1],
                    deps=[o_rowE, o_rowO, o_pbc,
                          o_psE if hp == 0 and qc == 0 else None])
                o_bm_prev = o_bm
                o_pbc = add("vector", lambda v, hp=hp, qc=qc: v.tensor_tensor(
                    xnD[:, hp * T + qc * 512:hp * T + (qc + 1) * 512],
                    xnD[:, hp * T + qc * 512:hp * T + (qc + 1) * 512],
                    ps[0][:, :], ALU.mult),
                    deps=[o_bm] + (o_hd_parts if hp == 0 and qc == 0 else []))
                o_scale.append(o_pbc)
        o_tr_hd = []
        for tt in range(TT):
            for k in range(KT):
                o_tr_hd.append(add("sync", lambda sc, tt=tt, k=k: sc.dma_start_transpose(
                    qD[:, tt * D + k * 128:tt * D + (k + 1) * 128],
                    xnD[:, k * T + tt * 128:k * T + (tt + 1) * 128]),
                    deps=o_scale + ([o_psE, o_psO, o_wo_mm_last] if tt == 0 and k == 0 else []),
                    dma=True))
        for tt in range(TT):
            o_spHT[(b, tt)] = add("gpsimd", lambda g, tt=tt, b=b: g.dma_start(
                out=sp_hdt[b * T + tt * 128:b * T + (tt + 1) * 128, :],
                in_=qD[:, tt * D:(tt + 1) * D]),
                deps=[o_tr_hd[tt * KT + KT - 1]], dma=True)
        all_last_A7 = o_scale + [o_spHT[(b, tt)] for tt in range(TT)]

        o_pR_last = None
        for tt in range(TT):
            o_lg = add("tensor", lambda t, tt=tt: [
                t.matmul(pR[:, :], xnD[:, k * T + tt * 128:k * T + (tt + 1) * 128],
                         wswT[:, k * NE:(k + 1) * NE], start=(k == 0), stop=(k == KT - 1))
                for k in range(KT)][-1],
                deps=o_scale + [o_pR_last, o_route_tail] +
                     ([o_parm_last] if b == 0 and tt == 0 else []))
            o_1 = add("vector", lambda v: v.tensor_tensor(lgt[:, :], pR[:, :], bswB[:, :],
                                                          ALU.add), deps=[o_lg, o_route_tail])
            o_2 = add("vector", lambda v: v.tensor_reduce(rt1[:, :], lgt[:, :], AX.X, ALU.max,
                                                          negate=True), deps=[o_1])
            o_3 = add("scalar", lambda sc: sc.activation(prb[:, :], lgt[:, :], AF.Exp,
                                                         bias=rt1[:, :]), deps=[o_2])
            o_4 = add("vector", lambda v: v.tensor_reduce(rt2[:, :], prb[:, :], AX.X, ALU.add),
                      deps=[o_3])
            o_5 = add("vector", lambda v, b=b, tt=tt: v.reciprocal(
                selP[:, b * TT + tt:b * TT + tt + 1], rt2[:, :]), deps=[o_4])
            o_6 = add("vector", lambda v: v.tensor_reduce(rt3[:, :], prb[:, :], AX.X, ALU.max),
                      deps=[o_3])
            o_7 = add("vector", lambda v: v.tensor_scalar(rt0[:, :], prb[:, :], rt3[:, :],
                                                          None, ALU.is_ge), deps=[o_6])
            o_8 = add("vector", lambda v: v.scalar_tensor_tensor(
                rt0[:, :], rt0[:, :], cst[:, 3:4], iota8[:, :], ALU.mult, ALU.add), deps=[o_7])
            o_9 = add("vector", lambda v: v.tensor_reduce(rt3[:, :], rt0[:, :], AX.X, ALU.min),
                      deps=[o_8])
            rcol = routesF[:, b * TT + tt:b * TT + tt + 1]
            o_10 = add("vector", lambda v, rcol=rcol: v.tensor_scalar(
                rcol, rt3[:, :], 1000.0, None, ALU.add), deps=[o_9])
            o_rts[b * TT + tt] = o_10
            ohsl = oneh8[:, :]
            o_11 = add("vector", lambda v, ohsl=ohsl, rcol=rcol: v.tensor_scalar(
                ohsl, iota8[:, :], rcol[:, 0:1], None, ALU.is_equal),
                deps=[o_10, o_cm_prev, o_tt2_prev])
            o_cm = add("tensor", lambda t, ohsl=ohsl: t.matmul(
                pR[:, :], m_uti[:, :], ohsl, start=True, stop=False), deps=[o_11, o_1])
            o_cm_prev = o_cm
            o_cb = add("tensor", lambda t: t.matmul(
                pR[:, :], ones1f[:, :], carry[:, :], start=False, stop=True),
                deps=[o_cm, o_route_tail])
            o_sel = add("vector", lambda v, ohsl=ohsl: v.tensor_tensor(
                rt0[:, :], pR[:, :], ohsl, ALU.mult), deps=[o_cb])
            o_pR_last = o_sel
            o_lp = add("vector", lambda v, b=b, tt=tt: v.tensor_reduce(
                lpos[:, b * TT + tt:b * TT + tt + 1], rt0[:, :], AX.X, ALU.add), deps=[o_sel])
            o_tt2 = add("tensor", lambda t, ohsl=ohsl: t.matmul(
                pRc[:, :], onescol[:, :], ohsl, start=True, stop=True),
                deps=[o_11, o_route_tail, o_sel])
            o_tt2_prev = o_tt2
            o_route_tail = add("vector", lambda v: v.tensor_tensor(
                carry[:, :], carry[:, :], pRc[:, :], ALU.add), deps=[o_tt2, o_lp])
        last_xnD = list(all_last_A7) + [o_pR_last, o_route_tail]
        last_qD = [o_spHT[(b, tt)] for tt in range(TT)]
        last_kD = [o_psE, o_psO]
        last_vD = [o_pv_war[0], o_pv_war[1]]
        last_oT = [o_vt_all[0], o_vt_all[1], o_vt_last, o_wo_mm_last]

    o_ci = add("gpsimd", lambda g: g.dma_start(out=cc_in[:, :], in_=carry[:, :]),
               deps=[o_route_tail], dma=True)

    o_b2a = add("gpsimd", lambda g: g.dma_start(out=rt1[0:8, 0:1], in_=carry[0:1, 0:NE]),
                deps=[o_route_tail], dma=True)
    o_b2b = add("vector", lambda v: v.tensor_scalar(rt2[0:8, 0:1], rt1[0:8, 0:1],
                                                    float(GR - 1), None, ALU.add), deps=[o_b2a])
    o_b2c1 = add("vector", lambda v: v.tensor_copy(slotu[0:8, 0:1], rt2[0:8, 0:1]),
                 deps=[o_b2b])
    o_b2c2 = add("vector", lambda v: v.tensor_scalar(slotu[0:8, 0:1], slotu[0:8, 0:1],
                                                     9, None, ALU.logical_shift_right),
                 deps=[o_b2c1])
    o_b2c3 = add("vector", lambda v: v.tensor_scalar(slotu[0:8, 0:1], slotu[0:8, 0:1],
                                                     9, None, ALU.logical_shift_left),
                 deps=[o_b2c2])
    o_b2d = add("vector", lambda v: v.tensor_copy(rt1[0:8, 0:1], slotu[0:8, 0:1]),
                deps=[o_b2c3])
    o_b2e = add("vector", lambda v: v.tensor_copy(b2cb[0:8, 0:1], rt1[0:8, 0:1]), deps=[o_b2d])
    o_b2f = add("tensor", lambda t: t.matmul(pA[0:8, 0:1], m_utn[0:8, 0:8], b2cb[0:8, 0:1],
                                             start=True, stop=True),
                deps=[o_b2e, o_route_tail, o_pR_last])
    o_b2g = add("vector", lambda v: v.tensor_scalar(rt2[0:8, 0:1], pA[0:8, 0:1], -1.0,
                                                    None, ALU.mult), deps=[o_b2f])
    o_b2h = add("vector", lambda v: v.tensor_copy(slotu[0:8, 0:1], rt2[0:8, 0:1]), deps=[o_b2g])
    o_aod1 = add("gpsimd", lambda g: g.dma_start(out=aoD[:, 0:1], in_=slotu[0:8, 0:1],
                                                 allow_slow_non_contiguous=True),
                 deps=[o_b2h], dma=True)
    o_aod2 = add("gpsimd", lambda g: g.dma_start(out=aoD[:, 1:2], in_=slotu[0:8, 0:1],
                                                 allow_slow_non_contiguous=True),
                 deps=[o_b2h], dma=True)
    o_b2i = add("vector", lambda v: v.tensor_tensor(rt3[0:8, 0:1], rt2[0:8, 0:1],
                                                    rt1[0:8, 0:1], ALU.add), deps=[o_b2g, o_b2d])
    o_b2i2 = add("vector", lambda v: v.tensor_scalar(rt3[0:8, 0:1], rt3[0:8, 0:1],
                                                     1.0 / GR, None, ALU.mult), deps=[o_b2i])
    o_b2j = add("gpsimd", lambda g: g.dma_start(out=b2row[0:1, 0:NE], in_=rt3[0:8, 0:1]),
                deps=[o_b2i2], dma=True)
    o_b2k = add("tensor", lambda t: t.matmul(pA[0:NG, 0:NE], ones1f[0:1, 0:NG],
                                             b2row[0:1, 0:NE], start=True, stop=True),
                deps=[o_b2j, o_b2g])
    o_b2l = add("vector", lambda v: v.tensor_scalar(egsb[0:NG, 0:NE], pA[0:NG, 0:NE],
                                                    widxP[0:NG, 0:1], None, ALU.is_le),
                deps=[o_b2k, o_ikp])
    o_b2m = add("vector", lambda v: v.tensor_reduce(egcol[0:NG, 0:1], egsb[0:NG, 0:NE],
                                                    AX.X, ALU.add), deps=[o_b2l])
    o_b2n = add("vector", lambda v: v.tensor_scalar(egcol[0:NG, 0:1], egcol[0:NG, 0:1],
                                                    7.0, None, ALU.min), deps=[o_b2m])
    o_b2o = add("gpsimd", lambda g: g.dma_start(out=b2row[0:1, 8:8 + NG],
                                                in_=egcol[0:NG, 0:1]),
                deps=[o_b2n], dma=True)

    o_aor = add("gpsimd", lambda g: g.dma_start(out=b2row[0:1, 8 + NG:8 + NG + NE],
                                                in_=rt2[0:8, 0:1]),
                deps=[o_b2g, o_b2o], dma=True)
    o_aom = add("tensor", lambda t: t.matmul(pA[:, 0:NE], ones1f[:, :],
                                             b2row[0:1, 8 + NG:8 + NG + NE],
                                             start=True, stop=True),
                deps=[o_aor, o_b2k, o_b2l])
    o_aob = add("vector", lambda v: v.tensor_copy(prb[:, :], pA[:, 0:NE]),
                deps=[o_aom])

    o_p2_prev = None
    o_p3_prev = None
    o_scT_prev = []
    o_slotT = {}
    for c in range(TT * B_LOC):
        o_p1 = add("vector", lambda v, c=c: v.tensor_scalar(
            rt0[:, :], iota8[:, :], routesF[:, c:c + 1], None, ALU.is_equal),
            deps=[o_rts[c], o_aob, o_p3_prev])
        o_p2 = add("vector", lambda v: v.tensor_tensor(rt0[:, :], rt0[:, :], prb[:, :],
                                                       ALU.mult), deps=[o_p1])
        o_p3 = add("vector", lambda v: v.tensor_reduce(rt1[:, 0:1], rt0[:, :], AX.X,
                                                       ALU.add), deps=[o_p2])
        o_p3_prev = o_p3
        o_p2_prev = o_p2
        o_p4 = add("vector", lambda v, c=c: v.tensor_scalar(
            rt2[:, :], lpos[:, c:c + 1], rt1[:, 0:1], -1.0, ALU.add, ALU.add),
            deps=[o_p3])
        o_p5 = add("vector", lambda v: v.tensor_copy(slotu[:, :], rt2[:, :]),
                   deps=[o_p4] + o_scT_prev + ([o_aod1, o_aod2] if c == 0 else []))
        o_sc2 = add("gpsimd", lambda g, c=c: g.indirect_dma_start(
            out=idxT[:, :], out_offset=bass.IndirectOffsetOnAxis(ap=slotu[:, :], axis=0),
            in_=idxC[:, 2 * c:2 * c + 2], in_offset=None),
            deps=[o_p5, o_idc, o_pref if c == 0 else None], ind=True)
        o_st2 = add("sync", lambda s, c=c: s.dma_start(
            out=slotT[c:c + 1, :], in_=slotu[:, :]),
            deps=[o_p5], dma=True)
        o_scT_prev = [o_sc2, o_st2]
        o_idx_all.append(o_sc2)
        o_slotT[c] = o_st2
        o_p3_last = o_p3
        o_p2_last = o_p2

    def f_cc(g):
        g.collective_compute("AllGather", ALU.bypass, replica_groups=[list(range(NCORES))],
                             ins=[cc_in_t.ap().opt()], outs=[cc_out_t.ap().opt()]).then_inc(cc_sem)
        g.wait_ge(cc_sem, 1)
        return g.engine_nop()

    o_cc = add("gpsimd", f_cc, deps=[o_ci])
    o_cl = add("gpsimd", lambda g: g.dma_start(out=cnts[:, :], in_=cc_out[:, :]),
               deps=[o_cc], dma=True)
    o_pm = add("tensor", lambda t: t.matmul(pRc[:, :], pmrow[:, :], cnts[:, :],
                                            start=True, stop=True),
               deps=[o_cl, o_route_tail, o_parm_last, o_b2g])
    o_pe = add("vector", lambda v: v.tensor_copy(rowst[0:1, 0:NE], pRc[:, :]),
               deps=[o_pm, o_ps2_last])
    o_pb = add("tensor", lambda t: t.matmul(pR[:, :], ones1f[:, :], rowst[0:1, 0:NE],
                                            start=True, stop=True),
               deps=[o_pe, o_pR_last, o_b2l])
    o_base = add("vector", lambda v: v.tensor_copy(base8[:, :], pR[:, :]), deps=[o_pb])

    o_sprev = o_base
    o_selp = {}
    for b in range(B_LOC):
        for tt in range(TT):
            o_a0 = add("vector", lambda v, b=b, tt=tt: v.tensor_scalar(
                rt0[:, :], iota8[:, :], routesF[:, b * TT + tt:b * TT + tt + 1],
                None, ALU.is_equal), deps=[o_base, o_sprev])
            o_a = add("vector", lambda v: v.tensor_tensor(
                rt0[:, :], rt0[:, :], base8[:, :], ALU.mult), deps=[o_a0])
            o_b2 = add("vector", lambda v: v.tensor_reduce(rt1[:, :], rt0[:, :], AX.X, ALU.add),
                       deps=[o_a])
            o_c2 = add("vector", lambda v, b=b, tt=tt: v.tensor_tensor(
                rt2[:, :], lpos[:, b * TT + tt:b * TT + tt + 1], rt1[:, :], ALU.add),
                deps=[o_b2])
            o_d2 = add("vector", lambda v: v.tensor_scalar(
                rt2[:, :], rt2[:, :], cst[:, 5:6], None, ALU.is_le), deps=[o_c2])
            o_e2 = add("vector", lambda v, b=b, tt=tt: v.tensor_scalar(
                selYb[:, b * TT + tt:b * TT + tt + 1], rt2[:, :],
                selP[:, b * TT + tt:b * TT + tt + 1], None, ALU.mult), deps=[o_d2])
            o_g2 = add("vector", lambda v, b=b, tt=tt: v.scalar_tensor_tensor(
                selZb[:, b * TT + tt:b * TT + tt + 1],
                selYb[:, b * TT + tt:b * TT + tt + 1], negb[:, 0:1],
                selP[:, b * TT + tt:b * TT + tt + 1], ALU.mult, ALU.add), deps=[o_e2])
            o_selp[(b, tt)] = o_g2
            o_sprev = o_g2

    sp_hdt_flat = sp_hdt
    sc_we_flat = sc_we.rearrange("e p k f -> (e p) (k f)")
    gbuf = [kD[:, 0:1024], kD[:, 1024:2048], kD[:, 4096:5120], kD[:, 5120:6144]]
    gXd = [kD[:, 2048:3072], kD[:, 3072:4096], kD[:, 6144:7168], kD[:, 7168:8192]]
    idxS4 = [idxS[0][:, :], idxS[1][:, :], widxU[:, 2:4], widxU[:, 4:6]]
    ybuf = [ev0, ev1]
    wEb = [vD[:, 0:9 * 1024], qD[:, 0:9 * 1024]]
    all_spHT = [o_spHT[(bb, tt)] for bb in range(B_LOC) for tt in range(TT)]
    o_wpv = None
    for g2 in range(NG):
        o_eg1p = add("tensor", lambda t, g2=g2: t.matmul(
            pA[:, 0:1], ones1f[0:1, 0:128], b2row[0:1, 8 + g2:9 + g2],
            start=True, stop=True),
            deps=[o_b2o, o_wpv] + ([o_pe, o_b2l, o_aob] if g2 == 0 else []))
        o_wpv = add("vector", lambda v, g2=g2: v.tensor_scalar(
            widx24F[:, g2:g2 + 1], pA[:, 0:1], 128.0, None, ALU.mult),
            deps=[o_eg1p] + ([o_b2m, o_p3_last, o_p2_last] if g2 == 0 else []))
    o_wadd = add("vector", lambda v: v.tensor_scalar(
        widx24F[:, :], widx24F[:, :], widxP[:, 0:1], None, ALU.add),
        deps=[o_wpv, o_ikp])
    o_wcast = add("vector", lambda v: v.tensor_copy(widx24U[:, :], widx24F[:, :]),
                  deps=[o_wadd])
    o_g_prev = [None, None, None, None]
    o_tr_prev = [None, None, None, None]
    o_mm_prev4 = [None, None, None, None]
    o_y_prev = [None, None]
    o_yst_prev = [None, None]
    o_pTe = None
    pTrot = [pT, bank6[:, 0:64].bitcast(BF16),
             ps[2][:, 0:64].bitcast(BF16), ps[3][:, 0:64].bitcast(BF16)]
    o_pTe_rot = [None, None, None, None]
    o_mm_prev = [None, None]
    o_wread = [None, None]
    o_wg_last = [None, None]
    o_eg_prev = []
    o_il_prev = [None, None]
    o_yst_all = []
    for g in range(NG):
        wp = g % 2
        o_wg = add("gpsimd", lambda gg, wp=wp, g=g: gg.indirect_dma_start(
            out=wEb[wp][:, 0:9 * 1024], out_offset=None,
            in_=sc_we_flat,
            in_offset=bass.IndirectOffsetOnAxis(ap=widx24U[:, g:g + 1], axis=0)),
            deps=[o_wcast, o_wread[wp]] +
                 ((last_vD if wp == 0 else last_qD) if g < 2 else []), ind=True)
        o_wg_last[wp] = o_wg
        o_bg = o_wg
        for t4 in range(4):
            s4 = g * 4 + t4
            par = s4 % 4
            par2 = t4 % 2
            r0 = g * GR + t4 * 128
            o_il = add("gpsimd", lambda gg, r0=r0, par=par: gg.dma_start(
                out=idxS4[par],
                in_=idxT[r0:r0 + 128, :]),
                deps=[o_il_prev[0], o_g_prev[par]] +
                     (o_idx_all if g == 0 and t4 == 0 else []), dma=True)
            o_il_prev[0] = o_il
            o_g = add("gpsimd", lambda gg, par=par: gg.indirect_dma_start(
                out=gbuf[par], out_offset=None,
                in_=sp_hdt_flat,
                in_offset=bass.IndirectOffsetOnAxis(ap=idxS4[par][:, 0:1], axis=0)),
                deps=[o_il, o_tr_prev[par]] +
                     (all_spHT + o_zlist if g == 0 else []), ind=True)
            o_g_prev[par] = o_g
            o_trs = None
            for k in range(KT):
                pTk = pTrot[k % 4]
                o_trk = add("tensor", lambda t, par=par, k=k, pTk=pTk: t.transpose(
                    pTk, gbuf[par][:, k * 128:(k + 1) * 128], ident[:, :]),
                    deps=[o_g, o_pTe_rot[k % 4]] +
                         ([o_pCe, o_pDe, o_b6_war] if g == 0 and t4 == 0 else []))
                o_ev = add("vector", lambda v, par=par, k=k, pTk=pTk: v.tensor_copy(
                    gXd[par][:, k * 128:(k + 1) * 128], pTk),
                    deps=[o_trk, o_mm_prev4[par] if k == 0 else None])
                o_pTe_rot[k % 4] = o_ev
                o_pTe = o_ev
                o_trs = o_ev
            o_tr_prev[par] = o_trs
            for fc in range(2):
                pb = ps[fc]
                o_mm = add("tensor", lambda t, par=par, fc=fc, pb=pb, wp=wp: [
                    t.matmul(pb[:, :], gXd[par][:, k * 128:(k + 1) * 128],
                             wEb[wp][:, k * D + fc * 512:k * D + fc * 512 + 512],
                             start=(k == 0), stop=(k == KT - 1))
                    for k in range(KT)][-1],
                    deps=[o_trs, o_wg_last[wp], o_y_prev[fc]] +
                         ([o_bm_prev, o_pR_last] if g == 0 and t4 == 0 else []))
                o_mm_prev4[par] = o_mm
                o_wread[wp] = o_mm
                o_ye = add("vector", lambda v, par2=par2, fc=fc, pb=pb, wp=wp: v.tensor_tensor(
                    ybuf[par2][:, fc * 512:(fc + 1) * 512], pb[:, :],
                    wEb[wp][:, 8 * 1024 + fc * 512:8 * 1024 + (fc + 1) * 512], ALU.add),
                    deps=[o_mm, o_bg, o_yst_prev[par2]] +
                         ([o_spA.get((B_LOC - 1, TT - 2)), o_spA.get((B_LOC - 1, TT - 1))]
                          if g == 0 and t4 < 2 else []))
                o_y_prev[fc] = o_ye
            o_yst = add("sync", lambda s, r0=r0, par2=par2: s.dma_start(
                out=yP[r0:r0 + 128, :], in_=ybuf[par2][:, :]),
                deps=[o_y_prev[0], o_y_prev[1]], dma=True)
            o_yst_prev[par2] = o_yst
            o_yst_all.append(o_yst)

    ybufs = [ev0[:, :], ev1[:, :],
             rmsq[:, 0:2048].bitcast(F32), rmsq[:, 2048:4096].bitcast(F32)]
    gbufs = [kD[:, 0:1024], kD[:, 1024:2048], oT[:, 0:1024], oT[:, 1024:2048]]
    xtbs = [xtb[0][:, :], xtb[1][:, :], slabT[:, :].bitcast(F32)]
    slotSs = [idxS[0][:, 0:1], idxS[1][:, 0:1], widxU[:, 0:1], widxU[:, 1:2]]
    NT = B_LOC * TT
    o_st_d = {}
    o_y2_d = {}
    o_yl_d = {}
    o_xl_d = {}
    o_hdl_d = {}
    o_sll_d = {}
    o_out_final = []
    o_prev_store = None
    o_prev_ln = None
    o_hdl_prev = None
    o_yl_prev = None
    o_xl_prev = None
    o_sll_prev = None
    o_al_prev = None

    def em_sll_yl(k):
        global_unused = None
        b, tt = divmod(k, TT)
        p4 = k % 4
        sS = slotSs[p4]
        yb = ybufs[p4]
        o_sll = add("sync", lambda s, k=k, sS=sS: s.dma_start(
            out=sS, in_=slotT[k:k + 1, :]),
            deps=[o_slotT[k], o_yl_d.get(k - 4), o_sll_d.get(k - 1)] +
                 ([o_g_prev[p4]] if k < 2 else []) +
                 ([o_wg_last[0], o_wg_last[1]] if k in (2, 3) else []),
            dma=True, key="sync_s")
        o_sll_d[k] = o_sll
        o_yl = add("gpsimd", lambda g, yb=yb, sS=sS: g.indirect_dma_start(
            out=yb, out_offset=None, in_=yP,
            in_offset=bass.IndirectOffsetOnAxis(ap=sS, axis=0)),
            deps=[o_sll, o_st_d.get(k - 4), o_yl_d.get(k - 1)] + o_yst_all[-1:] +
                 ([o_y_prev[0], o_y_prev[1]] if k < 2 else []) +
                 (o_vt_all if k in (2, 3) else []), ind=True)
        o_yl_d[k] = o_yl

    def em_hdl(k):
        b, tt = divmod(k, TT)
        p4 = k % 4
        gb = gbufs[p4]
        o_hdl = add("gpsimd", lambda g, b=b, tt=tt, gb=gb: g.dma_start(
            out=gb, in_=sp_hdt[b * T + tt * 128:b * T + (tt + 1) * 128, :]),
            deps=[o_hdl_d.get(k - 1), o_y2_d.get(k - 4)] +
                 ([o_tr_prev[p4], o_g_prev[p4]] if k < 2 else []) +
                 (o_vt_all if k in (2, 3) else []),
            dma=True, key="g_hd")
        o_hdl_d[k] = o_hdl

    def em_xl(k):
        b, tt = divmod(k, TT)
        p3 = k % 3
        xb = xtbs[p3]
        o_xl = add("sync", lambda s, b=b, tt=tt, xb=xb: s.dma_start(
            out=xb, in_=x[b, tt * 128:(tt + 1) * 128, :]),
            deps=[o_st_d.get(k - 3), o_xl_d.get(k - 1)] +
                 ([o_y_prev[0], o_y_prev[1]] if k < 3 else []),
            dma=True, key="sync_x")
        o_xl_d[k] = o_xl

    for k in range(3):
        em_hdl(k)
        em_sll_yl(k)
    for k in range(2):
        em_xl(k)

    o_xn_d = {}
    o_xn2_d = {}

    def em_xn(k):
        pp3 = k % 3
        xb2 = xtbs[pp3]
        mcol2 = mrs[:, k * 2:k * 2 + 1]
        rcol2 = mrs[:, k * 2 + 1:k * 2 + 2]
        o_a = add("vector", lambda v, xb2=xb2, mcol2=mcol2, rcol2=rcol2: v.tensor_scalar(
            cvo[0][:, :], xb2[:, 0:512], mcol2, rcol2,
            ALU.subtract, ALU.mult), deps=[o_xl_d[k]])
        o_xn2_d[k] = add("vector", lambda v, xb2=xb2, mcol2=mcol2, rcol2=rcol2:
            v.tensor_scalar(cvo[1][:, :], xb2[:, 512:1024], mcol2, rcol2,
                            ALU.subtract, ALU.mult), deps=[o_xl_d[k], o_a])
        o_xn_d[k] = o_a

    em_xn(0)
    for c in range(NT):
        if c + 3 < NT:
            em_hdl(c + 3)
            em_sll_yl(c + 3)
        if c + 2 < NT:
            em_xl(c + 2)
        b, tt = divmod(c, TT)
        p4 = c % 4
        p3 = c % 3
        yb = ybufs[p4]
        gb = gbufs[p4]
        xb = xtbs[p3]
        o_xl = o_xl_d[c]
        o_hdl = o_hdl_d[c]
        o_yl = o_yl_d[c]
        mcol = mrs[:, c * 2:c * 2 + 1]
        rcol = mrs[:, c * 2 + 1:c * 2 + 2]
        sycol = selYb[:, c:c + 1]
        szcol = selZb[:, c:c + 1]
        o_xn = o_xn_d[c]
        o_xn2 = o_xn2_d[c]
        o_y1 = add("vector", lambda v, yb=yb, sycol=sycol: v.tensor_scalar(
            yb, yb, sycol, None, ALU.mult),
            deps=[o_yl, o_selp[(b, tt)]])
        o_y2 = add("vector", lambda v, yb=yb, gb=gb, szcol=szcol: v.scalar_tensor_tensor(
            yb, gb, szcol, yb, ALU.mult, ALU.add), deps=[o_y1, o_hdl])
        o_y2_d[c] = o_y2
        o_x2 = add("vector", lambda v, yb=yb: v.tensor_tensor(
            yb[:, 0:512], yb[:, 0:512], cvo[0][:, :], ALU.add),
            deps=[o_y2, o_xn])
        o_x2b = add("vector", lambda v, yb=yb: v.tensor_tensor(
            yb[:, 512:1024], yb[:, 512:1024], cvo[1][:, :], ALU.add),
            deps=[o_x2, o_xn2])
        o_b1 = add("vector", lambda v, yb=yb: v.bn_stats(bnb[:, 0:6], yb[:, 0:512]),
                   deps=[o_x2b])
        o_b22 = add("vector", lambda v, yb=yb: v.bn_stats(bnb[:, 6:12], yb[:, 512:1024]),
                    deps=[o_b1])
        o_b3 = add("vector", lambda v: v.bn_aggr(bnm[:, 0:2], bnb[:, 0:12]), deps=[o_b22])
        if c + 1 < NT:
            em_xn(c + 1)
        o_b4 = add("scalar", lambda sc: sc.activation(bnb[:, 2:3], bnm[:, 1:2],
                                                      AF.Ln, bias=cst[:, 0:1]), deps=[o_b3])
        o_b5 = add("scalar", lambda sc: sc.activation(bnb[:, 3:4], bnb[:, 2:3], AF.Exp,
                                                      scale=-0.5), deps=[o_b4])
        o_n1 = add("vector", lambda v, yb=yb: v.tensor_scalar(
            yb, yb, bnm[:, 0:1], bnb[:, 3:4],
            ALU.subtract, ALU.mult), deps=[o_b5])
        o_n2 = add("vector", lambda v, yb=yb: v.tensor_tensor(
            yb, yb, gamB[:, :], ALU.mult), deps=[o_n1])
        o_n3 = add("vector", lambda v, yb=yb: v.tensor_tensor(
            yb, yb, betB[:, :], ALU.add), deps=[o_n2])
        o_n4 = add("vector", lambda v, yb=yb, xb=xb: v.tensor_tensor(
            yb, yb, xb, ALU.add), deps=[o_n3])
        o_al = add("gpsimd", lambda g, b=b, tt=tt, yb=yb: g.dma_start(
            out=yb, in_=sp_attn[b, tt * 128:(tt + 1) * 128, :],
            accum_op=ALU.add), deps=[o_n4, o_al_prev], dma=True, key="g_al")
        o_al_prev = o_al
        o_prev_ln = o_n4
        o_st = add("sync", lambda s, b=b, tt=tt, yb=yb: s.dma_start(
            out=out[b, tt * 128:(tt + 1) * 128, :], in_=yb),
            deps=[o_al, o_prev_store], dma=True)
        o_prev_store = o_st
        o_st_d[c] = o_st
        o_out_final.append(o_st)

    add("gpsimd", lambda g: g.engine_nop(), deps=o_out_final)
    add("vector", lambda v: v.engine_nop(), deps=o_out_final)
    add("sync", lambda s: s.nop(), deps=o_out_final)
    add("tensor", lambda t: t.nop(), deps=o_out_final)
    add("scalar", lambda sc: sc.nop(), deps=o_out_final)

    S.emit()
    return nc


_NC_CACHE = [None]


def kernel(**inputs):
    def f(name):
        return np.ascontiguousarray(np.asarray(inputs[name], dtype=np.float32))

    xs = f("L0_attn_hidden_states")
    base = {
        "wq": f("L0_attn_Wq"), "wk": f("L0_attn_Wk"), "wv": f("L0_attn_Wv"),
        "wo": f("L0_attn_Wo"), "cwq": f("L0_attn_conv_wq"), "cwk": f("L0_attn_conv_wk"),
        "cwv": f("L0_attn_conv_wv"), "onw": f("L0_attn_onorm_w"),
        "fwq": f("L0_ffn_Wq").reshape(D, D), "fwk": f("L0_ffn_Wk").reshape(D, D),
        "fwv": f("L0_ffn_Wv").reshape(D, D),
        "fbq": f("L0_ffn_bq").reshape(D), "fbk": f("L0_ffn_bk").reshape(D),
        "fbv": f("L0_ffn_bv").reshape(D),
        "fwe": f("L0_ffn_We"), "fbe": f("L0_ffn_be"), "fwsw": f("L0_ffn_Wsw"),
        "fbsw": f("L0_ffn_bsw"), "fgam": f("L0_ffn_gamma"), "fbet": f("L0_ffn_beta"),
    }
    in_maps = []
    for c in range(NCORES):
        m = dict(base)
        m["x"] = xs[c * B_LOC:(c + 1) * B_LOC]
        pm = np.zeros((NE, 1), dtype=np.float32)
        pm[:c, 0] = 1.0
        m["pmask"] = pm
        in_maps.append(m)
    if _NC_CACHE[0] is None:
        _NC_CACHE[0] = build_program()
    res = run_bass_kernel_spmd(_NC_CACHE[0], in_maps, core_ids=list(range(NCORES)))
    outs = [np.asarray(r["out"]) for r in res.results]
    return np.concatenate(outs, axis=0)
